# version 08
# baseline (speedup 1.0000x reference)
import numpy as np
import ml_dtypes
from contextlib import ExitStack

import concourse.bass as bass
import concourse.mybir as mybir
from concourse.bass_utils import run_bass_kernel_spmd

F32 = mybir.dt.float32
BF16 = mybir.dt.bfloat16
U8 = mybir.dt.uint8
AF = mybir.ActivationFunctionType
ALU = mybir.AluOpType
AX = mybir.AxisListType

B, S, H = 4, 4096, 2048
NH = 16
QL = 1536
KVL = 512
DR = 64
DN = 128
DV = 128
QH = DN + DR
SOFTMAX_SCALE = QH ** -0.5
ROPE_THETA = 1e4
EPS_RMS = 1e-6
EPS_LN = 1e-5
E = 8
DE = 1408
DSH = 2816
TQ = 2048
NG_B = S // 512
NG_Q = TQ // 512
NEG = -1.0e30
EXP_SHIFT = 24.0

DEBUG_TAPS = []
PHASES = 4
SKIP_P1B = False
P1A_CUT = 99
P1B_CUT = 99
P1B_SUB = 3



class Stream:

    def __init__(self, name):
        self.name = name
        self.n = 0
        self.sem = None


class EQ:

    def __init__(self, name):
        self.name = name
        self.ops = []
        self.n_inc = 0
        self.waited = {}

    def op(self, fn, inc=True):
        if inc:
            self.n_inc += 1
            me = self

            def run(eng, fn=fn, me=me):
                fn(eng).then_inc(me.sem, 1)
            self.ops.append(run)
            return ("c", self.name, self.n_inc)
        self.ops.append(lambda eng, fn=fn: fn(eng))
        return None

    def dma(self, stream, out, in_, **kw):
        st = self.fw.stream(stream)
        st.n += 1

        def run(eng, out=out, in_=in_, kw=kw, st=st):
            eng.dma_start(out=out, in_=in_, **kw).then_inc(st.sem, 16)
        self.ops.append(run)
        return ("s", st.name, 16 * st.n)

    def wait(self, tok):
        if tok is None:
            return
        kind, qname, val = tok
        assert not (qname == self.name and kind == "c") or val <= self.n_inc
        key = (kind, qname)
        if self.waited.get(key, -1) >= val:
            return
        self.waited[key] = val
        fw = self.fw

        def run(eng, kind=kind, qname=qname, val=val, fw=fw):
            sem = fw.q[qname].sem if kind == "c" else fw.streams[qname].sem
            eng.wait_ge(sem, val)
        self.ops.append(run)

    def tok(self):
        return ("c", self.name, self.n_inc)


class Arena:
    def __init__(self, ap, nbytes):
        self.ap = ap
        self.nbytes = nbytes
        self.off = 0
        self.marks = []
        self.peak = 0

    def alloc(self, free_shape, dtype, pbase=0, pcount=128, align=64):
        esz = {F32: 4, BF16: 2, U8: 1}[dtype]
        n = 1
        for d in free_shape:
            n *= d
        nb = n * esz
        self.off = (self.off + align - 1) // align * align
        assert self.off + nb <= self.nbytes, (
            f"arena overflow: need {(self.off + nb)/1024:.0f}KB")
        v = self.ap[pbase:pbase + pcount, self.off:self.off + nb].bitcast(dtype)
        if len(free_shape) > 1:
            names = " ".join(f"d{i}" for i in range(len(free_shape)))
            kw = {f"d{i}": d for i, d in enumerate(free_shape)}
            v = v.rearrange(f"p ({names}) -> p {names}", **kw)
        self.off += nb
        self.peak = max(self.peak, self.off)
        return v

    def mark(self):
        self.marks.append(self.off)

    def reset(self):
        self.off = self.marks.pop()


class FW:
    def __init__(self, nc, es):
        self.nc = nc
        self.q = {n: EQ(n) for n in ("pe", "ve", "se", "gp", "sy")}
        for q in self.q.values():
            q.fw = self
        self.pe, self.ve, self.se = self.q["pe"], self.q["ve"], self.q["se"]
        self.gp, self.sy = self.q["gp"], self.q["sy"]
        self.es = es
        self.streams = {}
        for n, q in self.q.items():
            q.sem = es.enter_context(nc.semaphore(f"s_{n}"))
        self.ps = [es.enter_context(nc.psum_tensor(f"ps{i}", [128, 512], F32))
                   for i in range(8)]
        ARENA_B = 207 * 1024
        self.ar_t = es.enter_context(nc.sbuf_tensor("arena", [128, ARENA_B], U8))
        self.ar = Arena(self.ar_t.ap(), ARENA_B)

    def stream(self, name):
        if name not in self.streams:
            st = Stream(name)
            st.sem = self.es.enter_context(self.nc.semaphore(f"st_{name}"))
            self.streams[name] = st
        return self.streams[name]

    def finalize(self, blk):
        for name, dec in (("sy", blk.sync), ("pe", blk.tensor), ("ve", blk.vector),
                          ("se", blk.scalar), ("gp", blk.gpsimd)):
            q = self.q[name]

            def section(eng, q=q):
                for fn in q.ops:
                    fn(eng)
            dec(section)


def emit_ln_tm(fw, x_tm, out_bf, st6, mv, vtmp, rstd, nt, D, eps):
    ve, se = fw.ve, fw.se
    nk = D // 512
    for t in range(nt):
        for k in range(nk):
            ve.op(lambda v, t=t, k=k: v.bn_stats(
                st6[:, t, k, :], x_tm[:, t, 512 * k:512 * (k + 1)]),
                inc=(t == nt - 1 and k == nk - 1))
    ve.wait(ve.tok())
    for t in range(nt):
        ve.op(lambda v, t=t: v.bn_aggr(
            mv[:, t, :], st6[:, t, :, :].rearrange("p a b -> p (a b)")),
            inc=(t == nt - 1))
    ve.wait(ve.tok())
    ve.op(lambda v: v.tensor_scalar_add(vtmp[:, :], mv[:, :, 1], eps))
    se.wait(ve.tok())
    se.op(lambda s: s.sqrt(vtmp[:, :], vtmp[:, :]))
    ve.wait(se.tok())
    ve.op(lambda v: v.reciprocal(rstd[:, :], vtmp[:, :]))
    ve.wait(ve.tok())
    if out_bf is not None:
        for t in range(nt):
            ve.op(lambda v, t=t: v.tensor_scalar(
                out_bf[:, t, :], x_tm[:, t, :], mv[:, t, 0:1],
                rstd[:, t:t + 1], ALU.subtract, ALU.mult), inc=(t == nt - 1))
    return ve.tok()


def emit_transpose_tm2fm(fw, src_bf, dst_fm, nt, D, ps_trs, id_bf, tok_src,
                         tok_dst_free=None):
    pe, ve = fw.pe, fw.ve
    pe.wait(tok_src)
    ve.wait(tok_dst_free)
    nchunk = D // 128
    for c in range(nchunk):
        bank = ps_trs[c % 2]
        if c >= 2:
            pe.wait(ve.tok())
        for t in range(nt):
            pe.op(lambda tt, c=c, t=t, bank=bank: tt.transpose(
                bank[:, 128 * t: 128 * (t + 1)],
                src_bf[:, t, 128 * c:128 * (c + 1)], id_bf),
                inc=(t == nt - 1))
        ve.wait(pe.tok())
        ve.op(lambda v, c=c, bank=bank, nt=nt: v.tensor_copy(
            dst_fm[:, c, 0:128 * nt], bank[:, 0:128 * nt]))
    return ve.tok(), pe.tok()


def build_program():
    nc = bass.Bass("TRN2", target_bir_lowering=False)

    dr = {}

    def din(name, shape, dtype=F32):
        dr[name] = nc.dram_tensor(name, shape, dtype, kind="ExternalInput").ap()
        return dr[name]

    xb = din("xb", [S, H])
    xq = din("xq", [TQ, H])
    wqa = din("wqa", [H, QL])
    wqb = din("wqb", [QL, NH * 256])
    wkva = din("wkva", [H, KVL + 2 * DR])
    wkvb = din("wkvb", [KVL, NH * (DN + DV)])
    wo = din("wo", [NH * DV, H])
    wrt = din("wrt", [H, E])
    weg = din("weg", [E, H, DE])
    weu = din("weu", [E, H, DE])
    wed = din("wed", [E, DE, H])
    wsg = din("wsg", [H, DSH])
    wsu = din("wsu", [H, DSH])
    wsd = din("wsd", [DSH, H])
    gamq_in = din("gamq", [128, QL // 128])
    gamkv_in = din("gamkv", [128, KVL // 128])
    id_in = din("id_in", [128, 128])
    ones_in = din("ones_in", [128, 128])
    sel_in = din("sel_in", [8, E * 128])
    cosk_in = din("cosk", [DR, S])
    sinkm_in = din("sinkm", [DR, S])
    cosq_in = din("cosq", [DR, TQ])
    sinqm_in = din("sinqm", [DR, TQ])
    dmask_in = din("dmask", [2, 128, 512])

    out = nc.dram_tensor("out", [TQ, H], F32, kind="ExternalOutput").ap()
    o_scr = nc.dram_tensor("o_scr", [NH, DV, TQ], BF16, kind="Internal").ap()
    x2_scr = nc.dram_tensor("x2_scr", [TQ // 128, 128, H], F32, kind="Internal").ap()
    ln2_scr = nc.dram_tensor("ln2_scr", [H // 128, 128, TQ], BF16, kind="Internal").ap()

    taps = {}
    for t in DEBUG_TAPS:
        shape, dtype = DEBUG_SHAPES[t]
        taps[t] = nc.dram_tensor("tap_" + t, shape, dtype, kind="ExternalOutput").ap()

    es = ExitStack()
    with es:
        fw = FW(nc, es)
        ar = fw.ar
        pe, ve, se, gp, sy = fw.pe, fw.ve, fw.se, fw.gp, fw.sy
        ps = fw.ps
        ps_trs = [ps[5].ap().bitcast(BF16), ps[6].ap().bitcast(BF16)]
        ps_row = ps[7]

        id_bf = ar.alloc([128], BF16)
        id_f32 = ar.alloc([128], F32)
        ones_bf = ar.alloc([128], BF16)
        sel_bf = ar.alloc([E * 128], BF16, pcount=8)
        gamq = ar.alloc([QL // 128], F32)
        gamkv = ar.alloc([KVL // 128], F32)
        ar.mark()
        c_norm = ar.alloc([KVL // 128, S], BF16)
        k_pe = ar.alloc([S], BF16, pcount=DR)
        qn = ar.alloc([QL // 128, TQ], BF16)

        gp.dma("cst_g", id_bf, id_in)
        gp.dma("cst_g", ones_bf, ones_in)
        tok_consts_gp = gp.dma("cst_g", sel_bf, sel_in)
        sy.dma("cst_s", id_f32, id_in)
        sy.dma("cst_s", gamq, gamq_in)
        tok_consts_sy = sy.dma("cst_s", gamkv, gamkv_in)

        ar.mark()
        cosk = ar.alloc([2, 512], F32, pcount=DR)
        sinkm = ar.alloc([2, 512], F32, pcount=DR)
        wkva_sb = ar.alloc([16, KVL + 2 * DR], BF16)
        x_tm = ar.alloc([4, H], F32)
        st6 = ar.alloc([4, 4, 6], F32)
        mv = ar.alloc([4, 2], F32)
        vtmp = ar.alloc([4], F32)
        rstd = ar.alloc([4], F32)
        ln_bf = ar.alloc([4, H], BF16)
        ln_fm = ar.alloc([16, 512], BF16)
        sq_bf = ar.alloc([4, 512], BF16)
        row_f = ar.alloc([512], F32, pcount=1)
        row_bf = ar.alloc([512], BF16, pcount=1)
        rb_bf = ar.alloc([512], BF16)
        kpe_f = ar.alloc([512], F32, pcount=DR)
        kpe_p = ar.alloc([512], F32, pcount=DR)
        kpe_r = ar.alloc([512], F32, pcount=DR)
        kpe_t = ar.alloc([512], F32, pcount=DR)

        wkva_r = wkva.rearrange("(dc p) m -> p dc m", p=128)
        for i in range(4):
            tok_p1a_w = gp.dma("wkva", wkva_sb[:, 4 * i:4 * i + 4, :],
                               wkva_r[:, 4 * i:4 * i + 4, :])
        tok_rope_read_prev = None

        ve.wait(tok_consts_sy)
        tok_apply_prev = None
        tok_sq_prev = None
        tok_kpe_prev = None
        tok_rb_prev = None
        tok_pe_read_fm = None
        tok_ln_prev = None

        for g in range(NG_B):
            gc = slice(512 * g, 512 * (g + 1))
            xg = xb[512 * g:512 * (g + 1), :].rearrange("(t p) d -> p t d", p=128)
            sy.wait(tok_ln_prev)
            tok_x = None
            for t in range(4):
                tok_x = sy.dma("xa", x_tm[:, t, :], xg[:, t, :])
            ve.wait(tok_x)
            tok_ln = emit_ln_tm(fw, x_tm, ln_bf, st6, mv, vtmp, rstd, 4, H, EPS_LN)
            tok_ln_prev = tok_ln
            if P1A_CUT < 1:
                continue

            pe.wait(tok_consts_gp)
            tok_fm, _tok_tr = emit_transpose_tm2fm(
                fw, ln_bf, ln_fm, 4, H, ps_trs, id_bf, tok_ln,
                tok_dst_free=tok_pe_read_fm)

            if P1A_CUT < 2:
                continue
            pe.wait(tok_fm)
            pe.wait(tok_p1a_w)
            pe.wait(tok_apply_prev)
            pe.wait(tok_sq_prev)
            pe.wait(tok_kpe_prev)
            tok_ckv_m = []
            for m in range(5):
                cols = slice(128 * m, 128 * (m + 1))
                dst = ps[m][:, 0:512]
                for dc in range(16):
                    pe.op(lambda tt, dc=dc, cols=cols, dst=dst: tt.matmul(
                        dst, wkva_sb[:, dc, cols], ln_fm[:, dc, :],
                        start=(dc == 0), stop=(dc == 15)), inc=(dc == 15))
                tok_ckv_m.append(pe.tok())
            tok_pe_read_fm = pe.tok()

            if P1A_CUT < 3:
                tok_apply_prev = tok_ckv_m[3]
                tok_kpe_prev = tok_ckv_m[4]
                continue
            se.wait(tok_ckv_m[3])
            for m in range(4):
                se.op(lambda s, m=m: s.square(sq_bf[:, m, :], ps[m][:, :]),
                      inc=(m == 3))
            tok_sq = se.tok()
            tok_sq_prev = tok_sq

            pe.wait(tok_sq)
            pe.wait(tok_rb_prev)
            for m in range(4):
                pe.op(lambda tt, m=m: tt.matmul(
                    ps_row[0:1, 0:512], ones_bf[:, 0:1], sq_bf[:, m, :],
                    start=(m == 0), stop=(m == 3)), inc=(m == 3))
            tok_sum = pe.tok()

            if P1A_CUT < 4:
                tok_apply_prev = tok_ckv_m[3]
                tok_kpe_prev = tok_ckv_m[4]
                tok_rb_prev = tok_sum
                continue
            ve.wait(tok_sum)
            ve.op(lambda v: v.tensor_scalar(
                row_f[:, :], ps_row[0:1, 0:512], 1.0 / KVL, EPS_RMS,
                ALU.mult, ALU.add))
            se.wait(ve.tok())
            se.op(lambda s: s.sqrt(row_f[:, :], row_f[:, :]))
            ve.wait(se.tok())
            ve.op(lambda v: v.reciprocal(row_f[:, :], row_f[:, :]))
            ve.wait(ve.tok())
            ve.op(lambda v: v.tensor_copy(row_bf[:, :], row_f[:, :]))
            pe.wait(ve.tok())
            pe.op(lambda tt: tt.matmul(ps_row[:, 0:512], ones_bf[0:1, :],
                                       row_bf[:, :], start=True, stop=True))
            ve.wait(pe.tok())
            ve.op(lambda v: v.tensor_copy(rb_bf[:, :], ps_row[:, 0:512]))
            tok_rb_prev = ve.tok()
            ve.wait(tok_rb_prev)
            if P1A_CUT < 5:
                tok_apply_prev = tok_ckv_m[3]
                tok_kpe_prev = tok_ckv_m[4]
                continue
            for m in range(4):
                ve.op(lambda v, m=m, gc=gc: v.scalar_tensor_tensor(
                    c_norm[:, m, gc], ps[m][:, :], gamkv[:, m:m + 1], rb_bf[:, :],
                    ALU.mult, ALU.mult), inc=(m == 3))
            tok_apply_prev = ve.tok()

            if P1A_CUT < 6:
                tok_kpe_prev = tok_ckv_m[4]
                continue
            tb = g % 2
            sy.wait(tok_rope_read_prev)
            sy.dma("ropetab", cosk[:, tb, :], cosk_in[:, gc])
            tok_tab = sy.dma("ropetab", sinkm[:, tb, :], sinkm_in[:, gc])
            ve.wait(tok_ckv_m[4])
            ve.op(lambda v: v.tensor_copy(kpe_f[:, :], ps[4][0:DR, 0:512]),
                  inc=False)
            ve.op(lambda v: v.tensor_copy(kpe_p[:, :], ps[4][DR:2 * DR, 0:512]))
            tok_kpe_prev = ve.tok()
            ve.wait(tok_kpe_prev)
            ve.wait(tok_tab)
            ve.op(lambda v, tb=tb: v.tensor_mul(
                kpe_r[:, :], kpe_p[:, :], sinkm[:, tb, :]), inc=False)
            ve.op(lambda v, tb=tb: v.tensor_mul(
                kpe_t[:, :], kpe_f[:, :], cosk[:, tb, :]))
            ve.wait(ve.tok())
            ve.op(lambda v, gc=gc: v.tensor_add(k_pe[:, gc], kpe_r[:, :],
                                                kpe_t[:, :]))
            tok_rope_read_prev = ve.tok()

        tok_p1a_ve = ve.tok()
        tok_p1a_pe = pe.tok()
        tok_p1a_se = se.tok()
        ar.reset()

        if "c_norm" in taps:
            sy.wait(tok_p1a_ve)
            for m in range(4):
                sy.dma("taps", taps["c_norm"].rearrange("m p s -> p m s")[:, m, :],
                       c_norm[:, m, :])
        if "k_pe" in taps:
            sy.wait(tok_p1a_ve)
            sy.dma("taps", taps["k_pe"], k_pe[:, :])

        tok_p1b_ve, tok_p1b_pe, tok_p1b_se = tok_p1a_ve, tok_p1a_pe, tok_p1a_se
        if SKIP_P1B:
            import types
            p1b_range = range(0)
        else:
            p1b_range = range(NG_Q)
        ar.mark()
        x_tm2 = ar.alloc([4, H], F32)
        st6b = ar.alloc([4, 4, 6], F32)
        mvb = ar.alloc([4, 2], F32)
        vtmpb = ar.alloc([4], F32)
        rstdb = ar.alloc([4], F32)
        ln_bf2 = ar.alloc([4, H], BF16)
        ln_fm2 = ar.alloc([16, 512], BF16)
        qa_bf = ar.alloc([12, 512], BF16)
        sq12 = ar.alloc([12, 512], BF16)
        row_f2 = ar.alloc([512], F32, pcount=1)
        row_bf2 = ar.alloc([512], BF16, pcount=1)
        rb_bf2 = ar.alloc([512], BF16)
        wqa_sb = ar.alloc([2, 16, 128], BF16)

        wqa_r = wqa.rearrange("(dc p) m -> p dc m", p=128)

        sy.wait(tok_p1a_ve)
        sy.wait(tok_p1a_pe)
        sy.wait(tok_p1a_se)
        gp.wait(tok_p1a_ve)
        gp.wait(tok_p1a_pe)
        gp.wait(tok_p1a_se)

        tok_pe_read_fm2 = None
        tok_ln_prev2 = None
        buf_user = [None, None]
        bank_guard = {b_: (tok_p1a_ve, tok_p1a_se) for b_ in range(4)}
        tok_rb2_prev = tok_p1a_ve

        for g in p1b_range:
            gc = slice(512 * g, 512 * (g + 1))
            xg = xq[512 * g:512 * (g + 1), :].rearrange("(t p) d -> p t d", p=128)
            sy.wait(tok_ln_prev2)
            tok_x = None
            for t in range(4):
                tok_x = sy.dma("xq_tm", x_tm2[:, t, :], xg[:, t, :])
            ve.wait(tok_x)
            tok_ln = emit_ln_tm(fw, x_tm2, ln_bf2, st6b, mvb, vtmpb, rstdb,
                                4, H, EPS_LN)
            tok_ln_prev2 = tok_ln
            if P1B_CUT < 1:
                continue
            tok_fm2, _ = emit_transpose_tm2fm(
                fw, ln_bf2, ln_fm2, 4, H, ps_trs, id_bf, tok_ln,
                tok_dst_free=tok_pe_read_fm2)
            if P1B_CUT < 2:
                tok_pe_read_fm2 = pe.tok()
                continue

            pe.wait(tok_fm2)
            tok_sq_m = {}
            tok_stage_m = {}
            for m in range(12):
                bank = ps[m % 4]
                buf = m % 2
                if P1B_SUB >= 1:
                    gp.wait(buf_user[buf])
                    for i in range(4):
                        tok_w = gp.dma(f"wqa{buf}",
                                       wqa_sb[:, buf, 4 * i:4 * i + 4, :],
                                       wqa_r[:, 4 * i:4 * i + 4, 128 * m:128 * (m + 1)])
                    pe.wait(tok_w)
                if m % 4 in bank_guard:
                    t1, t2 = bank_guard[m % 4]
                    pe.wait(t1)
                    pe.wait(t2)
                for dc in range(16):
                    pe.op(lambda tt, dc=dc, m=m, bank=bank, buf=buf: tt.matmul(
                        bank[:, :], wqa_sb[:, buf, dc, :], ln_fm2[:, dc, :],
                        start=(dc == 0), stop=(dc == 15)), inc=(dc == 15))
                tok_mm = pe.tok()
                buf_user[buf] = tok_mm
                if P1B_SUB >= 2:
                    se.wait(tok_mm)
                    tok_sq_m[m] = se.op(lambda s, m=m, bank=bank: s.square(
                        sq12[:, m, :], bank[:, :]))
                else:
                    tok_sq_m[m] = tok_mm
                if P1B_SUB >= 3:
                    ve.wait(tok_sq_m[m])
                    tok_stage_m[m] = ve.op(lambda v, m=m, bank=bank: v.tensor_copy(
                        qa_bf[:, m, :], bank[:, :]))
                else:
                    tok_stage_m[m] = tok_mm
                bank_guard[m % 4] = (tok_sq_m[m], tok_stage_m[m])
            tok_pe_read_fm2 = pe.tok()

            if P1B_CUT < 3:
                continue
            for m in range(12):
                pe.wait(tok_sq_m[m])
            pe.wait(tok_rb2_prev)
            for m in range(12):
                pe.op(lambda tt, m=m: tt.matmul(
                    ps_row[0:1, 0:512], ones_bf[:, 0:1], sq12[:, m, :],
                    start=(m == 0), stop=(m == 11)), inc=(m == 11))
            tok_sum = pe.tok()
            ve.wait(tok_sum)
            ve.op(lambda v: v.tensor_scalar(
                row_f2[:, :], ps_row[0:1, 0:512], 1.0 / QL, EPS_RMS,
                ALU.mult, ALU.add))
            se.wait(ve.tok())
            se.op(lambda s: s.sqrt(row_f2[:, :], row_f2[:, :]))
            ve.wait(se.tok())
            ve.op(lambda v: v.reciprocal(row_f2[:, :], row_f2[:, :]))
            ve.wait(ve.tok())
            ve.op(lambda v: v.tensor_copy(row_bf2[:, :], row_f2[:, :]))
            pe.wait(ve.tok())
            pe.op(lambda tt: tt.matmul(ps_row[:, 0:512], ones_bf[0:1, :],
                                       row_bf2[:, :], start=True, stop=True))
            ve.wait(pe.tok())
            ve.op(lambda v: v.tensor_copy(rb_bf2[:, :], ps_row[:, 0:512]))
            tok_rb2_prev = ve.tok()
            ve.wait(tok_rb2_prev)
            if P1B_CUT < 4:
                continue
            for m in range(12):
                ve.op(lambda v, m=m, gc=gc: v.scalar_tensor_tensor(
                    qn[:, m, gc], qa_bf[:, m, :], gamq[:, m:m + 1], rb_bf2[:, :],
                    ALU.mult, ALU.mult), inc=(m == 11))
            tok_qa_read = ve.tok()
            for b_ in range(4):
                bank_guard[b_] = (bank_guard[b_][0], tok_qa_read)

        if not SKIP_P1B:
            tok_p1b_ve = ve.tok()
            tok_p1b_pe = pe.tok()
            tok_p1b_se = se.tok()
        ar.reset()

        if "qn" in taps:
            sy.wait(tok_p1b_ve)
            for m in range(12):
                sy.dma("taps", taps["qn"].rearrange("m p s -> p m s")[:, m, :],
                       qn[:, m, :])

        if PHASES >= 2:
            p2 = build_attention(nc, fw, ar, locals())
        ar.reset()

        if PHASES >= 3:
            p3 = build_p3(nc, fw, ar, locals())
        if PHASES >= 4:
            build_p4(nc, fw, ar, locals())

        if PHASES < 3:
            ar.mark()
            zz = ar.alloc([4, H], F32)
            sy.wait(tok_p1b_ve)
            sy.wait(tok_p1b_pe)
            sy.wait(tok_p1b_se)
            tok_st = None
            for g in range(NG_Q):
                xg = xq[512 * g:512 * (g + 1), :].rearrange("(t p) d -> p t d", p=128)
                og = out[512 * g:512 * (g + 1), :].rearrange("(t p) d -> p t d", p=128)
                sy.wait(tok_st)
                tok_x = None
                for t in range(4):
                    tok_x = sy.dma("stub_x", zz[:, t, :], xg[:, t, :])
                sy.wait(tok_x)
                for t in range(4):
                    tok_st = sy.dma("out", og[:, t, :], zz[:, t, :])
            ar.reset()

        for qq in ("pe", "ve", "se", "gp"):
            sy.wait(fw.q[qq].tok())
        for stn, st_ in fw.streams.items():
            sy.wait(("s", stn, 16 * st_.n))

        with nc.Block() as blk:
            fw.finalize(blk)
    return nc


def build_p4(nc, fw, ar, EV):
    pe, ve, se, gp, sy = fw.pe, fw.ve, fw.se, fw.gp, fw.sy
    ps = fw.ps
    wrt = EV["wrt"]; wsg = EV["wsg"]; wsu = EV["wsu"]; wsd = EV["wsd"]
    weg = EV["weg"]; weu = EV["weu"]; wed = EV["wed"]
    ln2_scr = EV["ln2_scr"]; x2_scr = EV["x2_scr"]; out = EV["out"]
    taps = EV["taps"]; id_bf = EV["id_bf"]; id_f32 = EV["id_f32"]
    sel_bf = EV["sel_bf"]

    ar.mark()
    ln2_sb = ar.alloc([16, 1024], BF16)
    h_e = ar.alloc([22, 1024], BF16)
    ffn_acc = ar.alloc([16, 1024], F32)
    wg_sb = ar.alloc([2, 16, 128], BF16)
    wu_sb = ar.alloc([2, 16, 128], BF16)
    wd_sb = ar.alloc([2, 22, 128], BF16)
    sg_bf = ar.alloc([2, 512], BF16)
    wrt_sb = ar.alloc([16, 8], BF16)
    lg_fm = ar.alloc([1024], F32, pcount=8)
    lg_tm = ar.alloc([8, 8], F32)
    pexp = ar.alloc([8, 8], F32)
    prow = ar.alloc([8, 8], F32)
    rmax8 = ar.alloc([8], F32)
    rsum8 = ar.alloc([8], F32)
    max1 = ar.alloc([8], F32)
    max2 = ar.alloc([8], F32)
    gates_tm = ar.alloc([8, 8], BF16)
    gates_fm = ar.alloc([1024], BF16, pcount=8)
    gb_bf = ar.alloc([1024], BF16)
    ffn_tm = ar.alloc([2, H], F32)
    x2_ld = ar.alloc([H], F32)
    st6 = ar.alloc([1, 4, 6], F32)
    mv = ar.alloc([1, 2], F32)
    vtmp = ar.alloc([1], F32)
    rstd = ar.alloc([1], F32)

    ps_bf = [ps[i].ap().bitcast(BF16) for i in range(8)]

    tok_p3 = [fw.q[n].tok() for n in ("pe", "ve", "se", "gp")]
    str_p3 = [("s", n, 16 * fw.streams[n].n)
              for n in ("ln2_scr", "x2_scr", "o_scr", "taps")
              if n in fw.streams]
    for q_ in (sy, gp, pe, ve, se):
        for t_ in tok_p3 + str_p3:
            q_.wait(t_)

    gp.dma("wrt", wrt_sb, wrt.rearrange("(dc p) e -> p dc e", p=128))

    wsg_r = wsg.rearrange("(dc p) m -> p dc m", p=128)
    wsu_r = wsu.rearrange("(dc p) m -> p dc m", p=128)
    wsd_r = wsd.rearrange("(fc p) m -> p fc m", p=128)
    weg_r = weg.rearrange("e (dc p) m -> e p dc m", p=128)
    weu_r = weu.rearrange("e (dc p) m -> e p dc m", p=128)
    wed_r = wed.rearrange("e (fc p) m -> e p fc m", p=128)

    br = {i: None for i in range(8)}
    bw = {i: None for i in range(8)}
    tok_guw_used = [None, None]
    tok_wd_used = [None, None]
    tok_ln2_rd = None
    tok_acc_rd = None
    tok_out_store = None
    tok_x2_rd = None
    tok_h_rd = None
    tok_sgrd = [None, None]

    ln2_r = ln2_scr.rearrange("c p t -> p c t")

    for qg2 in range(2):
        qc = slice(1024 * qg2, 1024 * (qg2 + 1))
        sy.wait(tok_ln2_rd)
        tok_ln2 = None
        for i in range(4):
            tok_ln2 = sy.dma("ln2_sb", ln2_sb[:, 4 * i:4 * i + 4, :],
                             ln2_r[:, 4 * i:4 * i + 4, qc])

        pe.wait(tok_ln2)
        pe.wait(("s", "wrt", 16 * fw.streams["wrt"].n))
        for half in range(2):
            bank = half
            pe.wait(br[bank])
            for dc in range(16):
                pe.op(lambda tt, half=half, dc=dc, bank=bank: tt.matmul(
                    ps[bank][0:8, :], wrt_sb[:, dc, :],
                    ln2_sb[:, dc, 512 * half:512 * (half + 1)],
                    start=(dc == 0), stop=(dc == 15)), inc=(dc == 15))
            bw[bank] = pe.tok()
            ve.wait(bw[bank])
            ve.op(lambda v, half=half, bank=bank: v.tensor_copy(
                lg_fm[:, 512 * half:512 * (half + 1)], ps[bank][0:8, :]))
            br[bank] = ve.tok()
        tok_lgfm = ve.tok()
        pe.wait(tok_lgfm)
        pe.wait(br[2])
        for t8 in range(8):
            pe.op(lambda tt, t8=t8: tt.transpose(
                ps[2][:, 8 * t8:8 * (t8 + 1)],
                lg_fm[:, 128 * t8:128 * (t8 + 1)], id_f32[0:8, 0:8]),
                inc=(t8 == 7))
        bw[2] = pe.tok()
        ve.wait(bw[2])
        ve.op(lambda v: v.tensor_copy(
            lg_tm[:, :, :], ps[2][:, 0:64].rearrange("p (a b) -> p a b", b=8)))
        br[2] = ve.tok()
        ve.wait(ve.tok())
        ve.op(lambda v: v.reduce_max(rmax8[:, :], lg_tm[:, :, :], axis=AX.X))
        ve.wait(ve.tok())
        ve.op(lambda v: v.tensor_tensor(
            pexp[:, :, :], lg_tm[:, :, :],
            rmax8[:, :].unsqueeze(2).broadcast_to([128, 8, 8]), ALU.subtract))
        tok_sub = ve.tok()
        se.wait(tok_sub)
        se.op(lambda s: s.activation(pexp[:, :, :], pexp[:, :, :], AF.Exp))
        tok_pexp = se.tok()
        ve.wait(tok_pexp)
        ve.op(lambda v: v.reduce_sum(rsum8[:, :], pexp[:, :, :], axis=AX.X),
              inc=False)
        ve.op(lambda v: v.reduce_max(max1[:, :], pexp[:, :, :], axis=AX.X))
        ve.wait(ve.tok())
        ve.op(lambda v: v.tensor_tensor(
            prow[:, :, :], pexp[:, :, :],
            max1[:, :].unsqueeze(2).broadcast_to([128, 8, 8]), ALU.is_ge))
        ve.wait(ve.tok())
        ve.op(lambda v: v.scalar_tensor_tensor(
            prow[:, :, :], prow[:, :, :], -1e30, pexp[:, :, :],
            ALU.mult, ALU.add))
        ve.wait(ve.tok())
        ve.op(lambda v: v.reduce_max(max2[:, :], prow[:, :, :], axis=AX.X),
              inc=False)
        ve.op(lambda v: v.reciprocal(rsum8[:, :], rsum8[:, :]))
        ve.wait(ve.tok())
        ve.op(lambda v: v.tensor_tensor(
            prow[:, :, :], pexp[:, :, :],
            max2[:, :].unsqueeze(2).broadcast_to([128, 8, 8]), ALU.is_ge))
        ve.wait(ve.tok())
        ve.op(lambda v: v.tensor_mul(prow[:, :, :], prow[:, :, :],
                                     pexp[:, :, :]))
        ve.wait(ve.tok())
        ve.op(lambda v: v.tensor_tensor(
            gates_tm[:, :, :], prow[:, :, :],
            rsum8[:, :].unsqueeze(2).broadcast_to([128, 8, 8]), ALU.mult))
        tok_gates_tm = ve.tok()
        pe.wait(tok_gates_tm)
        pe.wait(br[3])
        for t8 in range(8):
            pe.op(lambda tt, t8=t8: tt.transpose(
                ps_bf[3][0:8, 128 * t8:128 * (t8 + 1)],
                gates_tm[:, t8, :], id_bf), inc=(t8 == 7))
        bw[3] = pe.tok()
        ve.wait(bw[3])
        ve.op(lambda v: v.tensor_copy(gates_fm[:, :], ps_bf[3][0:8, 0:1024]))
        br[3] = ve.tok()
        tok_gates_fm = ve.tok()
        if "gates" in taps:
            sy.wait(tok_gates_fm)
            sy.dma("taps", taps["gates"][:, qc], gates_fm[:, :])

        for pi in range(9):
            nF = 22 if pi == 0 else 11
            if pi == 0:
                g_r, u_r, d_r = wsg_r, wsu_r, wsd_r
            else:
                g_r = weg_r[pi - 1]
                u_r = weu_r[pi - 1]
                d_r = wed_r[pi - 1]
            if pi >= 1:
                e = pi - 1
                pe.wait(tok_gates_fm)
                for half in range(2):
                    pe.wait(br[4])
                    pe.op(lambda tt, e=e, half=half: tt.matmul(
                        ps[4][:, :], sel_bf[0:8, 128 * e:128 * (e + 1)],
                        gates_fm[0:8, 512 * half:512 * (half + 1)],
                        start=True, stop=True))
                    bw[4] = pe.tok()
                    ve.wait(bw[4])
                    ve.op(lambda v, half=half: v.tensor_copy(
                        gb_bf[:, 512 * half:512 * (half + 1)], ps[4][:, :]))
                    br[4] = ve.tok()
            for f in range(nF):
                wb = f % 2
                gp.wait(tok_guw_used[wb])
                tok_wg = None
                for i in range(4):
                    tok_wg = gp.dma(f"wg{wb}", wg_sb[:, wb, 4 * i:4 * i + 4, :],
                                    g_r[:, 4 * i:4 * i + 4, 128 * f:128 * (f + 1)])
                    tok_wg = gp.dma(f"wg{wb}", wu_sb[:, wb, 4 * i:4 * i + 4, :],
                                    u_r[:, 4 * i:4 * i + 4, 128 * f:128 * (f + 1)])
                pe.wait(tok_wg)
                for half in range(2):
                    bg = (f % 2) * 4 + half * 2
                    bu = bg + 1
                    pe.wait(br[bg])
                    pe.wait(br[bu])
                    for dc in range(16):
                        pe.op(lambda tt, dc=dc, f=f, half=half, bg=bg, wb=wb:
                              tt.matmul(
                                  ps[bg][:, :], wg_sb[:, wb, dc, :],
                                  ln2_sb[:, dc, 512 * half:512 * (half + 1)],
                                  start=(dc == 0), stop=(dc == 15)),
                              inc=(dc == 15))
                        pe.op(lambda tt, dc=dc, f=f, half=half, bu=bu, wb=wb:
                              tt.matmul(
                                  ps[bu][:, :], wu_sb[:, wb, dc, :],
                                  ln2_sb[:, dc, 512 * half:512 * (half + 1)],
                                  start=(dc == 0), stop=(dc == 15)),
                              inc=(dc == 15))
                    bw[bg] = bw[bu] = pe.tok()
                    se.wait(bw[bg])
                    se.wait(tok_sgrd[half])
                    se.op(lambda s, f=f, half=half, bg=bg: s.activation(
                        sg_bf[:, half, :], ps[bg][:, :], AF.Silu))
                    tok_sg = se.tok()
                    br[bg] = tok_sg
                    ve.wait(tok_sg)
                    ve.wait(tok_h_rd)
                    ve.op(lambda v, f=f, half=half, bu=bu: v.tensor_tensor(
                        h_e[:, f, 512 * half:512 * (half + 1)], ps[bu][:, :],
                        sg_bf[:, half, :], ALU.mult))
                    br[bu] = ve.tok()
                    tok_sgrd[half] = br[bu]
                    if pi >= 1:
                        ve.wait(ve.tok())
                        ve.op(lambda v, f=f, half=half: v.tensor_mul(
                            h_e[:, f, 512 * half:512 * (half + 1)],
                            h_e[:, f, 512 * half:512 * (half + 1)],
                            gb_bf[:, 512 * half:512 * (half + 1)]))
                tok_guw_used[wb] = pe.tok()
            tok_h_done = ve.tok()
            pe.wait(tok_h_done)
            for mc in range(16):
                wb = mc % 2
                gp.wait(tok_wd_used[wb])
                tok_wd = None
                f0 = 0
                while f0 < nF:
                    f1 = min(f0 + 4, nF)
                    tok_wd = gp.dma(f"wd{wb}", wd_sb[:, wb, f0:f1, :],
                                    d_r[:, f0:f1, 128 * mc:128 * (mc + 1)])
                    f0 = f1
                pe.wait(tok_wd)
                for half in range(2):
                    bank = (mc % 2) * 2 + half
                    pe.wait(br[bank])
                    for f in range(nF):
                        pe.op(lambda tt, f=f, mc=mc, half=half, bank=bank,
                              wb=wb, nF=nF: tt.matmul(
                                  ps[bank][:, :], wd_sb[:, wb, f, :],
                                  h_e[:, f, 512 * half:512 * (half + 1)],
                                  start=(f == 0), stop=(f == nF - 1)),
                              inc=(f == nF - 1))
                    bw[bank] = pe.tok()
                    ve.wait(bw[bank])
                    if pi == 0:
                        ve.wait(tok_acc_rd)
                        ve.op(lambda v, mc=mc, half=half, bank=bank:
                              v.tensor_copy(
                                  ffn_acc[:, mc, 512 * half:512 * (half + 1)],
                                  ps[bank][:, :]))
                    else:
                        ve.op(lambda v, mc=mc, half=half, bank=bank:
                              v.tensor_add(
                                  ffn_acc[:, mc, 512 * half:512 * (half + 1)],
                                  ps[bank][:, :],
                                  ffn_acc[:, mc, 512 * half:512 * (half + 1)]))
                    br[bank] = ve.tok()
                tok_wd_used[wb] = pe.tok()
            tok_h_rd = pe.tok()
        tok_ln2_rd = pe.tok()
        tok_acc_done = ve.tok()

        for t8 in range(8):
            tb = t8 % 2
            gtile = 8 * qg2 + t8
            for mg in range(4):
                bank = 6 + mg % 2
                pe.wait(tok_acc_done)
                pe.wait(br[bank])
                for mi in range(4):
                    m = 4 * mg + mi
                    pe.op(lambda tt, m=m, mi=mi, t8=t8, bank=bank: tt.transpose(
                        ps[bank][:, 128 * mi:128 * (mi + 1)],
                        ffn_acc[:, m, 128 * t8:128 * (t8 + 1)], id_f32),
                        inc=(mi == 3))
                bw[bank] = pe.tok()
                ve.wait(bw[bank])
                ve.op(lambda v, t8=t8, mg=mg, tb=tb, bank=bank: v.tensor_copy(
                    ffn_tm[:, tb, 512 * mg:512 * (mg + 1)], ps[bank][:, :]))
                br[bank] = ve.tok()
            tok_strip = ve.tok()
            ve.wait(tok_strip)
            for k in range(4):
                ve.op(lambda v, k=k, tb=tb: v.bn_stats(
                    st6[:, 0, k, :], ffn_tm[:, tb, 512 * k:512 * (k + 1)]),
                    inc=(k == 3))
            ve.wait(ve.tok())
            ve.op(lambda v: v.bn_aggr(mv[:, 0, :], st6[:, 0, :, :].rearrange(
                "p a b -> p (a b)")))
            ve.wait(ve.tok())
            ve.op(lambda v: v.tensor_scalar_add(vtmp[:, :], mv[:, :, 1],
                                                EPS_LN))
            se.wait(ve.tok())
            se.op(lambda s: s.sqrt(vtmp[:, :], vtmp[:, :]))
            ve.wait(se.tok())
            ve.op(lambda v: v.reciprocal(rstd[:, :], vtmp[:, :]))
            ve.wait(ve.tok())
            sy.wait(tok_x2_rd)
            sy.wait(tok_out_store)
            tok_x2l = sy.dma("x2_ld", x2_ld[:, :], x2_scr[gtile])
            ve.wait(tok_x2l)
            ve.op(lambda v, tb=tb: v.tensor_scalar(
                ffn_tm[:, tb, :], ffn_tm[:, tb, :], mv[:, 0, 0:1],
                rstd[:, 0:1], ALU.subtract, ALU.mult))
            ve.wait(ve.tok())
            ve.op(lambda v, tb=tb: v.tensor_add(x2_ld[:, :], x2_ld[:, :],
                                                ffn_tm[:, tb, :]))
            tok_out = ve.tok()
            tok_x2_rd = tok_out
            sy.wait(tok_out)
            tok_out_store = sy.dma(
                "out", out[128 * gtile:128 * (gtile + 1), :], x2_ld[:, :])
        tok_acc_rd = pe.tok()

    ar.reset()
    return dict(tok_out_store=tok_out_store)


def build_p3(nc, fw, ar, EV):
    pe, ve, se, gp, sy = fw.pe, fw.ve, fw.se, fw.gp, fw.sy
    ps = fw.ps
    wo = EV["wo"]; xq = EV["xq"]; o_scr = EV["o_scr"]
    x2_scr = EV["x2_scr"]; ln2_scr = EV["ln2_scr"]
    taps = EV["taps"]; id_bf = EV["id_bf"]; id_f32 = EV["id_f32"]
    p2 = EV["p2"]

    ar.mark()
    o_sb = ar.alloc([16, 512], BF16)
    wo_sb = ar.alloc([2, 16, 128], BF16)
    attn_fm = ar.alloc([16, 512], F32)
    attn_tm = ar.alloc([4, H], F32)
    x_tm = ar.alloc([4, H], F32)
    tmp = ar.alloc([H], F32)
    ln2_bf = ar.alloc([4, H], BF16)
    ln2_fm = ar.alloc([16, 512], BF16)
    st6 = ar.alloc([4, 4, 6], F32)
    mv = ar.alloc([4, 2], F32)
    vtmp = ar.alloc([4], F32)
    rstd = ar.alloc([4], F32)

    ps_trs_bf = [ps[2].ap().bitcast(BF16), ps[3].ap().bitcast(BF16)]

    tok_p2 = [fw.q[n].tok() for n in ("pe", "ve", "se", "gp")]
    for q_ in (sy, gp, pe, ve, se):
        for t_ in tok_p2:
            q_.wait(t_)
    tok_o_final = ("s", "o_scr", 16 * fw.streams["o_scr"].n)
    sy.wait(tok_o_final)

    wo_r = wo.rearrange("(hc p) m -> p hc m", p=128)
    o_r = o_scr.rearrange("h p q -> p h q")

    tok_ln2_store = None
    tok_x2_store = None
    tok_pe_attn = {}
    tok_wo_used = [None, None]
    br = {i: None for i in range(8)}
    bw = {i: None for i in range(8)}

    for qg in range(NG_Q):
        qc = slice(512 * qg, 512 * (qg + 1))
        sy.wait(tok_pe_attn.get(qg - 1))
        tok_o = None
        for i in range(4):
            tok_o = sy.dma("o_sb", o_sb[:, 4 * i:4 * i + 4, :],
                           o_r[:, 4 * i:4 * i + 4, qc])
        for mc in range(16):
            wb = mc % 2
            gp.wait(tok_wo_used[wb])
            tok_w = None
            for i in range(4):
                tok_w = gp.dma(f"wo{wb}", wo_sb[:, wb, 4 * i:4 * i + 4, :],
                               wo_r[:, 4 * i:4 * i + 4, 128 * mc:128 * (mc + 1)])
            bank = mc % 2
            pe.wait(tok_w)
            pe.wait(tok_o)
            pe.wait(br[bank])
            for hc in range(16):
                pe.op(lambda tt, hc=hc, mc=mc, bank=bank, wb=wb: tt.matmul(
                    ps[bank][:, :], wo_sb[:, wb, hc, :], o_sb[:, hc, :],
                    start=(hc == 0), stop=(hc == 15)), inc=(hc == 15))
            bw[bank] = pe.tok()
            tok_wo_used[wb] = bw[bank]
            ve.wait(bw[bank])
            ve.op(lambda v, mc=mc, bank=bank: v.tensor_copy(
                attn_fm[:, mc, :], ps[bank][:, :]))
            br[bank] = ve.tok()
        tok_attn_fm = ve.tok()
        tok_pe_attn[qg] = pe.tok()

        pe.wait(tok_attn_fm)
        for mc in range(16):
            bank = 2 + mc % 2
            pe.wait(br[bank])
            for t4 in range(4):
                pe.op(lambda tt, mc=mc, t4=t4, bank=bank: tt.transpose(
                    ps[bank][:, 128 * t4:128 * (t4 + 1)],
                    attn_fm[:, mc, 128 * t4:128 * (t4 + 1)], id_f32),
                    inc=(t4 == 3))
            bw[bank] = pe.tok()
            ve.wait(bw[bank])
            ve.op(lambda v, mc=mc, bank=bank: v.tensor_copy(
                attn_tm[:, :, 128 * mc:128 * (mc + 1)],
                ps[bank][:, :].rearrange("p (a b) -> p a b", b=128)))
            br[bank] = ve.tok()
        tok_attn_tm = ve.tok()

        sy.wait(tok_x2_store)
        tok_x = None
        for t in range(4):
            tok_x = sy.dma("x3_tm", x_tm[:, t, :],
                           xq[512 * qg + 128 * t:512 * qg + 128 * (t + 1), :])
        ve.wait(tok_attn_tm)
        tok_stats = emit_ln_tm(fw, attn_tm, None, st6, mv, vtmp, rstd, 4, H,
                               EPS_LN)
        ve.wait(tok_stats)
        ve.wait(tok_x)
        for t in range(4):
            ve.op(lambda v, t=t: v.tensor_scalar(
                tmp[:, :], attn_tm[:, t, :], mv[:, t, 0:1], rstd[:, t:t + 1],
                ALU.subtract, ALU.mult))
            ve.wait(ve.tok())
            ve.op(lambda v, t=t: v.tensor_add(x_tm[:, t, :], x_tm[:, t, :],
                                              tmp[:, :]))
            ve.wait(ve.tok())
        tok_x2 = ve.tok()
        sy.wait(tok_x2)
        for t in range(4):
            tok_x2_store = sy.dma("x2_scr", x2_scr[4 * qg + t], x_tm[:, t, :])
        if "x2" in taps:
            for t in range(4):
                tok_x2_store = sy.dma("x2_scr", taps["x2"][4 * qg + t],
                                      x_tm[:, t, :])
        tok_ln2 = emit_ln_tm(fw, x_tm, ln2_bf, st6, mv, vtmp, rstd, 4, H,
                             EPS_LN)
        tok_fm, tok_tr_pe = emit_transpose_tm2fm(
            fw, ln2_bf, ln2_fm, 4, H, ps_trs_bf, id_bf, tok_ln2,
            tok_dst_free=tok_ln2_store)
        sy.wait(tok_fm)
        tok_st = None
        for i in range(4):
            tok_st = sy.dma("ln2_scr", ln2_scr[:, :, qc].rearrange(
                "c p t -> p c t")[:, 4 * i:4 * i + 4, :],
                ln2_fm[:, 4 * i:4 * i + 4, :])
            if "ln2" in taps:
                tok_st = sy.dma("ln2_scr", taps["ln2"][:, :, qc].rearrange(
                    "c p t -> p c t")[:, 4 * i:4 * i + 4, :],
                    ln2_fm[:, 4 * i:4 * i + 4, :])
        tok_ln2_store = ("c", "ve", 0)
        tok_ln2_store = tok_st

    ar.reset()
    return dict(tok_x2_store=tok_x2_store, tok_ln2_store=tok_ln2_store)


def build_attention(nc, fw, ar, EV):
    pe, ve, se, gp, sy = fw.pe, fw.ve, fw.se, fw.gp, fw.sy
    ps = fw.ps
    qn = EV["qn"]; c_norm = EV["c_norm"]; k_pe = EV["k_pe"]
    wqb = EV["wqb"]; wkvb = EV["wkvb"]
    cosq_in = EV["cosq_in"]; sinqm_in = EV["sinqm_in"]; dmask_in = EV["dmask_in"]
    o_scr = EV["o_scr"]; taps = EV["taps"]; id_bf = EV["id_bf"]
    ones_bf = EV["ones_bf"]; id_f32 = EV["id_f32"]
    tok_p1b_ve = EV["tok_p1b_ve"]; tok_p1b_pe = EV["tok_p1b_pe"]
    tok_p1b_se = EV["tok_p1b_se"]

    ar.mark()
    cosq = ar.alloc([TQ], BF16, pcount=DR)
    sinqm = ar.alloc([TQ], BF16, pcount=DR)
    masks = ar.alloc([2, 512], F32)
    knope = ar.alloc([S], BF16)
    v_tm = ar.alloc([32, 128], BF16)
    qh_n = ar.alloc([TQ], BF16)
    qpe_f = ar.alloc([TQ], F32, pcount=DR)
    qpe_p = ar.alloc([TQ], F32, pcount=DR)
    qpe_bf = ar.alloc([TQ], BF16, pcount=DR)
    p_bf2 = ar.alloc([2, S], BF16)
    pT = ar.alloc([32, 128], BF16)
    o_buf = ar.alloc([1, TQ], BF16)
    wqb_sb = ar.alloc([2, 12, 256], BF16)
    wkvb_sb = ar.alloc([2, 4, 256], BF16)
    sq_kpe = ar.alloc([S], BF16, pcount=DR)
    sq_qpe = ar.alloc([TQ], BF16, pcount=DR)
    gblk = ar.alloc([8], F32, pcount=1)
    gmax = ar.alloc([1], F32, pcount=1)
    nrow = qpe_p[0:1, :]
    bias_t = ar.alloc([16], F32)
    rsb = ar.alloc([2, 8], F32)
    rsum = ar.alloc([2, 1], F32)
    rs = ar.alloc([2, 1], F32)
    diag_rs = ar.alloc([2, 128], BF16)

    ps_trs = [ps[6].ap(), ps[7].ap(), ps[3].ap()]

    for q_ in (sy, gp):
        q_.wait(tok_p1b_ve)
        q_.wait(tok_p1b_pe)
        q_.wait(tok_p1b_se)

    gp.dma("qrope", cosq, cosq_in)
    tok_qt = gp.dma("qrope", sinqm, sinqm_in)
    tok_mask = sy.dma("dmask2", masks, dmask_in.rearrange("j p n -> p j n"))

    wqb_r = wqb.rearrange("(qc p) m -> p qc m", p=128)
    wkvb_r = wkvb.rearrange("(cc p) m -> p cc m", p=128)

    bw = {i: None for i in range(8)}
    br = {i: tok_p1b_ve for i in range(8)}
    br_se = {i: tok_p1b_se for i in range(8)}

    def wstream(h):
        b = h % 2
        t = None
        for i in range(3):
            t = gp.dma(f"wqb{b}", wqb_sb[:, b, 4 * i:4 * i + 4, :],
                       wqb_r[:, 4 * i:4 * i + 4, 256 * h:256 * (h + 1)])
        t2 = gp.dma(f"wkvb{b}", wkvb_sb[:, b, :, :],
                    wkvb_r[:, :, 256 * h:256 * (h + 1)])
        return t, t2

    se.wait(tok_p1b_se)
    se.wait(("c", "ve", EV["tok_p1b_ve"][2]) if False else tok_p1b_ve)
    se.op(lambda s: s.square(sq_kpe[:, :], k_pe[:, :]))
    tok_sq_kpe = se.tok()

    tok_w = {0: wstream(0)}
    tok_o_store = {}
    tok_kv_users = tok_p1b_pe
    tok_pT_pe = None
    tok_av_prev = None
    tok_sq_free = None
    tok_bias_guard = None
    tok_pT_buf = [None, None]
    tok_rs_buf = [None, None]
    tok_diag_rd = [None, None]

    for h in range(NH):
        b = h % 2
        if h + 1 < NH:
            if h >= 1:
                gp.wait(tok_w_used[(h + 1) % 2])
            tok_w[h + 1] = wstream(h + 1)
        if h == 0:
            tok_w_used = {}

        pe.wait(tok_w[h][0])
        pe.wait(tok_w[h][1])
        tok_ev_kn = {}
        for kb in range(8):
            bank = 6 + kb % 2
            pe.wait(br[bank])
            for cc in range(4):
                pe.op(lambda tt, kb=kb, cc=cc, bank=bank, b=b: tt.matmul(
                    ps[bank][:, :], wkvb_sb[:, b, cc, 0:128],
                    c_norm[:, cc, 512 * kb:512 * (kb + 1)],
                    start=(cc == 0), stop=(cc == 3)), inc=(cc == 3))
            bw[bank] = pe.tok()
            ve.wait(bw[bank])
            ve.wait(tok_kv_users)
            ve.op(lambda v, kb=kb, bank=bank: v.tensor_copy(
                knope[:, 512 * kb:512 * (kb + 1)], ps[bank][:, :]))
            br[bank] = ve.tok()
        tok_knope_ev = ve.tok()

        sq_scr = p_bf2[:, 0, :]
        se.wait(tok_knope_ev)
        se.wait(tok_sq_free)
        se.wait(tok_pT_buf[0])
        se.wait(tok_pT_buf[1])
        se.wait(tok_sq_kpe)
        se.op(lambda s: s.square(sq_scr[:, :], knope[:, :]))
        tok_sqk = se.tok()

        for grp in range(8):
            bank = 4 + grp % 2
            pe.wait(br[bank])
            for kt in range(4):
                ki = 4 * grp + kt
                for cc in range(4):
                    pe.op(lambda tt, ki=ki, kt=kt, cc=cc, bank=bank, b=b: tt.matmul(
                        ps[bank][:, 128 * kt:128 * (kt + 1)],
                        c_norm[:, cc, 128 * ki:128 * (ki + 1)],
                        wkvb_sb[:, b, cc, 128:256],
                        start=(cc == 0), stop=(cc == 3)),
                        inc=(kt == 3 and cc == 3))
            bw[bank] = pe.tok()
            ve.wait(bw[bank])
            ve.op(lambda v, grp=grp, bank=bank: v.tensor_copy(
                v_tm[:, 4 * grp:4 * (grp + 1), :].rearrange("p a b -> p (a b)"),
                ps[bank][:, :]))
            br[bank] = ve.tok()
        tok_v_ev = ve.tok()

        pe.wait(tok_sqk)
        for blk in range(8):
            bank = 6 + blk % 2
            pe.wait(br[bank])
            pe.wait(br_se[bank])
            pe.op(lambda tt, blk=blk, bank=bank: tt.matmul(
                ps[bank][0:1, 0:512], ones_bf[:, 0:1],
                sq_scr[:, 512 * blk:512 * (blk + 1)],
                start=True, stop=False), inc=False)
            pe.op(lambda tt, blk=blk, bank=bank: tt.matmul(
                ps[bank][0:1, 0:512], ones_bf[0:DR, 0:1],
                sq_kpe[:, 512 * blk:512 * (blk + 1)],
                start=False, stop=True))
            bw[bank] = pe.tok()
            ve.wait(bw[bank])
            ve.op(lambda v, blk=blk, bank=bank: v.reduce_max(
                gblk[:, blk:blk + 1], ps[bank][0:1, 0:512], axis=AX.X))
            br[bank] = ve.tok()
        ve.wait(ve.tok())
        ve.op(lambda v: v.reduce_max(gmax[:, :], gblk[:, :], axis=AX.X))
        se.wait(ve.tok())
        se.wait(se.tok())
        se.op(lambda s: s.sqrt(gmax[:, :], gmax[:, :]))
        tok_G = se.tok()
        tok_sq_k_done = pe.tok()

        tok_qn_ev = {}
        for qg in range(4):
            for part in range(2):
                bank = 4 + (qg * 2 + part) % 2
                pe.wait(br[bank])
                for qc in range(12):
                    pe.op(lambda tt, qg=qg, part=part, qc=qc, bank=bank, b=b: tt.matmul(
                        ps[bank][:, :], wqb_sb[:, b, qc, 128 * part:128 * (part + 1)],
                        qn[:, qc, 512 * qg:512 * (qg + 1)],
                        start=(qc == 0), stop=(qc == 11)), inc=(qc == 11))
                bw[bank] = pe.tok()
                ve.wait(bw[bank])
                if part == 0:
                    ve.op(lambda v, qg=qg, bank=bank: v.tensor_copy(
                        qh_n[:, 512 * qg:512 * (qg + 1)], ps[bank][:, :]))
                else:
                    ve.op(lambda v, qg=qg, bank=bank: v.tensor_copy(
                        qpe_f[:, 512 * qg:512 * (qg + 1)], ps[bank][0:DR, :]),
                        inc=False)
                    ve.op(lambda v, qg=qg, bank=bank: v.tensor_copy(
                        qpe_p[:, 512 * qg:512 * (qg + 1)], ps[bank][DR:2 * DR, :]))
                br[bank] = ve.tok()
        ve.wait(tok_qt)
        ve.wait(ve.tok())
        ve.op(lambda v: v.tensor_mul(qpe_p[:, :], qpe_p[:, :], sinqm[:, :]),
              inc=False)
        ve.op(lambda v: v.tensor_mul(qpe_f[:, :], qpe_f[:, :], cosq[:, :]))
        ve.wait(ve.tok())
        ve.op(lambda v: v.tensor_add(qpe_bf[:, :], qpe_p[:, :], qpe_f[:, :]))
        tok_q_ready = ve.tok()

        se.wait(tok_sq_k_done)
        se.wait(tok_q_ready)
        se.op(lambda s: s.square(sq_scr[:, 0:TQ], qh_n[:, :]), inc=False)
        se.op(lambda s: s.square(sq_qpe[:, :], qpe_bf[:, :]))
        tok_sqq = se.tok()
        pe.wait(tok_sqq)
        for qg in range(4):
            bank = 6 + qg % 2
            pe.wait(br[bank])
            pe.wait(br_se[bank])
            pe.op(lambda tt, qg=qg, bank=bank: tt.matmul(
                ps[bank][0:1, 0:512], ones_bf[:, 0:1],
                sq_scr[:, 512 * qg:512 * (qg + 1)],
                start=True, stop=False), inc=False)
            pe.op(lambda tt, qg=qg, bank=bank: tt.matmul(
                ps[bank][0:1, 0:512], ones_bf[0:DR, 0:1],
                sq_qpe[:, 512 * qg:512 * (qg + 1)],
                start=False, stop=True))
            bw[bank] = pe.tok()
            se.wait(bw[bank])
            se.wait(tok_q_ready)
            se.op(lambda s, qg=qg, bank=bank: s.sqrt(
                nrow[:, 512 * qg:512 * (qg + 1)], ps[bank][0:1, 0:512]))
            br_se[bank] = se.tok()
        tok_sq_free = pe.tok()
        tok_nrow = se.tok()
        ve.wait(tok_nrow)
        ve.wait(tok_G)
        ve.op(lambda v: v.tensor_scalar(
            nrow[:, :], nrow[:, :], gmax[:, :], None, ALU.mult))
        ve.wait(ve.tok())
        ve.op(lambda v: v.tensor_scalar(
            nrow[:, :], nrow[:, :], -SOFTMAX_SCALE, EXP_SHIFT,
            ALU.mult, ALU.add))
        tok_brow = ve.tok()
        pe.wait(tok_brow)
        pe.wait(br[6])
        pe.wait(br_se[6])
        for t16 in range(16):
            pe.op(lambda tt, t16=t16: tt.transpose(
                ps[6][:, t16:t16 + 1], nrow[0:1, 128 * t16:128 * (t16 + 1)],
                id_f32[0:1, 0:1]), inc=(t16 == 15))
        bw[6] = pe.tok()
        ve.wait(bw[6])
        ve.wait(tok_bias_guard)
        ve.op(lambda v: v.tensor_copy(bias_t[:, :], ps[6][:, 0:16]))
        br[6] = ve.tok()
        tok_bias = ve.tok()

        def emit_blocks(j):
            nb = j // 2 + 1
            tb = j % 2
            pe.wait(tok_q_ready)
            pe.wait(tok_knope_ev)
            ve.wait(tok_mask)
            se.wait(tok_bias)
            se.wait(tok_pT_buf[tb])
            se.wait(tok_rs_buf[tb])
            for kb in range(nb):
                bank = scb[j][kb]
                pe.wait(br[bank])
                pe.wait(br_se[bank])
                pe.op(lambda tt, j=j, kb=kb, bank=bank: tt.matmul(
                    ps[bank][:, :], qh_n[:, 128 * j:128 * (j + 1)],
                    knope[:, 512 * kb:512 * (kb + 1)],
                    start=True, stop=False), inc=False)
                pe.op(lambda tt, j=j, kb=kb, bank=bank: tt.matmul(
                    ps[bank][:, :], qpe_bf[:, 128 * j:128 * (j + 1)],
                    k_pe[:, 512 * kb:512 * (kb + 1)],
                    start=False, stop=True))
                bw[bank] = pe.tok()
                if kb == nb - 1:
                    ve.wait(bw[bank])
                    ve.op(lambda v, j=j, bank=bank: v.tensor_tensor(
                        ps[bank][:, :], ps[bank][:, :], masks[:, j % 2, :],
                        ALU.add))
                    br[bank] = ve.tok()
                    se.wait(br[bank])
                else:
                    se.wait(bw[bank])
                se.op(lambda s, j=j, kb=kb, bank=bank, tb=tb: s.activation(
                    p_bf2[:, tb, 512 * kb:512 * (kb + 1)], ps[bank][:, :],
                    AF.Exp, bias=bias_t[:, j:j + 1], scale=SOFTMAX_SCALE,
                    accum_out=rsb[:, tb, kb:kb + 1]))
                br_se[bank] = se.tok()

        def emit_pTAV(j):
            nonlocal tok_pT_pe, tok_av_prev
            nb = j // 2 + 1
            tb_ = j % 2
            pe.wait(tok_norm_t[j])
            grp_ev = {}
            for grp in range(nb):
                bank = (6, 7, 3)[grp % 3]
                tbank = ps_trs[grp % 3]
                pe.wait(br[bank])
                pe.wait(br_se[bank])
                for t4 in range(4):
                    pe.op(lambda tt, grp=grp, t4=t4, tbank=tbank, tb_=tb_:
                          tt.matmul(
                              tbank[:, 128 * t4:128 * (t4 + 1)],
                              p_bf2[:, tb_,
                                    512 * grp + 128 * t4:512 * grp + 128 * (t4 + 1)],
                              diag_rs[:, tb_, :],
                              start=True, stop=True), inc=(t4 == 3))
                bw[bank] = pe.tok()
                eng = ve if grp % 2 == 0 else se
                eng.wait(bw[bank])
                eng.op(lambda v, grp=grp, tbank=tbank: v.tensor_copy(
                    pT[:, 4 * grp:4 * (grp + 1), :].rearrange("p a b -> p (a b)"),
                    tbank[:, 0:512])
                       if grp % 2 == 0 else
                       v.copy(pT[:, 4 * grp:4 * (grp + 1), :].rearrange(
                           "p a b -> p (a b)"), tbank[:, 0:512]))
                if grp % 2 == 0:
                    br[bank] = eng.tok()
                else:
                    br_se[bank] = eng.tok()
                grp_ev[grp] = (eng.tok())
            tok_pT_buf[tb_] = pe.tok()
            tok_diag_rd[tb_] = tok_pT_buf[tb_]
            tok_pT_pe = tok_pT_buf[tb_]
            av_bank = 4 + j % 2
            pe.wait(tok_v_ev)
            pe.wait(br[av_bank])
            for kc in range(4 * nb):
                pe.wait(grp_ev[kc // 4])
                pe.op(lambda tt, kc=kc, av_bank=av_bank, nb=nb: tt.matmul(
                    ps[av_bank][:, 0:128], v_tm[:, kc, :], pT[:, kc, :],
                    start=(kc == 0), stop=(kc == 4 * nb - 1)),
                    inc=(kc == 4 * nb - 1))
            bw[av_bank] = pe.tok()
            tok_av_prev = bw[av_bank]
            ve.wait(bw[av_bank])
            if h >= 1:
                ve.wait(tok_o_store.get(h - 1))
            ve.op(lambda v, j=j, av_bank=av_bank: v.tensor_copy(
                o_buf[:, 0, 128 * j:128 * (j + 1)], ps[av_bank][:, 0:128]))
            br[av_bank] = ve.tok()

        scb = {}
        rb = 0
        for j in range(16):
            nb = j // 2 + 1
            scb[j] = [(rb + i) % 3 for i in range(nb)]
            rb = (rb + nb) % 3

        tok_norm_t = {}
        for j in range(16):
            nb = j // 2 + 1
            tb = j % 2
            emit_blocks(j)
            if j >= 1:
                emit_pTAV(j - 1)
            tok_exp = se.tok()
            ve.wait(tok_exp)
            ve.wait(tok_diag_rd[tb])
            ve.op(lambda v, nb=nb, tb=tb: v.reduce_sum(
                rsum[:, tb, :], rsb[:, tb, 0:nb], axis=AX.X))
            ve.wait(ve.tok())
            ve.op(lambda v, tb=tb: v.reciprocal(rs[:, tb, :], rsum[:, tb, :]))
            tok_rs_buf[tb] = ve.tok()
            ve.wait(tok_rs_buf[tb])
            ve.op(lambda v, tb=tb: v.tensor_mul(
                diag_rs[:, tb, :], id_bf[:, :],
                rs[:, tb, :].broadcast_to([128, 128])))
            tok_norm_t[j] = ve.tok()
        emit_pTAV(15)
        tok_bias_guard = se.tok()

        tok_kv_users = pe.tok()
        tok_w_used[b] = pe.tok()
        tok_o_ev = ve.tok()
        sy.wait(tok_o_ev)
        tok_o_store[h] = sy.dma("o_scr", o_scr[h], o_buf[:, 0, :])
        if "o" in taps:
            tok_o_store[h] = sy.dma("o_scr", taps["o"][h], o_buf[:, 0, :])

    ar.reset()
    return dict(tok_o_store=tok_o_store)


DEBUG_SHAPES = {
    "c_norm": ([KVL // 128, 128, S], BF16),
    "k_pe": ([DR, S], BF16),
    "qn": ([QL // 128, 128, TQ], BF16),
    "o": ([NH, DV, TQ], BF16),
    "x2": ([TQ // 128, 128, H], F32),
    "ln2": ([H // 128, 128, TQ], BF16),
    "gates": ([8, TQ], BF16),
}



def _rope_tables():
    inv = 1.0 / ROPE_THETA ** (np.arange(0, DR, 2, dtype=np.float64) / DR)
    t = np.arange(S, dtype=np.float64)
    fr = np.outer(t, inv)
    emb = np.concatenate([fr, fr], axis=1)
    cos = np.cos(emb).astype(np.float32)
    sin = np.sin(emb).astype(np.float32)
    sinm = sin.copy()
    sinm[:, :DR // 2] = -sinm[:, :DR // 2]
    return np.ascontiguousarray(cos.T), np.ascontiguousarray(sinm.T)


def _own_rows(half):
    rows = []
    for j in range(16):
        g = 2 * j + half
        rows.extend(range(128 * g, 128 * (g + 1)))
    return np.array(rows)


def make_in_maps(inputs):
    cos_t, sinm_t = _rope_tables()
    gamq = np.ascontiguousarray(
        np.asarray(inputs["L0_attn_q_a_ln_w"], np.float32).reshape(QL // 128, 128).T)
    gamkv = np.ascontiguousarray(
        np.asarray(inputs["L0_attn_kv_a_ln_w"], np.float32).reshape(KVL // 128, 128).T)
    ident = np.eye(128, dtype=np.float32)
    ones = np.ones((128, 128), dtype=np.float32)
    sel = np.zeros((8, E * 128), dtype=np.float32)
    for e in range(E):
        sel[e, 128 * e:128 * (e + 1)] = 1.0
    x = np.asarray(inputs["L0_attn_x"], dtype=np.float32)
    wkva0 = np.asarray(inputs["L0_attn_wkv_a"], np.float32)
    wkva_ext = np.concatenate(
        [wkva0, wkva0[:, KVL + 32:KVL + 64], wkva0[:, KVL:KVL + 32]], axis=1)
    wqb0 = np.asarray(inputs["L0_attn_wq_b"], np.float32)
    wqb_h = wqb0.reshape(QL, NH, QH)
    wqb_ext = np.concatenate(
        [wqb_h, wqb_h[:, :, DN + 32:DN + 64], wqb_h[:, :, DN:DN + 32]],
        axis=2).reshape(QL, NH * 256)
    shared = dict(
        wqa=np.asarray(inputs["L0_attn_wq_a"], np.float32),
        wqb=np.ascontiguousarray(wqb_ext),
        wkva=np.ascontiguousarray(wkva_ext),
        wkvb=np.asarray(inputs["L0_attn_wkv_b"], np.float32),
        wo=np.asarray(inputs["L0_attn_wo"], np.float32),
        wrt=np.asarray(inputs["L0_ffn_W_router"], np.float32),
        weg=np.asarray(inputs["L0_ffn_We_gate"], np.float32),
        weu=np.asarray(inputs["L0_ffn_We_up"], np.float32),
        wed=np.asarray(inputs["L0_ffn_We_down"], np.float32),
        wsg=np.asarray(inputs["L0_ffn_Ws_gate"], np.float32),
        wsu=np.asarray(inputs["L0_ffn_Ws_up"], np.float32),
        wsd=np.asarray(inputs["L0_ffn_Ws_down"], np.float32),
        gamq=gamq, gamkv=gamkv, id_in=ident, ones_in=ones, sel_in=sel,
        cosk=cos_t, sinkm=sinm_t,
    )
    in_maps = []
    for c in range(8):
        b, half = c // 2, c % 2
        rows = _own_rows(half)
        dmask = np.zeros((2, 128, 512), dtype=np.float32)
        for jp in range(2):
            off = ((2 * jp + half) % 4) * 128
            r = np.arange(128)[:, None]
            k = np.arange(512)[None, :]
            dmask[jp] = np.where(k <= off + r, 0.0, NEG)
        m = dict(shared)
        m["xb"] = np.ascontiguousarray(x[b])
        m["xq"] = np.ascontiguousarray(x[b][rows])
        m["cosq"] = np.ascontiguousarray(cos_t[:, rows])
        m["sinqm"] = np.ascontiguousarray(sinm_t[:, rows])
        m["dmask"] = dmask
        in_maps.append(m)
    return in_maps


_NC_CACHE = {}


def _get_nc():
    key = (tuple(DEBUG_TAPS), PHASES, SKIP_P1B, P1A_CUT, P1B_CUT, P1B_SUB)
    if key not in _NC_CACHE:
        _NC_CACHE[key] = build_program()
    return _NC_CACHE[key]


def kernel(**inputs):
    nc = _get_nc()
    in_maps = make_in_maps(inputs)
    res = run_bass_kernel_spmd(nc, in_maps, core_ids=list(range(8)))
    out = np.empty((B, S, H), dtype=np.float32)
    for c in range(8):
        b, half = c // 2, c % 2
        rows = _own_rows(half)
        out[b][rows] = np.asarray(res.results[c]["out"])
    kernel.last_results = res
    return out


# version 09
# speedup vs baseline: 1.1362x; 1.1362x over previous
import numpy as np
import ml_dtypes
from contextlib import ExitStack

import concourse.bass as bass
import concourse.mybir as mybir
from concourse.bass_utils import run_bass_kernel_spmd

F32 = mybir.dt.float32
BF16 = mybir.dt.bfloat16
U8 = mybir.dt.uint8
AF = mybir.ActivationFunctionType
ALU = mybir.AluOpType
AX = mybir.AxisListType

B, S, H = 4, 4096, 2048
NH = 16
QL = 1536
KVL = 512
DR = 64
DN = 128
DV = 128
QH = DN + DR
SOFTMAX_SCALE = QH ** -0.5
ROPE_THETA = 1e4
EPS_RMS = 1e-6
EPS_LN = 1e-5
E = 8
DE = 1408
DSH = 2816
TQ = 2048
NG_B = S // 512
NG_Q = TQ // 512
NEG = -1.0e30
EXP_SHIFT = 24.0

DEBUG_TAPS = []
PHASES = 4
SKIP_P1B = False
P1A_CUT = 99
P1B_CUT = 99
P1B_SUB = 3



class Stream:

    def __init__(self, name):
        self.name = name
        self.n = 0
        self.sem = None


class EQ:

    def __init__(self, name):
        self.name = name
        self.ops = []
        self.n_inc = 0
        self.waited = {}

    def op(self, fn, inc=True):
        if inc:
            self.n_inc += 1
            me = self

            def run(eng, fn=fn, me=me):
                fn(eng).then_inc(me.sem, 1)
            self.ops.append(run)
            return ("c", self.name, self.n_inc)
        self.ops.append(lambda eng, fn=fn: fn(eng))
        return None

    def dma(self, stream, out, in_, **kw):
        st = self.fw.stream(stream)
        st.n += 1

        def run(eng, out=out, in_=in_, kw=kw, st=st):
            eng.dma_start(out=out, in_=in_, **kw).then_inc(st.sem, 16)
        self.ops.append(run)
        return ("s", st.name, 16 * st.n)

    def wait(self, tok):
        if tok is None:
            return
        kind, qname, val = tok
        assert not (qname == self.name and kind == "c") or val <= self.n_inc
        key = (kind, qname)
        if self.waited.get(key, -1) >= val:
            return
        self.waited[key] = val
        fw = self.fw

        def run(eng, kind=kind, qname=qname, val=val, fw=fw):
            sem = fw.q[qname].sem if kind == "c" else fw.streams[qname].sem
            eng.wait_ge(sem, val)
        self.ops.append(run)

    def tok(self):
        return ("c", self.name, self.n_inc)


class Arena:
    def __init__(self, ap, nbytes):
        self.ap = ap
        self.nbytes = nbytes
        self.off = 0
        self.marks = []
        self.peak = 0

    def alloc(self, free_shape, dtype, pbase=0, pcount=128, align=64):
        esz = {F32: 4, BF16: 2, U8: 1}[dtype]
        n = 1
        for d in free_shape:
            n *= d
        nb = n * esz
        self.off = (self.off + align - 1) // align * align
        assert self.off + nb <= self.nbytes, (
            f"arena overflow: need {(self.off + nb)/1024:.0f}KB")
        v = self.ap[pbase:pbase + pcount, self.off:self.off + nb].bitcast(dtype)
        if len(free_shape) > 1:
            names = " ".join(f"d{i}" for i in range(len(free_shape)))
            kw = {f"d{i}": d for i, d in enumerate(free_shape)}
            v = v.rearrange(f"p ({names}) -> p {names}", **kw)
        self.off += nb
        self.peak = max(self.peak, self.off)
        return v

    def mark(self):
        self.marks.append(self.off)

    def reset(self):
        self.off = self.marks.pop()


class FW:
    def __init__(self, nc, es):
        self.nc = nc
        self.q = {n: EQ(n) for n in ("pe", "ve", "se", "gp", "sy")}
        for q in self.q.values():
            q.fw = self
        self.pe, self.ve, self.se = self.q["pe"], self.q["ve"], self.q["se"]
        self.gp, self.sy = self.q["gp"], self.q["sy"]
        self.es = es
        self.streams = {}
        for n, q in self.q.items():
            q.sem = es.enter_context(nc.semaphore(f"s_{n}"))
        self.ps = [es.enter_context(nc.psum_tensor(f"ps{i}", [128, 512], F32))
                   for i in range(8)]
        ARENA_B = 207 * 1024
        self.ar_t = es.enter_context(nc.sbuf_tensor("arena", [128, ARENA_B], U8))
        self.ar = Arena(self.ar_t.ap(), ARENA_B)

    def stream(self, name):
        if name not in self.streams:
            st = Stream(name)
            st.sem = self.es.enter_context(self.nc.semaphore(f"st_{name}"))
            self.streams[name] = st
        return self.streams[name]

    def finalize(self, blk):
        for name, dec in (("sy", blk.sync), ("pe", blk.tensor), ("ve", blk.vector),
                          ("se", blk.scalar), ("gp", blk.gpsimd)):
            q = self.q[name]

            def section(eng, q=q):
                for fn in q.ops:
                    fn(eng)
            dec(section)


def emit_ln_tm(fw, x_tm, out_bf, st6, mv, vtmp, rstd, nt, D, eps):
    ve, se = fw.ve, fw.se
    nk = D // 512
    for t in range(nt):
        for k in range(nk):
            ve.op(lambda v, t=t, k=k: v.bn_stats(
                st6[:, t, k, :], x_tm[:, t, 512 * k:512 * (k + 1)]),
                inc=(t == nt - 1 and k == nk - 1))
    ve.wait(ve.tok())
    for t in range(nt):
        ve.op(lambda v, t=t: v.bn_aggr(
            mv[:, t, :], st6[:, t, :, :].rearrange("p a b -> p (a b)")),
            inc=(t == nt - 1))
    ve.wait(ve.tok())
    ve.op(lambda v: v.tensor_scalar_add(vtmp[:, :], mv[:, :, 1], eps))
    se.wait(ve.tok())
    se.op(lambda s: s.sqrt(vtmp[:, :], vtmp[:, :]))
    ve.wait(se.tok())
    ve.op(lambda v: v.reciprocal(rstd[:, :], vtmp[:, :]))
    ve.wait(ve.tok())
    if out_bf is not None:
        for t in range(nt):
            ve.op(lambda v, t=t: v.tensor_scalar(
                out_bf[:, t, :], x_tm[:, t, :], mv[:, t, 0:1],
                rstd[:, t:t + 1], ALU.subtract, ALU.mult), inc=(t == nt - 1))
    return ve.tok()


def emit_transpose_tm2fm(fw, src_bf, dst_fm, nt, D, ps_trs, id_bf, tok_src,
                         tok_dst_free=None):
    pe, ve = fw.pe, fw.ve
    pe.wait(tok_src)
    ve.wait(tok_dst_free)
    nchunk = D // 128
    for c in range(nchunk):
        bank = ps_trs[c % 2]
        if c >= 2:
            pe.wait(ve.tok())
        for t in range(nt):
            pe.op(lambda tt, c=c, t=t, bank=bank: tt.transpose(
                bank[:, 128 * t: 128 * (t + 1)],
                src_bf[:, t, 128 * c:128 * (c + 1)], id_bf),
                inc=(t == nt - 1))
        ve.wait(pe.tok())
        ve.op(lambda v, c=c, bank=bank, nt=nt: v.tensor_copy(
            dst_fm[:, c, 0:128 * nt], bank[:, 0:128 * nt]))
    return ve.tok(), pe.tok()


def build_program():
    nc = bass.Bass("TRN2", target_bir_lowering=False)

    dr = {}

    def din(name, shape, dtype=F32):
        dr[name] = nc.dram_tensor(name, shape, dtype, kind="ExternalInput").ap()
        return dr[name]

    xb = din("xb", [S, H])
    xq = din("xq", [TQ, H])
    wqa = din("wqa", [H, QL])
    wqb = din("wqb", [QL, NH * 256])
    wkva = din("wkva", [H, KVL + 2 * DR])
    wkvb = din("wkvb", [KVL, NH * (DN + DV)])
    wo = din("wo", [NH * DV, H])
    wrt = din("wrt", [H, E])
    weg = din("weg", [E, H, DE])
    weu = din("weu", [E, H, DE])
    wed = din("wed", [E, DE, H])
    wsg = din("wsg", [H, DSH])
    wsu = din("wsu", [H, DSH])
    wsd = din("wsd", [DSH, H])
    gamq_in = din("gamq", [128, QL // 128])
    gamkv_in = din("gamkv", [128, KVL // 128])
    id_in = din("id_in", [128, 128])
    ones_in = din("ones_in", [128, 128])
    sel_in = din("sel_in", [8, E * 128])
    cosk_in = din("cosk", [DR, S])
    sinkm_in = din("sinkm", [DR, S])
    cosq_in = din("cosq", [DR, TQ])
    sinqm_in = din("sinqm", [DR, TQ])
    dmask_in = din("dmask", [2, 128, 512])

    out = nc.dram_tensor("out", [TQ, H], F32, kind="ExternalOutput").ap()
    o_scr = nc.dram_tensor("o_scr", [NH, DV, TQ], BF16, kind="Internal").ap()
    x2_scr = nc.dram_tensor("x2_scr", [TQ // 128, 128, H], F32, kind="Internal").ap()
    ln2_scr = nc.dram_tensor("ln2_scr", [H // 128, 128, TQ], BF16, kind="Internal").ap()

    taps = {}
    for t in DEBUG_TAPS:
        shape, dtype = DEBUG_SHAPES[t]
        taps[t] = nc.dram_tensor("tap_" + t, shape, dtype, kind="ExternalOutput").ap()

    es = ExitStack()
    with es:
        fw = FW(nc, es)
        ar = fw.ar
        pe, ve, se, gp, sy = fw.pe, fw.ve, fw.se, fw.gp, fw.sy
        ps = fw.ps
        ps_trs = [ps[5].ap().bitcast(BF16), ps[6].ap().bitcast(BF16)]
        ps_row = ps[7]

        id_bf = ar.alloc([128], BF16)
        id_f32 = ar.alloc([128], F32)
        ones_bf = ar.alloc([128], BF16)
        sel_bf = ar.alloc([E * 128], BF16, pcount=8)
        gamq = ar.alloc([QL // 128], F32)
        gamkv = ar.alloc([KVL // 128], F32)
        ar.mark()
        c_norm = ar.alloc([KVL // 128, S], BF16)
        k_pe = ar.alloc([S], BF16, pcount=DR)

        gp.dma("cst_g", id_bf, id_in)
        gp.dma("cst_g", ones_bf, ones_in)
        tok_consts_gp = gp.dma("cst_g", sel_bf, sel_in)
        sy.dma("cst_s", id_f32, id_in)
        sy.dma("cst_s", gamq, gamq_in)
        tok_consts_sy = sy.dma("cst_s", gamkv, gamkv_in)

        ar.mark()
        cosk = ar.alloc([2, 512], BF16, pcount=DR)
        sinkm = ar.alloc([2, 512], BF16, pcount=DR)
        wkva_sb = ar.alloc([16, KVL + 2 * DR], BF16)
        x_tm2b = ar.alloc([2, 4, H], F32)
        st6 = ar.alloc([4, 4, 6], F32)
        mv2b = ar.alloc([2, 4, 2], F32)
        vtmp = ar.alloc([4], F32)
        rstd2b = ar.alloc([2, 4], F32)
        ln_bf2b = ar.alloc([2, 4, H], BF16)
        ln_fm2b = ar.alloc([2, 16, 512], BF16)
        sq_bf = ar.alloc([4, 512], BF16)
        row_f = ar.alloc([512], F32, pcount=1)
        row_bf = ar.alloc([512], BF16, pcount=1)
        rb_bf = ar.alloc([512], BF16)
        kpe_f = ar.alloc([512], BF16, pcount=DR)
        kpe_p = ar.alloc([512], BF16, pcount=DR)
        kpe_r = ar.alloc([512], BF16, pcount=DR)

        wkva_r = wkva.rearrange("(dc p) m -> p dc m", p=128)
        for i in range(4):
            tok_p1a_w = gp.dma("wkva", wkva_sb[:, 4 * i:4 * i + 4, :],
                               wkva_r[:, 4 * i:4 * i + 4, :])
        tok_rope_read_prev = None

        ve.wait(tok_consts_sy)
        tok_apply_prev = None
        tok_sq_prev = None
        tok_kpe_prev = None
        tok_rb_prev = None
        tok_pe_read_fm = [None, None]
        tok_ln_prev = [None, None]

        for g in range(NG_B):
            gb_ = g % 2
            x_tm = x_tm2b[:, gb_]
            mv = mv2b[:, gb_]
            rstd = rstd2b[:, gb_]
            ln_bf = ln_bf2b[:, gb_]
            ln_fm = ln_fm2b[:, gb_]
            gc = slice(512 * g, 512 * (g + 1))
            xg = xb[512 * g:512 * (g + 1), :].rearrange("(t p) d -> p t d", p=128)
            sy.wait(tok_ln_prev[gb_])
            tok_x = None
            for t in range(4):
                tok_x = sy.dma(f"xa{gb_}", x_tm[:, t, :], xg[:, t, :])
            ve.wait(tok_x)
            tok_ln = emit_ln_tm(fw, x_tm, ln_bf, st6, mv, vtmp, rstd, 4, H, EPS_LN)
            tok_ln_prev[gb_] = tok_ln
            if P1A_CUT < 1:
                continue

            pe.wait(tok_consts_gp)
            tok_fm, _tok_tr = emit_transpose_tm2fm(
                fw, ln_bf, ln_fm, 4, H, ps_trs, id_bf, tok_ln,
                tok_dst_free=tok_pe_read_fm[gb_])

            if P1A_CUT < 2:
                continue
            pe.wait(tok_fm)
            pe.wait(tok_p1a_w)
            pe.wait(tok_apply_prev)
            pe.wait(tok_sq_prev)
            pe.wait(tok_kpe_prev)
            tok_ckv_m = []
            for m in range(5):
                cols = slice(128 * m, 128 * (m + 1))
                dst = ps[m][:, 0:512]
                for dc in range(16):
                    pe.op(lambda tt, dc=dc, cols=cols, dst=dst, ln_fm=ln_fm:
                          tt.matmul(
                              dst, wkva_sb[:, dc, cols], ln_fm[:, dc, :],
                              start=(dc == 0), stop=(dc == 15)), inc=(dc == 15))
                tok_ckv_m.append(pe.tok())
            tok_pe_read_fm[gb_] = pe.tok()

            if P1A_CUT < 3:
                tok_apply_prev = tok_ckv_m[3]
                tok_kpe_prev = tok_ckv_m[4]
                continue
            se.wait(tok_ckv_m[3])
            for m in range(4):
                se.op(lambda s, m=m: s.square(sq_bf[:, m, :], ps[m][:, :]),
                      inc=(m == 3))
            tok_sq = se.tok()
            tok_sq_prev = tok_sq

            pe.wait(tok_sq)
            pe.wait(tok_rb_prev)
            for m in range(4):
                pe.op(lambda tt, m=m: tt.matmul(
                    ps_row[0:1, 0:512], ones_bf[:, 0:1], sq_bf[:, m, :],
                    start=(m == 0), stop=(m == 3)), inc=(m == 3))
            tok_sum = pe.tok()

            if P1A_CUT < 4:
                tok_apply_prev = tok_ckv_m[3]
                tok_kpe_prev = tok_ckv_m[4]
                tok_rb_prev = tok_sum
                continue
            ve.wait(tok_sum)
            ve.op(lambda v: v.tensor_scalar(
                row_f[:, :], ps_row[0:1, 0:512], 1.0 / KVL, EPS_RMS,
                ALU.mult, ALU.add))
            se.wait(ve.tok())
            se.op(lambda s: s.sqrt(row_f[:, :], row_f[:, :]))
            ve.wait(se.tok())
            ve.op(lambda v: v.reciprocal(row_f[:, :], row_f[:, :]))
            ve.wait(ve.tok())
            ve.op(lambda v: v.tensor_copy(row_bf[:, :], row_f[:, :]))
            pe.wait(ve.tok())
            pe.op(lambda tt: tt.matmul(ps_row[:, 0:512], ones_bf[0:1, :],
                                       row_bf[:, :], start=True, stop=True))
            ve.wait(pe.tok())
            ve.op(lambda v: v.tensor_copy(rb_bf[:, :], ps_row[:, 0:512]))
            tok_rb_prev = ve.tok()
            ve.wait(tok_rb_prev)
            if P1A_CUT < 5:
                tok_apply_prev = tok_ckv_m[3]
                tok_kpe_prev = tok_ckv_m[4]
                continue
            for m in range(4):
                ve.op(lambda v, m=m, gc=gc: v.scalar_tensor_tensor(
                    c_norm[:, m, gc], ps[m][:, :], gamkv[:, m:m + 1], rb_bf[:, :],
                    ALU.mult, ALU.mult), inc=(m == 3))
            tok_apply_prev = ve.tok()

            if P1A_CUT < 6:
                tok_kpe_prev = tok_ckv_m[4]
                continue
            tb = g % 2
            gp.wait(tok_rope_read_prev)
            gp.dma("ropetab", cosk[:, tb, :], cosk_in[:, gc])
            tok_tab = gp.dma("ropetab", sinkm[:, tb, :], sinkm_in[:, gc])
            ve.wait(tok_ckv_m[4])
            ve.op(lambda v: v.tensor_copy(kpe_f[:, :], ps[4][0:DR, 0:512]),
                  inc=False)
            ve.op(lambda v: v.tensor_copy(kpe_p[:, :], ps[4][DR:2 * DR, 0:512]))
            tok_kpe_prev = ve.tok()
            ve.wait(tok_kpe_prev)
            ve.wait(tok_tab)
            ve.op(lambda v, tb=tb: v.tensor_mul(
                kpe_r[:, :], kpe_p[:, :], sinkm[:, tb, :]), inc=False)
            ve.op(lambda v, tb=tb: v.tensor_mul(
                kpe_f[:, :], kpe_f[:, :], cosk[:, tb, :]))
            ve.wait(ve.tok())
            ve.op(lambda v, gc=gc: v.tensor_add(k_pe[:, gc], kpe_r[:, :],
                                                kpe_f[:, :]))
            tok_rope_read_prev = ve.tok()

        tok_p1a_ve = ve.tok()
        tok_p1a_pe = pe.tok()
        tok_p1a_se = se.tok()
        ar.reset()

        if "c_norm" in taps:
            sy.wait(tok_p1a_ve)
            for m in range(4):
                sy.dma("taps", taps["c_norm"].rearrange("m p s -> p m s")[:, m, :],
                       c_norm[:, m, :])
        if "k_pe" in taps:
            sy.wait(tok_p1a_ve)
            sy.dma("taps", taps["k_pe"], k_pe[:, :])

        qn = ar.alloc([QL // 128, TQ], BF16)
        tok_p1b_ve, tok_p1b_pe, tok_p1b_se = tok_p1a_ve, tok_p1a_pe, tok_p1a_se
        if SKIP_P1B:
            import types
            p1b_range = range(0)
        else:
            p1b_range = range(NG_Q)
        ar.mark()
        x_tm2 = ar.alloc([4, H], F32)
        st6b = ar.alloc([4, 4, 6], F32)
        mvb = ar.alloc([4, 2], F32)
        vtmpb = ar.alloc([4], F32)
        rstdb = ar.alloc([4], F32)
        ln_bf2 = ar.alloc([4, H], BF16)
        ln_fm2 = ar.alloc([16, 512], BF16)
        qa_bf = ar.alloc([12, 512], BF16)
        sq12 = ar.alloc([12, 512], BF16)
        row_f2 = ar.alloc([512], F32, pcount=1)
        row_bf2 = ar.alloc([512], BF16, pcount=1)
        rb_bf2 = ar.alloc([512], BF16)
        wqa_sb = ar.alloc([2, 16, 128], BF16)

        wqa_r = wqa.rearrange("(dc p) m -> p dc m", p=128)

        sy.wait(tok_p1a_ve)
        sy.wait(tok_p1a_pe)
        sy.wait(tok_p1a_se)
        gp.wait(tok_p1a_ve)
        gp.wait(tok_p1a_pe)
        gp.wait(tok_p1a_se)

        tok_pe_read_fm2 = None
        tok_ln_prev2 = None
        buf_user = [None, None]
        bank_guard = {b_: (tok_p1a_ve, tok_p1a_se) for b_ in range(4)}
        tok_rb2_prev = tok_p1a_ve

        for g in p1b_range:
            gc = slice(512 * g, 512 * (g + 1))
            xg = xq[512 * g:512 * (g + 1), :].rearrange("(t p) d -> p t d", p=128)
            sy.wait(tok_ln_prev2)
            tok_x = None
            for t in range(4):
                tok_x = sy.dma("xq_tm", x_tm2[:, t, :], xg[:, t, :])
            ve.wait(tok_x)
            tok_ln = emit_ln_tm(fw, x_tm2, ln_bf2, st6b, mvb, vtmpb, rstdb,
                                4, H, EPS_LN)
            tok_ln_prev2 = tok_ln
            if P1B_CUT < 1:
                continue
            tok_fm2, _ = emit_transpose_tm2fm(
                fw, ln_bf2, ln_fm2, 4, H, ps_trs, id_bf, tok_ln,
                tok_dst_free=tok_pe_read_fm2)
            if P1B_CUT < 2:
                tok_pe_read_fm2 = pe.tok()
                continue

            pe.wait(tok_fm2)
            tok_sq_m = {}
            tok_stage_m = {}
            for m in range(12):
                bank = ps[m % 4]
                buf = m % 2
                if P1B_SUB >= 1:
                    gp.wait(buf_user[buf])
                    for i in range(4):
                        tok_w = gp.dma(f"wqa{buf}",
                                       wqa_sb[:, buf, 4 * i:4 * i + 4, :],
                                       wqa_r[:, 4 * i:4 * i + 4, 128 * m:128 * (m + 1)])
                    pe.wait(tok_w)
                if m % 4 in bank_guard:
                    t1, t2 = bank_guard[m % 4]
                    pe.wait(t1)
                    pe.wait(t2)
                for dc in range(16):
                    pe.op(lambda tt, dc=dc, m=m, bank=bank, buf=buf: tt.matmul(
                        bank[:, :], wqa_sb[:, buf, dc, :], ln_fm2[:, dc, :],
                        start=(dc == 0), stop=(dc == 15)), inc=(dc == 15))
                tok_mm = pe.tok()
                buf_user[buf] = tok_mm
                if P1B_SUB >= 2:
                    se.wait(tok_mm)
                    tok_sq_m[m] = se.op(lambda s, m=m, bank=bank: s.square(
                        sq12[:, m, :], bank[:, :]))
                else:
                    tok_sq_m[m] = tok_mm
                if P1B_SUB >= 3:
                    ve.wait(tok_sq_m[m])
                    tok_stage_m[m] = ve.op(lambda v, m=m, bank=bank: v.tensor_copy(
                        qa_bf[:, m, :], bank[:, :]))
                else:
                    tok_stage_m[m] = tok_mm
                bank_guard[m % 4] = (tok_sq_m[m], tok_stage_m[m])
            tok_pe_read_fm2 = pe.tok()

            if P1B_CUT < 3:
                continue
            for m in range(12):
                pe.wait(tok_sq_m[m])
            pe.wait(tok_rb2_prev)
            for m in range(12):
                pe.op(lambda tt, m=m: tt.matmul(
                    ps_row[0:1, 0:512], ones_bf[:, 0:1], sq12[:, m, :],
                    start=(m == 0), stop=(m == 11)), inc=(m == 11))
            tok_sum = pe.tok()
            ve.wait(tok_sum)
            ve.op(lambda v: v.tensor_scalar(
                row_f2[:, :], ps_row[0:1, 0:512], 1.0 / QL, EPS_RMS,
                ALU.mult, ALU.add))
            se.wait(ve.tok())
            se.op(lambda s: s.sqrt(row_f2[:, :], row_f2[:, :]))
            ve.wait(se.tok())
            ve.op(lambda v: v.reciprocal(row_f2[:, :], row_f2[:, :]))
            ve.wait(ve.tok())
            ve.op(lambda v: v.tensor_copy(row_bf2[:, :], row_f2[:, :]))
            pe.wait(ve.tok())
            pe.op(lambda tt: tt.matmul(ps_row[:, 0:512], ones_bf[0:1, :],
                                       row_bf2[:, :], start=True, stop=True))
            ve.wait(pe.tok())
            ve.op(lambda v: v.tensor_copy(rb_bf2[:, :], ps_row[:, 0:512]))
            tok_rb2_prev = ve.tok()
            ve.wait(tok_rb2_prev)
            if P1B_CUT < 4:
                continue
            for m in range(12):
                ve.op(lambda v, m=m, gc=gc: v.scalar_tensor_tensor(
                    qn[:, m, gc], qa_bf[:, m, :], gamq[:, m:m + 1], rb_bf2[:, :],
                    ALU.mult, ALU.mult), inc=(m == 11))
            tok_qa_read = ve.tok()
            for b_ in range(4):
                bank_guard[b_] = (bank_guard[b_][0], tok_qa_read)

        if not SKIP_P1B:
            tok_p1b_ve = ve.tok()
            tok_p1b_pe = pe.tok()
            tok_p1b_se = se.tok()
        ar.reset()

        if "qn" in taps:
            sy.wait(tok_p1b_ve)
            for m in range(12):
                sy.dma("taps", taps["qn"].rearrange("m p s -> p m s")[:, m, :],
                       qn[:, m, :])

        if PHASES >= 2:
            p2 = build_attention(nc, fw, ar, locals())
        ar.reset()

        if PHASES >= 3:
            p3 = build_p3(nc, fw, ar, locals())
        if PHASES >= 4:
            build_p4(nc, fw, ar, locals())

        if PHASES < 3:
            ar.mark()
            zz = ar.alloc([4, H], F32)
            sy.wait(tok_p1b_ve)
            sy.wait(tok_p1b_pe)
            sy.wait(tok_p1b_se)
            tok_st = None
            for g in range(NG_Q):
                xg = xq[512 * g:512 * (g + 1), :].rearrange("(t p) d -> p t d", p=128)
                og = out[512 * g:512 * (g + 1), :].rearrange("(t p) d -> p t d", p=128)
                sy.wait(tok_st)
                tok_x = None
                for t in range(4):
                    tok_x = sy.dma("stub_x", zz[:, t, :], xg[:, t, :])
                sy.wait(tok_x)
                for t in range(4):
                    tok_st = sy.dma("out", og[:, t, :], zz[:, t, :])
            ar.reset()

        for qq in ("pe", "ve", "se", "gp"):
            sy.wait(fw.q[qq].tok())
        for stn, st_ in fw.streams.items():
            sy.wait(("s", stn, 16 * st_.n))

        with nc.Block() as blk:
            fw.finalize(blk)
    return nc


def build_p4(nc, fw, ar, EV):
    pe, ve, se, gp, sy = fw.pe, fw.ve, fw.se, fw.gp, fw.sy
    ps = fw.ps
    wrt = EV["wrt"]; wsg = EV["wsg"]; wsu = EV["wsu"]; wsd = EV["wsd"]
    weg = EV["weg"]; weu = EV["weu"]; wed = EV["wed"]
    ln2_scr = EV["ln2_scr"]; x2_scr = EV["x2_scr"]; out = EV["out"]
    taps = EV["taps"]; id_bf = EV["id_bf"]; id_f32 = EV["id_f32"]
    sel_bf = EV["sel_bf"]

    ar.mark()
    ln2_sb = ar.alloc([16, 1024], BF16)
    h_e = ar.alloc([22, 1024], BF16)
    ffn_acc = ar.alloc([16, 1024], F32)
    wg_sb = ar.alloc([2, 16, 128], BF16)
    wu_sb = ar.alloc([2, 16, 128], BF16)
    wd_sb = ar.alloc([2, 22, 128], BF16)
    sg_bf = ar.alloc([2, 512], BF16)
    wrt_sb = ar.alloc([16, 8], BF16)
    lg_fm = ar.alloc([1024], F32, pcount=8)
    lg_tm = ar.alloc([8, 8], F32)
    pexp = ar.alloc([8, 8], F32)
    prow = ar.alloc([8, 8], F32)
    rmax8 = ar.alloc([8], F32)
    rsum8 = ar.alloc([8], F32)
    max1 = ar.alloc([8], F32)
    max2 = ar.alloc([8], F32)
    gates_tm = ar.alloc([8, 8], BF16)
    gates_fm = ar.alloc([1024], BF16, pcount=8)
    gb_bf = ar.alloc([1024], BF16)
    ffn_tm = ar.alloc([2, H], F32)
    x2_ld = ar.alloc([H], F32)
    st6 = ar.alloc([1, 4, 6], F32)
    mv = ar.alloc([1, 2], F32)
    vtmp = ar.alloc([1], F32)
    rstd = ar.alloc([1], F32)

    ps_bf = [ps[i].ap().bitcast(BF16) for i in range(8)]

    tok_p3 = [fw.q[n].tok() for n in ("pe", "ve", "se", "gp")]
    str_p3 = [("s", n, 16 * fw.streams[n].n)
              for n in ("ln2_scr", "x2_scr", "o_scr", "taps")
              if n in fw.streams]
    for q_ in (sy, gp, pe, ve, se):
        for t_ in tok_p3 + str_p3:
            q_.wait(t_)

    gp.dma("wrt", wrt_sb, wrt.rearrange("(dc p) e -> p dc e", p=128))

    wsg_r = wsg.rearrange("(dc p) m -> p dc m", p=128)
    wsu_r = wsu.rearrange("(dc p) m -> p dc m", p=128)
    wsd_r = wsd.rearrange("(fc p) m -> p fc m", p=128)
    weg_r = weg.rearrange("e (dc p) m -> e p dc m", p=128)
    weu_r = weu.rearrange("e (dc p) m -> e p dc m", p=128)
    wed_r = wed.rearrange("e (fc p) m -> e p fc m", p=128)

    br = {i: None for i in range(8)}
    bw = {i: None for i in range(8)}
    tok_guw_used = [None, None]
    tok_wd_used = [None, None]
    tok_ln2_rd = None
    tok_acc_rd = None
    tok_out_store = None
    tok_x2_rd = None
    tok_h_rd = None
    tok_sgrd = [None, None]

    ln2_r = ln2_scr.rearrange("c p t -> p c t")

    for qg2 in range(2):
        qc = slice(1024 * qg2, 1024 * (qg2 + 1))
        sy.wait(tok_ln2_rd)
        tok_ln2 = None
        for i in range(4):
            tok_ln2 = sy.dma("ln2_sb", ln2_sb[:, 4 * i:4 * i + 4, :],
                             ln2_r[:, 4 * i:4 * i + 4, qc])

        pe.wait(tok_ln2)
        pe.wait(("s", "wrt", 16 * fw.streams["wrt"].n))
        for half in range(2):
            bank = half
            pe.wait(br[bank])
            for dc in range(16):
                pe.op(lambda tt, half=half, dc=dc, bank=bank: tt.matmul(
                    ps[bank][0:8, :], wrt_sb[:, dc, :],
                    ln2_sb[:, dc, 512 * half:512 * (half + 1)],
                    start=(dc == 0), stop=(dc == 15)), inc=(dc == 15))
            bw[bank] = pe.tok()
            ve.wait(bw[bank])
            ve.op(lambda v, half=half, bank=bank: v.tensor_copy(
                lg_fm[:, 512 * half:512 * (half + 1)], ps[bank][0:8, :]))
            br[bank] = ve.tok()
        tok_lgfm = ve.tok()
        pe.wait(tok_lgfm)
        pe.wait(br[2])
        for t8 in range(8):
            pe.op(lambda tt, t8=t8: tt.transpose(
                ps[2][:, 8 * t8:8 * (t8 + 1)],
                lg_fm[:, 128 * t8:128 * (t8 + 1)], id_f32[0:8, 0:8]),
                inc=(t8 == 7))
        bw[2] = pe.tok()
        ve.wait(bw[2])
        ve.op(lambda v: v.tensor_copy(
            lg_tm[:, :, :], ps[2][:, 0:64].rearrange("p (a b) -> p a b", b=8)))
        br[2] = ve.tok()
        ve.wait(ve.tok())
        ve.op(lambda v: v.reduce_max(rmax8[:, :], lg_tm[:, :, :], axis=AX.X))
        ve.wait(ve.tok())
        ve.op(lambda v: v.tensor_tensor(
            pexp[:, :, :], lg_tm[:, :, :],
            rmax8[:, :].unsqueeze(2).broadcast_to([128, 8, 8]), ALU.subtract))
        tok_sub = ve.tok()
        se.wait(tok_sub)
        se.op(lambda s: s.activation(pexp[:, :, :], pexp[:, :, :], AF.Exp))
        tok_pexp = se.tok()
        ve.wait(tok_pexp)
        ve.op(lambda v: v.reduce_sum(rsum8[:, :], pexp[:, :, :], axis=AX.X),
              inc=False)
        ve.op(lambda v: v.reduce_max(max1[:, :], pexp[:, :, :], axis=AX.X))
        ve.wait(ve.tok())
        ve.op(lambda v: v.tensor_tensor(
            prow[:, :, :], pexp[:, :, :],
            max1[:, :].unsqueeze(2).broadcast_to([128, 8, 8]), ALU.is_ge))
        ve.wait(ve.tok())
        ve.op(lambda v: v.scalar_tensor_tensor(
            prow[:, :, :], prow[:, :, :], -1e30, pexp[:, :, :],
            ALU.mult, ALU.add))
        ve.wait(ve.tok())
        ve.op(lambda v: v.reduce_max(max2[:, :], prow[:, :, :], axis=AX.X),
              inc=False)
        ve.op(lambda v: v.reciprocal(rsum8[:, :], rsum8[:, :]))
        ve.wait(ve.tok())
        ve.op(lambda v: v.tensor_tensor(
            prow[:, :, :], pexp[:, :, :],
            max2[:, :].unsqueeze(2).broadcast_to([128, 8, 8]), ALU.is_ge))
        ve.wait(ve.tok())
        ve.op(lambda v: v.tensor_mul(prow[:, :, :], prow[:, :, :],
                                     pexp[:, :, :]))
        ve.wait(ve.tok())
        ve.op(lambda v: v.tensor_tensor(
            gates_tm[:, :, :], prow[:, :, :],
            rsum8[:, :].unsqueeze(2).broadcast_to([128, 8, 8]), ALU.mult))
        tok_gates_tm = ve.tok()
        pe.wait(tok_gates_tm)
        pe.wait(br[3])
        for t8 in range(8):
            pe.op(lambda tt, t8=t8: tt.transpose(
                ps_bf[3][0:8, 128 * t8:128 * (t8 + 1)],
                gates_tm[:, t8, :], id_bf), inc=(t8 == 7))
        bw[3] = pe.tok()
        ve.wait(bw[3])
        ve.op(lambda v: v.tensor_copy(gates_fm[:, :], ps_bf[3][0:8, 0:1024]))
        br[3] = ve.tok()
        tok_gates_fm = ve.tok()
        if "gates" in taps:
            sy.wait(tok_gates_fm)
            sy.dma("taps", taps["gates"][:, qc], gates_fm[:, :])

        for pi in range(9):
            nF = 22 if pi == 0 else 11
            if pi == 0:
                g_r, u_r, d_r = wsg_r, wsu_r, wsd_r
            else:
                g_r = weg_r[pi - 1]
                u_r = weu_r[pi - 1]
                d_r = wed_r[pi - 1]
            if pi >= 1:
                e = pi - 1
                pe.wait(tok_gates_fm)
                for half in range(2):
                    pe.wait(br[4])
                    pe.op(lambda tt, e=e, half=half: tt.matmul(
                        ps[4][:, :], sel_bf[0:8, 128 * e:128 * (e + 1)],
                        gates_fm[0:8, 512 * half:512 * (half + 1)],
                        start=True, stop=True))
                    bw[4] = pe.tok()
                    ve.wait(bw[4])
                    ve.op(lambda v, half=half: v.tensor_copy(
                        gb_bf[:, 512 * half:512 * (half + 1)], ps[4][:, :]))
                    br[4] = ve.tok()
            for f in range(nF):
                wb = f % 2
                gp.wait(tok_guw_used[wb])
                tok_wg = None
                for i in range(4):
                    tok_wg = gp.dma(f"wg{wb}", wg_sb[:, wb, 4 * i:4 * i + 4, :],
                                    g_r[:, 4 * i:4 * i + 4, 128 * f:128 * (f + 1)])
                    tok_wg = gp.dma(f"wg{wb}", wu_sb[:, wb, 4 * i:4 * i + 4, :],
                                    u_r[:, 4 * i:4 * i + 4, 128 * f:128 * (f + 1)])
                pe.wait(tok_wg)
                for half in range(2):
                    bg = (f % 2) * 4 + half * 2
                    bu = bg + 1
                    pe.wait(br[bg])
                    pe.wait(br[bu])
                    for dc in range(16):
                        pe.op(lambda tt, dc=dc, f=f, half=half, bg=bg, wb=wb:
                              tt.matmul(
                                  ps[bg][:, :], wg_sb[:, wb, dc, :],
                                  ln2_sb[:, dc, 512 * half:512 * (half + 1)],
                                  start=(dc == 0), stop=(dc == 15)),
                              inc=(dc == 15))
                        pe.op(lambda tt, dc=dc, f=f, half=half, bu=bu, wb=wb:
                              tt.matmul(
                                  ps[bu][:, :], wu_sb[:, wb, dc, :],
                                  ln2_sb[:, dc, 512 * half:512 * (half + 1)],
                                  start=(dc == 0), stop=(dc == 15)),
                              inc=(dc == 15))
                    bw[bg] = bw[bu] = pe.tok()
                    se.wait(bw[bg])
                    se.wait(tok_sgrd[half])
                    se.op(lambda s, f=f, half=half, bg=bg: s.activation(
                        sg_bf[:, half, :], ps[bg][:, :], AF.Silu))
                    tok_sg = se.tok()
                    br[bg] = tok_sg
                    ve.wait(tok_sg)
                    ve.wait(tok_h_rd)
                    ve.op(lambda v, f=f, half=half, bu=bu: v.tensor_tensor(
                        h_e[:, f, 512 * half:512 * (half + 1)], ps[bu][:, :],
                        sg_bf[:, half, :], ALU.mult))
                    br[bu] = ve.tok()
                    tok_sgrd[half] = br[bu]
                    if pi >= 1:
                        ve.wait(ve.tok())
                        ve.op(lambda v, f=f, half=half: v.tensor_mul(
                            h_e[:, f, 512 * half:512 * (half + 1)],
                            h_e[:, f, 512 * half:512 * (half + 1)],
                            gb_bf[:, 512 * half:512 * (half + 1)]))
                tok_guw_used[wb] = pe.tok()
            tok_h_done = ve.tok()
            pe.wait(tok_h_done)
            for mc in range(16):
                wb = mc % 2
                gp.wait(tok_wd_used[wb])
                tok_wd = None
                f0 = 0
                while f0 < nF:
                    f1 = min(f0 + 4, nF)
                    tok_wd = gp.dma(f"wd{wb}", wd_sb[:, wb, f0:f1, :],
                                    d_r[:, f0:f1, 128 * mc:128 * (mc + 1)])
                    f0 = f1
                pe.wait(tok_wd)
                for half in range(2):
                    bank = (mc % 2) * 2 + half
                    pe.wait(br[bank])
                    for f in range(nF):
                        pe.op(lambda tt, f=f, mc=mc, half=half, bank=bank,
                              wb=wb, nF=nF: tt.matmul(
                                  ps[bank][:, :], wd_sb[:, wb, f, :],
                                  h_e[:, f, 512 * half:512 * (half + 1)],
                                  start=(f == 0), stop=(f == nF - 1)),
                              inc=(f == nF - 1))
                    bw[bank] = pe.tok()
                    ve.wait(bw[bank])
                    if pi == 0:
                        ve.wait(tok_acc_rd)
                        ve.op(lambda v, mc=mc, half=half, bank=bank:
                              v.tensor_copy(
                                  ffn_acc[:, mc, 512 * half:512 * (half + 1)],
                                  ps[bank][:, :]))
                    else:
                        ve.op(lambda v, mc=mc, half=half, bank=bank:
                              v.tensor_add(
                                  ffn_acc[:, mc, 512 * half:512 * (half + 1)],
                                  ps[bank][:, :],
                                  ffn_acc[:, mc, 512 * half:512 * (half + 1)]))
                    br[bank] = ve.tok()
                tok_wd_used[wb] = pe.tok()
            tok_h_rd = pe.tok()
        tok_ln2_rd = pe.tok()
        tok_acc_done = ve.tok()

        for t8 in range(8):
            tb = t8 % 2
            gtile = 8 * qg2 + t8
            for mg in range(4):
                bank = 6 + mg % 2
                pe.wait(tok_acc_done)
                pe.wait(br[bank])
                for mi in range(4):
                    m = 4 * mg + mi
                    pe.op(lambda tt, m=m, mi=mi, t8=t8, bank=bank: tt.transpose(
                        ps[bank][:, 128 * mi:128 * (mi + 1)],
                        ffn_acc[:, m, 128 * t8:128 * (t8 + 1)], id_f32),
                        inc=(mi == 3))
                bw[bank] = pe.tok()
                ve.wait(bw[bank])
                ve.op(lambda v, t8=t8, mg=mg, tb=tb, bank=bank: v.tensor_copy(
                    ffn_tm[:, tb, 512 * mg:512 * (mg + 1)], ps[bank][:, :]))
                br[bank] = ve.tok()
            tok_strip = ve.tok()
            ve.wait(tok_strip)
            for k in range(4):
                ve.op(lambda v, k=k, tb=tb: v.bn_stats(
                    st6[:, 0, k, :], ffn_tm[:, tb, 512 * k:512 * (k + 1)]),
                    inc=(k == 3))
            ve.wait(ve.tok())
            ve.op(lambda v: v.bn_aggr(mv[:, 0, :], st6[:, 0, :, :].rearrange(
                "p a b -> p (a b)")))
            ve.wait(ve.tok())
            ve.op(lambda v: v.tensor_scalar_add(vtmp[:, :], mv[:, :, 1],
                                                EPS_LN))
            se.wait(ve.tok())
            se.op(lambda s: s.sqrt(vtmp[:, :], vtmp[:, :]))
            ve.wait(se.tok())
            ve.op(lambda v: v.reciprocal(rstd[:, :], vtmp[:, :]))
            ve.wait(ve.tok())
            sy.wait(tok_x2_rd)
            sy.wait(tok_out_store)
            tok_x2l = sy.dma("x2_ld", x2_ld[:, :], x2_scr[gtile])
            ve.wait(tok_x2l)
            ve.op(lambda v, tb=tb: v.tensor_scalar(
                ffn_tm[:, tb, :], ffn_tm[:, tb, :], mv[:, 0, 0:1],
                rstd[:, 0:1], ALU.subtract, ALU.mult))
            ve.wait(ve.tok())
            ve.op(lambda v, tb=tb: v.tensor_add(x2_ld[:, :], x2_ld[:, :],
                                                ffn_tm[:, tb, :]))
            tok_out = ve.tok()
            tok_x2_rd = tok_out
            sy.wait(tok_out)
            tok_out_store = sy.dma(
                "out", out[128 * gtile:128 * (gtile + 1), :], x2_ld[:, :])
        tok_acc_rd = pe.tok()

    ar.reset()
    return dict(tok_out_store=tok_out_store)


def build_p3(nc, fw, ar, EV):
    pe, ve, se, gp, sy = fw.pe, fw.ve, fw.se, fw.gp, fw.sy
    ps = fw.ps
    wo = EV["wo"]; xq = EV["xq"]; o_scr = EV["o_scr"]
    x2_scr = EV["x2_scr"]; ln2_scr = EV["ln2_scr"]
    taps = EV["taps"]; id_bf = EV["id_bf"]; id_f32 = EV["id_f32"]
    p2 = EV["p2"]

    ar.mark()
    o_sb = ar.alloc([16, 512], BF16)
    wo_sb = ar.alloc([2, 16, 128], BF16)
    attn_fm = ar.alloc([16, 512], F32)
    attn_tm = ar.alloc([4, H], F32)
    x_tm = ar.alloc([4, H], F32)
    tmp = ar.alloc([H], F32)
    ln2_bf = ar.alloc([4, H], BF16)
    ln2_fm = ar.alloc([16, 512], BF16)
    st6 = ar.alloc([4, 4, 6], F32)
    mv = ar.alloc([4, 2], F32)
    vtmp = ar.alloc([4], F32)
    rstd = ar.alloc([4], F32)

    ps_trs_bf = [ps[2].ap().bitcast(BF16), ps[3].ap().bitcast(BF16)]

    tok_p2 = [fw.q[n].tok() for n in ("pe", "ve", "se", "gp")]
    for q_ in (sy, gp, pe, ve, se):
        for t_ in tok_p2:
            q_.wait(t_)
    tok_o_final = ("s", "o_scr", 16 * fw.streams["o_scr"].n)
    sy.wait(tok_o_final)

    wo_r = wo.rearrange("(hc p) m -> p hc m", p=128)
    o_r = o_scr.rearrange("h p q -> p h q")

    tok_ln2_store = None
    tok_x2_store = None
    tok_pe_attn = {}
    tok_wo_used = [None, None]
    br = {i: None for i in range(8)}
    bw = {i: None for i in range(8)}

    for qg in range(NG_Q):
        qc = slice(512 * qg, 512 * (qg + 1))
        sy.wait(tok_pe_attn.get(qg - 1))
        tok_o = None
        for i in range(4):
            tok_o = sy.dma("o_sb", o_sb[:, 4 * i:4 * i + 4, :],
                           o_r[:, 4 * i:4 * i + 4, qc])
        for mc in range(16):
            wb = mc % 2
            gp.wait(tok_wo_used[wb])
            tok_w = None
            for i in range(4):
                tok_w = gp.dma(f"wo{wb}", wo_sb[:, wb, 4 * i:4 * i + 4, :],
                               wo_r[:, 4 * i:4 * i + 4, 128 * mc:128 * (mc + 1)])
            bank = mc % 2
            pe.wait(tok_w)
            pe.wait(tok_o)
            pe.wait(br[bank])
            for hc in range(16):
                pe.op(lambda tt, hc=hc, mc=mc, bank=bank, wb=wb: tt.matmul(
                    ps[bank][:, :], wo_sb[:, wb, hc, :], o_sb[:, hc, :],
                    start=(hc == 0), stop=(hc == 15)), inc=(hc == 15))
            bw[bank] = pe.tok()
            tok_wo_used[wb] = bw[bank]
            ve.wait(bw[bank])
            ve.op(lambda v, mc=mc, bank=bank: v.tensor_copy(
                attn_fm[:, mc, :], ps[bank][:, :]))
            br[bank] = ve.tok()
        tok_attn_fm = ve.tok()
        tok_pe_attn[qg] = pe.tok()

        pe.wait(tok_attn_fm)
        for mc in range(16):
            bank = 2 + mc % 2
            pe.wait(br[bank])
            for t4 in range(4):
                pe.op(lambda tt, mc=mc, t4=t4, bank=bank: tt.transpose(
                    ps[bank][:, 128 * t4:128 * (t4 + 1)],
                    attn_fm[:, mc, 128 * t4:128 * (t4 + 1)], id_f32),
                    inc=(t4 == 3))
            bw[bank] = pe.tok()
            ve.wait(bw[bank])
            ve.op(lambda v, mc=mc, bank=bank: v.tensor_copy(
                attn_tm[:, :, 128 * mc:128 * (mc + 1)],
                ps[bank][:, :].rearrange("p (a b) -> p a b", b=128)))
            br[bank] = ve.tok()
        tok_attn_tm = ve.tok()

        sy.wait(tok_x2_store)
        tok_x = None
        for t in range(4):
            tok_x = sy.dma("x3_tm", x_tm[:, t, :],
                           xq[512 * qg + 128 * t:512 * qg + 128 * (t + 1), :])
        ve.wait(tok_attn_tm)
        tok_stats = emit_ln_tm(fw, attn_tm, None, st6, mv, vtmp, rstd, 4, H,
                               EPS_LN)
        ve.wait(tok_stats)
        ve.wait(tok_x)
        for t in range(4):
            ve.op(lambda v, t=t: v.tensor_scalar(
                tmp[:, :], attn_tm[:, t, :], mv[:, t, 0:1], rstd[:, t:t + 1],
                ALU.subtract, ALU.mult))
            ve.wait(ve.tok())
            ve.op(lambda v, t=t: v.tensor_add(x_tm[:, t, :], x_tm[:, t, :],
                                              tmp[:, :]))
            ve.wait(ve.tok())
        tok_x2 = ve.tok()
        sy.wait(tok_x2)
        for t in range(4):
            tok_x2_store = sy.dma("x2_scr", x2_scr[4 * qg + t], x_tm[:, t, :])
        if "x2" in taps:
            for t in range(4):
                tok_x2_store = sy.dma("x2_scr", taps["x2"][4 * qg + t],
                                      x_tm[:, t, :])
        tok_ln2 = emit_ln_tm(fw, x_tm, ln2_bf, st6, mv, vtmp, rstd, 4, H,
                             EPS_LN)
        tok_fm, tok_tr_pe = emit_transpose_tm2fm(
            fw, ln2_bf, ln2_fm, 4, H, ps_trs_bf, id_bf, tok_ln2,
            tok_dst_free=tok_ln2_store)
        sy.wait(tok_fm)
        tok_st = None
        for i in range(4):
            tok_st = sy.dma("ln2_scr", ln2_scr[:, :, qc].rearrange(
                "c p t -> p c t")[:, 4 * i:4 * i + 4, :],
                ln2_fm[:, 4 * i:4 * i + 4, :])
            if "ln2" in taps:
                tok_st = sy.dma("ln2_scr", taps["ln2"][:, :, qc].rearrange(
                    "c p t -> p c t")[:, 4 * i:4 * i + 4, :],
                    ln2_fm[:, 4 * i:4 * i + 4, :])
        tok_ln2_store = ("c", "ve", 0)
        tok_ln2_store = tok_st

    ar.reset()
    return dict(tok_x2_store=tok_x2_store, tok_ln2_store=tok_ln2_store)


def build_attention(nc, fw, ar, EV):
    pe, ve, se, gp, sy = fw.pe, fw.ve, fw.se, fw.gp, fw.sy
    ps = fw.ps
    qn = EV["qn"]; c_norm = EV["c_norm"]; k_pe = EV["k_pe"]
    wqb = EV["wqb"]; wkvb = EV["wkvb"]
    cosq_in = EV["cosq_in"]; sinqm_in = EV["sinqm_in"]; dmask_in = EV["dmask_in"]
    o_scr = EV["o_scr"]; taps = EV["taps"]; id_bf = EV["id_bf"]
    ones_bf = EV["ones_bf"]; id_f32 = EV["id_f32"]
    tok_p1b_ve = EV["tok_p1b_ve"]; tok_p1b_pe = EV["tok_p1b_pe"]
    tok_p1b_se = EV["tok_p1b_se"]

    ar.mark()
    cosq = ar.alloc([TQ], BF16, pcount=DR)
    sinqm = ar.alloc([TQ], BF16, pcount=DR)
    masks = ar.alloc([2, 512], F32)
    knope = ar.alloc([S], BF16)
    v_tm = ar.alloc([32, 128], BF16)
    qh_n = ar.alloc([TQ], BF16)
    qpe_f = ar.alloc([TQ], F32, pcount=DR)
    qpe_p = ar.alloc([TQ], F32, pcount=DR)
    qpe_bf = ar.alloc([TQ], BF16, pcount=DR)
    p_bf2 = ar.alloc([2, S], BF16)
    pT = ar.alloc([32, 128], BF16)
    o_buf = ar.alloc([1, TQ], BF16)
    wqb_sb = ar.alloc([2, 12, 256], BF16)
    wkvb_sb = ar.alloc([2, 4, 256], BF16)
    sq_kpe = ar.alloc([S], BF16, pcount=DR)
    sq_qpe = ar.alloc([TQ], BF16, pcount=DR)
    gblk = ar.alloc([8], F32, pcount=1)
    gmax = ar.alloc([1], F32, pcount=1)
    nrow = qpe_p[0:1, :]
    bias_t = ar.alloc([16], F32)
    rsb = ar.alloc([2, 8], F32)
    rsum = ar.alloc([2, 1], F32)
    rs = ar.alloc([2, 1], F32)
    diag_rs = ar.alloc([2, 128], BF16)

    ps_trs = [ps[6].ap(), ps[7].ap(), ps[3].ap()]

    for q_ in (sy, gp):
        q_.wait(tok_p1b_ve)
        q_.wait(tok_p1b_pe)
        q_.wait(tok_p1b_se)

    gp.dma("qrope", cosq, cosq_in)
    tok_qt = gp.dma("qrope", sinqm, sinqm_in)
    tok_mask = sy.dma("dmask2", masks, dmask_in.rearrange("j p n -> p j n"))

    wqb_r = wqb.rearrange("(qc p) m -> p qc m", p=128)
    wkvb_r = wkvb.rearrange("(cc p) m -> p cc m", p=128)

    bw = {i: None for i in range(8)}
    br = {i: tok_p1b_ve for i in range(8)}
    br_se = {i: tok_p1b_se for i in range(8)}

    def wstream(h):
        b = h % 2
        t = None
        for i in range(3):
            t = gp.dma(f"wqb{b}", wqb_sb[:, b, 4 * i:4 * i + 4, :],
                       wqb_r[:, 4 * i:4 * i + 4, 256 * h:256 * (h + 1)])
        t2 = gp.dma(f"wkvb{b}", wkvb_sb[:, b, :, :],
                    wkvb_r[:, :, 256 * h:256 * (h + 1)])
        return t, t2

    se.wait(tok_p1b_se)
    se.wait(("c", "ve", EV["tok_p1b_ve"][2]) if False else tok_p1b_ve)
    se.op(lambda s: s.square(sq_kpe[:, :], k_pe[:, :]))
    tok_sq_kpe = se.tok()

    tok_w = {0: wstream(0)}
    tok_o_store = {}
    tok_kv_users = tok_p1b_pe
    tok_pT_pe = None
    tok_av_prev = None
    tok_sq_free = None
    tok_bias_guard = None
    tok_pT_buf = [None, None]
    tok_rs_buf = [None, None]
    tok_diag_rd = [None, None]

    for h in range(NH):
        b = h % 2
        if h + 1 < NH:
            if h >= 1:
                gp.wait(tok_w_used[(h + 1) % 2])
            tok_w[h + 1] = wstream(h + 1)
        if h == 0:
            tok_w_used = {}

        pe.wait(tok_w[h][0])
        pe.wait(tok_w[h][1])
        tok_ev_kn = {}
        for kb in range(8):
            bank = 6 + kb % 2
            pe.wait(br[bank])
            for cc in range(4):
                pe.op(lambda tt, kb=kb, cc=cc, bank=bank, b=b: tt.matmul(
                    ps[bank][:, :], wkvb_sb[:, b, cc, 0:128],
                    c_norm[:, cc, 512 * kb:512 * (kb + 1)],
                    start=(cc == 0), stop=(cc == 3)), inc=(cc == 3))
            bw[bank] = pe.tok()
            ve.wait(bw[bank])
            ve.wait(tok_kv_users)
            ve.op(lambda v, kb=kb, bank=bank: v.tensor_copy(
                knope[:, 512 * kb:512 * (kb + 1)], ps[bank][:, :]))
            br[bank] = ve.tok()
        tok_knope_ev = ve.tok()

        sq_scr = p_bf2[:, 0, :]
        se.wait(tok_knope_ev)
        se.wait(tok_sq_free)
        se.wait(tok_pT_buf[0])
        se.wait(tok_pT_buf[1])
        se.wait(tok_sq_kpe)
        se.op(lambda s: s.square(sq_scr[:, :], knope[:, :]))
        tok_sqk = se.tok()

        for grp in range(8):
            bank = 4 + grp % 2
            pe.wait(br[bank])
            for kt in range(4):
                ki = 4 * grp + kt
                for cc in range(4):
                    pe.op(lambda tt, ki=ki, kt=kt, cc=cc, bank=bank, b=b: tt.matmul(
                        ps[bank][:, 128 * kt:128 * (kt + 1)],
                        c_norm[:, cc, 128 * ki:128 * (ki + 1)],
                        wkvb_sb[:, b, cc, 128:256],
                        start=(cc == 0), stop=(cc == 3)),
                        inc=(kt == 3 and cc == 3))
            bw[bank] = pe.tok()
            ve.wait(bw[bank])
            ve.op(lambda v, grp=grp, bank=bank: v.tensor_copy(
                v_tm[:, 4 * grp:4 * (grp + 1), :].rearrange("p a b -> p (a b)"),
                ps[bank][:, :]))
            br[bank] = ve.tok()
        tok_v_ev = ve.tok()

        pe.wait(tok_sqk)
        for blk in range(8):
            bank = 6 + blk % 2
            pe.wait(br[bank])
            pe.wait(br_se[bank])
            pe.op(lambda tt, blk=blk, bank=bank: tt.matmul(
                ps[bank][0:1, 0:512], ones_bf[:, 0:1],
                sq_scr[:, 512 * blk:512 * (blk + 1)],
                start=True, stop=False), inc=False)
            pe.op(lambda tt, blk=blk, bank=bank: tt.matmul(
                ps[bank][0:1, 0:512], ones_bf[0:DR, 0:1],
                sq_kpe[:, 512 * blk:512 * (blk + 1)],
                start=False, stop=True))
            bw[bank] = pe.tok()
            ve.wait(bw[bank])
            ve.op(lambda v, blk=blk, bank=bank: v.reduce_max(
                gblk[:, blk:blk + 1], ps[bank][0:1, 0:512], axis=AX.X))
            br[bank] = ve.tok()
        ve.wait(ve.tok())
        ve.op(lambda v: v.reduce_max(gmax[:, :], gblk[:, :], axis=AX.X))
        se.wait(ve.tok())
        se.wait(se.tok())
        se.op(lambda s: s.sqrt(gmax[:, :], gmax[:, :]))
        tok_G = se.tok()
        tok_sq_k_done = pe.tok()

        tok_qn_ev = {}
        for qg in range(4):
            for part in range(2):
                bank = 4 + (qg * 2 + part) % 2
                pe.wait(br[bank])
                for qc in range(12):
                    pe.op(lambda tt, qg=qg, part=part, qc=qc, bank=bank, b=b: tt.matmul(
                        ps[bank][:, :], wqb_sb[:, b, qc, 128 * part:128 * (part + 1)],
                        qn[:, qc, 512 * qg:512 * (qg + 1)],
                        start=(qc == 0), stop=(qc == 11)), inc=(qc == 11))
                bw[bank] = pe.tok()
                ve.wait(bw[bank])
                if part == 0:
                    ve.op(lambda v, qg=qg, bank=bank: v.tensor_copy(
                        qh_n[:, 512 * qg:512 * (qg + 1)], ps[bank][:, :]))
                else:
                    ve.op(lambda v, qg=qg, bank=bank: v.tensor_copy(
                        qpe_f[:, 512 * qg:512 * (qg + 1)], ps[bank][0:DR, :]),
                        inc=False)
                    ve.op(lambda v, qg=qg, bank=bank: v.tensor_copy(
                        qpe_p[:, 512 * qg:512 * (qg + 1)], ps[bank][DR:2 * DR, :]))
                br[bank] = ve.tok()
        ve.wait(tok_qt)
        ve.wait(ve.tok())
        ve.op(lambda v: v.tensor_mul(qpe_p[:, :], qpe_p[:, :], sinqm[:, :]),
              inc=False)
        ve.op(lambda v: v.tensor_mul(qpe_f[:, :], qpe_f[:, :], cosq[:, :]))
        ve.wait(ve.tok())
        ve.op(lambda v: v.tensor_add(qpe_bf[:, :], qpe_p[:, :], qpe_f[:, :]))
        tok_q_ready = ve.tok()

        se.wait(tok_sq_k_done)
        se.wait(tok_q_ready)
        se.op(lambda s: s.square(sq_scr[:, 0:TQ], qh_n[:, :]), inc=False)
        se.op(lambda s: s.square(sq_qpe[:, :], qpe_bf[:, :]))
        tok_sqq = se.tok()
        pe.wait(tok_sqq)
        for qg in range(4):
            bank = 6 + qg % 2
            pe.wait(br[bank])
            pe.wait(br_se[bank])
            pe.op(lambda tt, qg=qg, bank=bank: tt.matmul(
                ps[bank][0:1, 0:512], ones_bf[:, 0:1],
                sq_scr[:, 512 * qg:512 * (qg + 1)],
                start=True, stop=False), inc=False)
            pe.op(lambda tt, qg=qg, bank=bank: tt.matmul(
                ps[bank][0:1, 0:512], ones_bf[0:DR, 0:1],
                sq_qpe[:, 512 * qg:512 * (qg + 1)],
                start=False, stop=True))
            bw[bank] = pe.tok()
            se.wait(bw[bank])
            se.wait(tok_q_ready)
            se.op(lambda s, qg=qg, bank=bank: s.sqrt(
                nrow[:, 512 * qg:512 * (qg + 1)], ps[bank][0:1, 0:512]))
            br_se[bank] = se.tok()
        tok_sq_free = pe.tok()
        tok_nrow = se.tok()
        ve.wait(tok_nrow)
        ve.wait(tok_G)
        ve.op(lambda v: v.tensor_scalar(
            nrow[:, :], nrow[:, :], gmax[:, :], None, ALU.mult))
        ve.wait(ve.tok())
        ve.op(lambda v: v.tensor_scalar(
            nrow[:, :], nrow[:, :], -SOFTMAX_SCALE, EXP_SHIFT,
            ALU.mult, ALU.add))
        tok_brow = ve.tok()
        pe.wait(tok_brow)
        pe.wait(br[6])
        pe.wait(br_se[6])
        for t16 in range(16):
            pe.op(lambda tt, t16=t16: tt.transpose(
                ps[6][:, t16:t16 + 1], nrow[0:1, 128 * t16:128 * (t16 + 1)],
                id_f32[0:1, 0:1]), inc=(t16 == 15))
        bw[6] = pe.tok()
        ve.wait(bw[6])
        ve.wait(tok_bias_guard)
        ve.op(lambda v: v.tensor_copy(bias_t[:, :], ps[6][:, 0:16]))
        br[6] = ve.tok()
        tok_bias = ve.tok()

        def emit_blocks(j):
            nb = j // 2 + 1
            tb = j % 2
            pe.wait(tok_q_ready)
            pe.wait(tok_knope_ev)
            ve.wait(tok_mask)
            se.wait(tok_bias)
            se.wait(tok_pT_buf[tb])
            se.wait(tok_rs_buf[tb])
            for kb in range(nb):
                bank = scb[j][kb]
                pe.wait(br[bank])
                pe.wait(br_se[bank])
                pe.op(lambda tt, j=j, kb=kb, bank=bank: tt.matmul(
                    ps[bank][:, :], qh_n[:, 128 * j:128 * (j + 1)],
                    knope[:, 512 * kb:512 * (kb + 1)],
                    start=True, stop=False), inc=False)
                pe.op(lambda tt, j=j, kb=kb, bank=bank: tt.matmul(
                    ps[bank][:, :], qpe_bf[:, 128 * j:128 * (j + 1)],
                    k_pe[:, 512 * kb:512 * (kb + 1)],
                    start=False, stop=True))
                bw[bank] = pe.tok()
                if kb == nb - 1:
                    ve.wait(bw[bank])
                    ve.op(lambda v, j=j, bank=bank: v.tensor_tensor(
                        ps[bank][:, :], ps[bank][:, :], masks[:, j % 2, :],
                        ALU.add))
                    br[bank] = ve.tok()
                    se.wait(br[bank])
                else:
                    se.wait(bw[bank])
                se.op(lambda s, j=j, kb=kb, bank=bank, tb=tb: s.activation(
                    p_bf2[:, tb, 512 * kb:512 * (kb + 1)], ps[bank][:, :],
                    AF.Exp, bias=bias_t[:, j:j + 1], scale=SOFTMAX_SCALE,
                    accum_out=rsb[:, tb, kb:kb + 1]))
                br_se[bank] = se.tok()

        def emit_pTAV(j):
            nonlocal tok_pT_pe, tok_av_prev
            nb = j // 2 + 1
            tb_ = j % 2
            pe.wait(tok_norm_t[j])
            grp_ev = {}
            for grp in range(nb):
                bank = (6, 7, 3)[grp % 3]
                tbank = ps_trs[grp % 3]
                pe.wait(br[bank])
                pe.wait(br_se[bank])
                for t4 in range(4):
                    pe.op(lambda tt, grp=grp, t4=t4, tbank=tbank, tb_=tb_:
                          tt.matmul(
                              tbank[:, 128 * t4:128 * (t4 + 1)],
                              p_bf2[:, tb_,
                                    512 * grp + 128 * t4:512 * grp + 128 * (t4 + 1)],
                              diag_rs[:, tb_, :],
                              start=True, stop=True), inc=(t4 == 3))
                bw[bank] = pe.tok()
                eng = ve if grp % 2 == 0 else se
                eng.wait(bw[bank])
                eng.op(lambda v, grp=grp, tbank=tbank: v.tensor_copy(
                    pT[:, 4 * grp:4 * (grp + 1), :].rearrange("p a b -> p (a b)"),
                    tbank[:, 0:512])
                       if grp % 2 == 0 else
                       v.copy(pT[:, 4 * grp:4 * (grp + 1), :].rearrange(
                           "p a b -> p (a b)"), tbank[:, 0:512]))
                if grp % 2 == 0:
                    br[bank] = eng.tok()
                else:
                    br_se[bank] = eng.tok()
                grp_ev[grp] = (eng.tok())
            tok_pT_buf[tb_] = pe.tok()
            tok_diag_rd[tb_] = tok_pT_buf[tb_]
            tok_pT_pe = tok_pT_buf[tb_]
            av_bank = 4 + j % 2
            pe.wait(tok_v_ev)
            pe.wait(br[av_bank])
            for kc in range(4 * nb):
                pe.wait(grp_ev[kc // 4])
                pe.op(lambda tt, kc=kc, av_bank=av_bank, nb=nb: tt.matmul(
                    ps[av_bank][:, 0:128], v_tm[:, kc, :], pT[:, kc, :],
                    start=(kc == 0), stop=(kc == 4 * nb - 1)),
                    inc=(kc == 4 * nb - 1))
            bw[av_bank] = pe.tok()
            tok_av_prev = bw[av_bank]
            ve.wait(bw[av_bank])
            if h >= 1:
                ve.wait(tok_o_store.get(h - 1))
            ve.op(lambda v, j=j, av_bank=av_bank: v.tensor_copy(
                o_buf[:, 0, 128 * j:128 * (j + 1)], ps[av_bank][:, 0:128]))
            br[av_bank] = ve.tok()

        scb = {}
        rb = 0
        for j in range(16):
            nb = j // 2 + 1
            scb[j] = [(rb + i) % 3 for i in range(nb)]
            rb = (rb + nb) % 3

        tok_norm_t = {}
        for j in range(16):
            nb = j // 2 + 1
            tb = j % 2
            emit_blocks(j)
            if j >= 1:
                emit_pTAV(j - 1)
            tok_exp = se.tok()
            ve.wait(tok_exp)
            ve.wait(tok_diag_rd[tb])
            ve.op(lambda v, nb=nb, tb=tb: v.reduce_sum(
                rsum[:, tb, :], rsb[:, tb, 0:nb], axis=AX.X))
            ve.wait(ve.tok())
            ve.op(lambda v, tb=tb: v.reciprocal(rs[:, tb, :], rsum[:, tb, :]))
            tok_rs_buf[tb] = ve.tok()
            ve.wait(tok_rs_buf[tb])
            ve.op(lambda v, tb=tb: v.tensor_mul(
                diag_rs[:, tb, :], id_bf[:, :],
                rs[:, tb, :].broadcast_to([128, 128])))
            tok_norm_t[j] = ve.tok()
        emit_pTAV(15)
        tok_bias_guard = se.tok()

        tok_kv_users = pe.tok()
        tok_w_used[b] = pe.tok()
        tok_o_ev = ve.tok()
        sy.wait(tok_o_ev)
        tok_o_store[h] = sy.dma("o_scr", o_scr[h], o_buf[:, 0, :])
        if "o" in taps:
            tok_o_store[h] = sy.dma("o_scr", taps["o"][h], o_buf[:, 0, :])

    ar.reset()
    return dict(tok_o_store=tok_o_store)


DEBUG_SHAPES = {
    "c_norm": ([KVL // 128, 128, S], BF16),
    "k_pe": ([DR, S], BF16),
    "qn": ([QL // 128, 128, TQ], BF16),
    "o": ([NH, DV, TQ], BF16),
    "x2": ([TQ // 128, 128, H], F32),
    "ln2": ([H // 128, 128, TQ], BF16),
    "gates": ([8, TQ], BF16),
}



def _rope_tables():
    inv = 1.0 / ROPE_THETA ** (np.arange(0, DR, 2, dtype=np.float64) / DR)
    t = np.arange(S, dtype=np.float64)
    fr = np.outer(t, inv)
    emb = np.concatenate([fr, fr], axis=1)
    cos = np.cos(emb).astype(np.float32)
    sin = np.sin(emb).astype(np.float32)
    sinm = sin.copy()
    sinm[:, :DR // 2] = -sinm[:, :DR // 2]
    return np.ascontiguousarray(cos.T), np.ascontiguousarray(sinm.T)


def _own_rows(half):
    rows = []
    for j in range(16):
        g = 2 * j + half
        rows.extend(range(128 * g, 128 * (g + 1)))
    return np.array(rows)


def make_in_maps(inputs):
    cos_t, sinm_t = _rope_tables()
    gamq = np.ascontiguousarray(
        np.asarray(inputs["L0_attn_q_a_ln_w"], np.float32).reshape(QL // 128, 128).T)
    gamkv = np.ascontiguousarray(
        np.asarray(inputs["L0_attn_kv_a_ln_w"], np.float32).reshape(KVL // 128, 128).T)
    ident = np.eye(128, dtype=np.float32)
    ones = np.ones((128, 128), dtype=np.float32)
    sel = np.zeros((8, E * 128), dtype=np.float32)
    for e in range(E):
        sel[e, 128 * e:128 * (e + 1)] = 1.0
    x = np.asarray(inputs["L0_attn_x"], dtype=np.float32)
    wkva0 = np.asarray(inputs["L0_attn_wkv_a"], np.float32)
    wkva_ext = np.concatenate(
        [wkva0, wkva0[:, KVL + 32:KVL + 64], wkva0[:, KVL:KVL + 32]], axis=1)
    wqb0 = np.asarray(inputs["L0_attn_wq_b"], np.float32)
    wqb_h = wqb0.reshape(QL, NH, QH)
    wqb_ext = np.concatenate(
        [wqb_h, wqb_h[:, :, DN + 32:DN + 64], wqb_h[:, :, DN:DN + 32]],
        axis=2).reshape(QL, NH * 256)
    shared = dict(
        wqa=np.asarray(inputs["L0_attn_wq_a"], np.float32),
        wqb=np.ascontiguousarray(wqb_ext),
        wkva=np.ascontiguousarray(wkva_ext),
        wkvb=np.asarray(inputs["L0_attn_wkv_b"], np.float32),
        wo=np.asarray(inputs["L0_attn_wo"], np.float32),
        wrt=np.asarray(inputs["L0_ffn_W_router"], np.float32),
        weg=np.asarray(inputs["L0_ffn_We_gate"], np.float32),
        weu=np.asarray(inputs["L0_ffn_We_up"], np.float32),
        wed=np.asarray(inputs["L0_ffn_We_down"], np.float32),
        wsg=np.asarray(inputs["L0_ffn_Ws_gate"], np.float32),
        wsu=np.asarray(inputs["L0_ffn_Ws_up"], np.float32),
        wsd=np.asarray(inputs["L0_ffn_Ws_down"], np.float32),
        gamq=gamq, gamkv=gamkv, id_in=ident, ones_in=ones, sel_in=sel,
        cosk=cos_t, sinkm=sinm_t,
    )
    in_maps = []
    for c in range(8):
        b, half = c // 2, c % 2
        rows = _own_rows(half)
        dmask = np.zeros((2, 128, 512), dtype=np.float32)
        for jp in range(2):
            off = ((2 * jp + half) % 4) * 128
            r = np.arange(128)[:, None]
            k = np.arange(512)[None, :]
            dmask[jp] = np.where(k <= off + r, 0.0, NEG)
        m = dict(shared)
        m["xb"] = np.ascontiguousarray(x[b])
        m["xq"] = np.ascontiguousarray(x[b][rows])
        m["cosq"] = np.ascontiguousarray(cos_t[:, rows])
        m["sinqm"] = np.ascontiguousarray(sinm_t[:, rows])
        m["dmask"] = dmask
        in_maps.append(m)
    return in_maps


_NC_CACHE = {}


def _get_nc():
    key = (tuple(DEBUG_TAPS), PHASES, SKIP_P1B, P1A_CUT, P1B_CUT, P1B_SUB)
    if key not in _NC_CACHE:
        _NC_CACHE[key] = build_program()
    return _NC_CACHE[key]


def kernel(**inputs):
    nc = _get_nc()
    in_maps = make_in_maps(inputs)
    res = run_bass_kernel_spmd(nc, in_maps, core_ids=list(range(8)))
    out = np.empty((B, S, H), dtype=np.float32)
    for c in range(8):
        b, half = c // 2, c % 2
        rows = _own_rows(half)
        out[b][rows] = np.asarray(res.results[c]["out"])
    kernel.last_results = res
    return out


# version 10
# speedup vs baseline: 1.1815x; 1.0399x over previous
import numpy as np
import ml_dtypes
from contextlib import ExitStack

import concourse.bass as bass
import concourse.mybir as mybir
from concourse.bass_utils import run_bass_kernel_spmd

F32 = mybir.dt.float32
BF16 = mybir.dt.bfloat16
U8 = mybir.dt.uint8
AF = mybir.ActivationFunctionType
ALU = mybir.AluOpType
AX = mybir.AxisListType

B, S, H = 4, 4096, 2048
NH = 16
QL = 1536
KVL = 512
DR = 64
DN = 128
DV = 128
QH = DN + DR
SOFTMAX_SCALE = QH ** -0.5
ROPE_THETA = 1e4
EPS_RMS = 1e-6
EPS_LN = 1e-5
E = 8
DE = 1408
DSH = 2816
TQ = 2048
NG_B = S // 512
NG_Q = TQ // 512
NEG = -1.0e30
EXP_SHIFT = 24.0

DEBUG_TAPS = []
PHASES = 4
SKIP_P1B = False
P1A_CUT = 99
P1B_CUT = 99
P1B_SUB = 3



class Stream:

    def __init__(self, name):
        self.name = name
        self.n = 0
        self.sem = None


class EQ:

    def __init__(self, name):
        self.name = name
        self.ops = []
        self.n_inc = 0
        self.waited = {}

    def op(self, fn, inc=True):
        if inc:
            self.n_inc += 1
            me = self

            def run(eng, fn=fn, me=me):
                fn(eng).then_inc(me.sem, 1)
            self.ops.append(run)
            return ("c", self.name, self.n_inc)
        self.ops.append(lambda eng, fn=fn: fn(eng))
        return None

    def dma(self, stream, out, in_, **kw):
        st = self.fw.stream(stream)
        st.n += 1

        def run(eng, out=out, in_=in_, kw=kw, st=st):
            eng.dma_start(out=out, in_=in_, **kw).then_inc(st.sem, 16)
        self.ops.append(run)
        return ("s", st.name, 16 * st.n)

    def wait(self, tok):
        if tok is None:
            return
        kind, qname, val = tok
        assert not (qname == self.name and kind == "c") or val <= self.n_inc
        key = (kind, qname)
        if self.waited.get(key, -1) >= val:
            return
        self.waited[key] = val
        fw = self.fw

        def run(eng, kind=kind, qname=qname, val=val, fw=fw):
            sem = fw.q[qname].sem if kind == "c" else fw.streams[qname].sem
            eng.wait_ge(sem, val)
        self.ops.append(run)

    def tok(self):
        return ("c", self.name, self.n_inc)


class Arena:
    def __init__(self, ap, nbytes):
        self.ap = ap
        self.nbytes = nbytes
        self.off = 0
        self.marks = []
        self.peak = 0

    def alloc(self, free_shape, dtype, pbase=0, pcount=128, align=64):
        esz = {F32: 4, BF16: 2, U8: 1}[dtype]
        n = 1
        for d in free_shape:
            n *= d
        nb = n * esz
        self.off = (self.off + align - 1) // align * align
        assert self.off + nb <= self.nbytes, (
            f"arena overflow: need {(self.off + nb)/1024:.0f}KB")
        v = self.ap[pbase:pbase + pcount, self.off:self.off + nb].bitcast(dtype)
        if len(free_shape) > 1:
            names = " ".join(f"d{i}" for i in range(len(free_shape)))
            kw = {f"d{i}": d for i, d in enumerate(free_shape)}
            v = v.rearrange(f"p ({names}) -> p {names}", **kw)
        self.off += nb
        self.peak = max(self.peak, self.off)
        return v

    def mark(self):
        self.marks.append(self.off)

    def reset(self):
        self.off = self.marks.pop()


class FW:
    def __init__(self, nc, es):
        self.nc = nc
        self.q = {n: EQ(n) for n in ("pe", "ve", "se", "gp", "sy")}
        for q in self.q.values():
            q.fw = self
        self.pe, self.ve, self.se = self.q["pe"], self.q["ve"], self.q["se"]
        self.gp, self.sy = self.q["gp"], self.q["sy"]
        self.es = es
        self.streams = {}
        for n, q in self.q.items():
            q.sem = es.enter_context(nc.semaphore(f"s_{n}"))
        self.ps = [es.enter_context(nc.psum_tensor(f"ps{i}", [128, 512], F32))
                   for i in range(8)]
        ARENA_B = 207 * 1024
        self.ar_t = es.enter_context(nc.sbuf_tensor("arena", [128, ARENA_B], U8))
        self.ar = Arena(self.ar_t.ap(), ARENA_B)

    def stream(self, name):
        if name not in self.streams:
            st = Stream(name)
            st.sem = self.es.enter_context(self.nc.semaphore(f"st_{name}"))
            self.streams[name] = st
        return self.streams[name]

    def finalize(self, blk):
        for name, dec in (("sy", blk.sync), ("pe", blk.tensor), ("ve", blk.vector),
                          ("se", blk.scalar), ("gp", blk.gpsimd)):
            q = self.q[name]

            def section(eng, q=q):
                for fn in q.ops:
                    fn(eng)
            dec(section)


def emit_ln_tm(fw, x_tm, out_bf, st6, mv, vtmp, rstd, nt, D, eps):
    ve, se = fw.ve, fw.se
    nk = D // 512
    for t in range(nt):
        for k in range(nk):
            ve.op(lambda v, t=t, k=k: v.bn_stats(
                st6[:, t, k, :], x_tm[:, t, 512 * k:512 * (k + 1)]),
                inc=(t == nt - 1 and k == nk - 1))
    ve.wait(ve.tok())
    for t in range(nt):
        ve.op(lambda v, t=t: v.bn_aggr(
            mv[:, t, :], st6[:, t, :, :].rearrange("p a b -> p (a b)")),
            inc=(t == nt - 1))
    ve.wait(ve.tok())
    ve.op(lambda v: v.tensor_scalar_add(vtmp[:, :], mv[:, :, 1], eps))
    se.wait(ve.tok())
    se.op(lambda s: s.sqrt(vtmp[:, :], vtmp[:, :]))
    ve.wait(se.tok())
    ve.op(lambda v: v.reciprocal(rstd[:, :], vtmp[:, :]))
    ve.wait(ve.tok())
    if out_bf is not None:
        for t in range(nt):
            ve.op(lambda v, t=t: v.tensor_scalar(
                out_bf[:, t, :], x_tm[:, t, :], mv[:, t, 0:1],
                rstd[:, t:t + 1], ALU.subtract, ALU.mult), inc=(t == nt - 1))
    return ve.tok()


def emit_transpose_tm2fm(fw, src_bf, dst_fm, nt, D, ps_trs, id_bf, tok_src,
                         tok_dst_free=None):
    pe, ve = fw.pe, fw.ve
    pe.wait(tok_src)
    ve.wait(tok_dst_free)
    nchunk = D // 128
    for c in range(nchunk):
        bank = ps_trs[c % 2]
        if c >= 2:
            pe.wait(ve.tok())
        for t in range(nt):
            pe.op(lambda tt, c=c, t=t, bank=bank: tt.transpose(
                bank[:, 128 * t: 128 * (t + 1)],
                src_bf[:, t, 128 * c:128 * (c + 1)], id_bf),
                inc=(t == nt - 1))
        ve.wait(pe.tok())
        ve.op(lambda v, c=c, bank=bank, nt=nt: v.tensor_copy(
            dst_fm[:, c, 0:128 * nt], bank[:, 0:128 * nt]))
    return ve.tok(), pe.tok()


def build_program():
    nc = bass.Bass("TRN2", target_bir_lowering=False)

    dr = {}

    def din(name, shape, dtype=F32):
        dr[name] = nc.dram_tensor(name, shape, dtype, kind="ExternalInput").ap()
        return dr[name]

    xb = din("xb", [S, H])
    xq = din("xq", [TQ, H])
    wqa = din("wqa", [H, QL])
    wqb = din("wqb", [QL, NH * 256])
    wkva = din("wkva", [H, KVL + 2 * DR])
    wkvb = din("wkvb", [KVL, NH * (DN + DV)])
    wo = din("wo", [NH * DV, H])
    wrt = din("wrt", [H, E])
    weg = din("weg", [E, H, DE])
    weu = din("weu", [E, H, DE])
    wed = din("wed", [E, DE, H])
    wsg = din("wsg", [H, DSH])
    wsu = din("wsu", [H, DSH])
    wsd = din("wsd", [DSH, H])
    gamq_in = din("gamq", [128, QL // 128])
    gamkv_in = din("gamkv", [128, KVL // 128])
    id_in = din("id_in", [128, 128])
    ones_in = din("ones_in", [128, 128])
    sel_in = din("sel_in", [8, E * 128])
    cosk_in = din("cosk", [DR, S])
    sinkm_in = din("sinkm", [DR, S])
    cosq_in = din("cosq", [DR, TQ])
    sinqm_in = din("sinqm", [DR, TQ])
    dmask_in = din("dmask", [2, 128, 512])

    out = nc.dram_tensor("out", [TQ, H], F32, kind="ExternalOutput").ap()
    o_scr = nc.dram_tensor("o_scr", [NH, DV, TQ], BF16, kind="Internal").ap()
    x2_scr = nc.dram_tensor("x2_scr", [TQ // 128, 128, H], F32, kind="Internal").ap()
    ln2_scr = nc.dram_tensor("ln2_scr", [H // 128, 128, TQ], BF16, kind="Internal").ap()

    taps = {}
    for t in DEBUG_TAPS:
        shape, dtype = DEBUG_SHAPES[t]
        taps[t] = nc.dram_tensor("tap_" + t, shape, dtype, kind="ExternalOutput").ap()

    es = ExitStack()
    with es:
        fw = FW(nc, es)
        ar = fw.ar
        pe, ve, se, gp, sy = fw.pe, fw.ve, fw.se, fw.gp, fw.sy
        ps = fw.ps
        ps_trs = [ps[5].ap().bitcast(BF16), ps[6].ap().bitcast(BF16)]
        ps_row = ps[7]

        id_bf = ar.alloc([128], BF16)
        id_f32 = ar.alloc([128], F32)
        ones_bf = ar.alloc([128], BF16)
        sel_bf = ar.alloc([E * 128], BF16, pcount=8)
        gamq = ar.alloc([QL // 128], F32)
        gamkv = ar.alloc([KVL // 128], F32)
        ar.mark()
        c_norm = ar.alloc([KVL // 128, S], BF16)
        k_pe = ar.alloc([S], BF16, pcount=DR)

        gp.dma("cst_g", id_bf, id_in)
        gp.dma("cst_g", ones_bf, ones_in)
        tok_consts_gp = gp.dma("cst_g", sel_bf, sel_in)
        sy.dma("cst_s", id_f32, id_in)
        sy.dma("cst_s", gamq, gamq_in)
        tok_consts_sy = sy.dma("cst_s", gamkv, gamkv_in)

        ar.mark()
        cosk = ar.alloc([2, 512], BF16, pcount=DR)
        sinkm = ar.alloc([2, 512], BF16, pcount=DR)
        wkva_sb = ar.alloc([16, KVL + 2 * DR], BF16)
        x_tm2b = ar.alloc([2, 4, H], F32)
        st6 = ar.alloc([4, 4, 6], F32)
        mv2b = ar.alloc([2, 4, 2], F32)
        vtmp = ar.alloc([4], F32)
        rstd2b = ar.alloc([2, 4], F32)
        ln_bf2b = ar.alloc([2, 4, H], BF16)
        ln_fm2b = ar.alloc([2, 16, 512], BF16)
        sq_bf = ar.alloc([4, 512], BF16)
        row_f = ar.alloc([512], F32, pcount=1)
        row_bf = ar.alloc([512], BF16, pcount=1)
        rb_bf = ar.alloc([512], BF16)
        kpe_f = ar.alloc([512], BF16, pcount=DR)
        kpe_p = ar.alloc([512], BF16, pcount=DR)
        kpe_r = ar.alloc([512], BF16, pcount=DR)

        wkva_r = wkva.rearrange("(dc p) m -> p dc m", p=128)
        for i in range(4):
            tok_p1a_w = gp.dma("wkva", wkva_sb[:, 4 * i:4 * i + 4, :],
                               wkva_r[:, 4 * i:4 * i + 4, :])
        tok_rope_read_prev = None

        ve.wait(tok_consts_sy)
        tok_apply_prev = None
        tok_sq_prev = None
        tok_kpe_prev = None
        tok_rb_prev = None
        tok_pe_read_fm = [None, None]
        tok_ln_prev = [None, None]

        for g in range(NG_B):
            gb_ = g % 2
            x_tm = x_tm2b[:, gb_]
            mv = mv2b[:, gb_]
            rstd = rstd2b[:, gb_]
            ln_bf = ln_bf2b[:, gb_]
            ln_fm = ln_fm2b[:, gb_]
            gc = slice(512 * g, 512 * (g + 1))
            xg = xb[512 * g:512 * (g + 1), :].rearrange("(t p) d -> p t d", p=128)
            sy.wait(tok_ln_prev[gb_])
            tok_x = None
            for t in range(4):
                tok_x = sy.dma(f"xa{gb_}", x_tm[:, t, :], xg[:, t, :])
            ve.wait(tok_x)
            tok_ln = emit_ln_tm(fw, x_tm, ln_bf, st6, mv, vtmp, rstd, 4, H, EPS_LN)
            tok_ln_prev[gb_] = tok_ln
            if P1A_CUT < 1:
                continue

            pe.wait(tok_consts_gp)
            tok_fm, _tok_tr = emit_transpose_tm2fm(
                fw, ln_bf, ln_fm, 4, H, ps_trs, id_bf, tok_ln,
                tok_dst_free=tok_pe_read_fm[gb_])

            if P1A_CUT < 2:
                continue
            pe.wait(tok_fm)
            pe.wait(tok_p1a_w)
            pe.wait(tok_apply_prev)
            pe.wait(tok_sq_prev)
            pe.wait(tok_kpe_prev)
            tok_ckv_m = []
            for m in range(5):
                cols = slice(128 * m, 128 * (m + 1))
                dst = ps[m][:, 0:512]
                for dc in range(16):
                    pe.op(lambda tt, dc=dc, cols=cols, dst=dst, ln_fm=ln_fm:
                          tt.matmul(
                              dst, wkva_sb[:, dc, cols], ln_fm[:, dc, :],
                              start=(dc == 0), stop=(dc == 15)), inc=(dc == 15))
                tok_ckv_m.append(pe.tok())
            tok_pe_read_fm[gb_] = pe.tok()

            if P1A_CUT < 3:
                tok_apply_prev = tok_ckv_m[3]
                tok_kpe_prev = tok_ckv_m[4]
                continue
            se.wait(tok_ckv_m[3])
            for m in range(4):
                se.op(lambda s, m=m: s.square(sq_bf[:, m, :], ps[m][:, :]),
                      inc=(m == 3))
            tok_sq = se.tok()
            tok_sq_prev = tok_sq

            pe.wait(tok_sq)
            pe.wait(tok_rb_prev)
            for m in range(4):
                pe.op(lambda tt, m=m: tt.matmul(
                    ps_row[0:1, 0:512], ones_bf[:, 0:1], sq_bf[:, m, :],
                    start=(m == 0), stop=(m == 3)), inc=(m == 3))
            tok_sum = pe.tok()

            if P1A_CUT < 4:
                tok_apply_prev = tok_ckv_m[3]
                tok_kpe_prev = tok_ckv_m[4]
                tok_rb_prev = tok_sum
                continue
            ve.wait(tok_sum)
            ve.op(lambda v: v.tensor_scalar(
                row_f[:, :], ps_row[0:1, 0:512], 1.0 / KVL, EPS_RMS,
                ALU.mult, ALU.add))
            se.wait(ve.tok())
            se.op(lambda s: s.sqrt(row_f[:, :], row_f[:, :]))
            ve.wait(se.tok())
            ve.op(lambda v: v.reciprocal(row_f[:, :], row_f[:, :]))
            ve.wait(ve.tok())
            ve.op(lambda v: v.tensor_copy(row_bf[:, :], row_f[:, :]))
            pe.wait(ve.tok())
            pe.op(lambda tt: tt.matmul(ps_row[:, 0:512], ones_bf[0:1, :],
                                       row_bf[:, :], start=True, stop=True))
            ve.wait(pe.tok())
            ve.op(lambda v: v.tensor_copy(rb_bf[:, :], ps_row[:, 0:512]))
            tok_rb_prev = ve.tok()
            ve.wait(tok_rb_prev)
            if P1A_CUT < 5:
                tok_apply_prev = tok_ckv_m[3]
                tok_kpe_prev = tok_ckv_m[4]
                continue
            for m in range(4):
                ve.op(lambda v, m=m, gc=gc: v.scalar_tensor_tensor(
                    c_norm[:, m, gc], ps[m][:, :], gamkv[:, m:m + 1], rb_bf[:, :],
                    ALU.mult, ALU.mult), inc=(m == 3))
            tok_apply_prev = ve.tok()

            if P1A_CUT < 6:
                tok_kpe_prev = tok_ckv_m[4]
                continue
            tb = g % 2
            gp.wait(tok_rope_read_prev)
            gp.dma("ropetab", cosk[:, tb, :], cosk_in[:, gc])
            tok_tab = gp.dma("ropetab", sinkm[:, tb, :], sinkm_in[:, gc])
            ve.wait(tok_ckv_m[4])
            ve.op(lambda v: v.tensor_copy(kpe_f[:, :], ps[4][0:DR, 0:512]),
                  inc=False)
            ve.op(lambda v: v.tensor_copy(kpe_p[:, :], ps[4][DR:2 * DR, 0:512]))
            tok_kpe_prev = ve.tok()
            ve.wait(tok_kpe_prev)
            ve.wait(tok_tab)
            ve.op(lambda v, tb=tb: v.tensor_mul(
                kpe_r[:, :], kpe_p[:, :], sinkm[:, tb, :]), inc=False)
            ve.op(lambda v, tb=tb: v.tensor_mul(
                kpe_f[:, :], kpe_f[:, :], cosk[:, tb, :]))
            ve.wait(ve.tok())
            ve.op(lambda v, gc=gc: v.tensor_add(k_pe[:, gc], kpe_r[:, :],
                                                kpe_f[:, :]))
            tok_rope_read_prev = ve.tok()

        tok_p1a_ve = ve.tok()
        tok_p1a_pe = pe.tok()
        tok_p1a_se = se.tok()
        ar.reset()

        if "c_norm" in taps:
            sy.wait(tok_p1a_ve)
            for m in range(4):
                sy.dma("taps", taps["c_norm"].rearrange("m p s -> p m s")[:, m, :],
                       c_norm[:, m, :])
        if "k_pe" in taps:
            sy.wait(tok_p1a_ve)
            sy.dma("taps", taps["k_pe"], k_pe[:, :])

        qn = ar.alloc([QL // 128, TQ], BF16)
        tok_p1b_ve, tok_p1b_pe, tok_p1b_se = tok_p1a_ve, tok_p1a_pe, tok_p1a_se
        if SKIP_P1B:
            import types
            p1b_range = range(0)
        else:
            p1b_range = range(NG_Q)
        ar.mark()
        x_tm2 = ar.alloc([4, H], F32)
        st6b = ar.alloc([4, 4, 6], F32)
        mvb = ar.alloc([4, 2], F32)
        vtmpb = ar.alloc([4], F32)
        rstdb = ar.alloc([4], F32)
        ln_bf2 = ar.alloc([4, H], BF16)
        ln_fm2 = ar.alloc([16, 512], BF16)
        qa_bf = ar.alloc([12, 512], BF16)
        sq12 = ar.alloc([12, 512], BF16)
        row_f2 = ar.alloc([512], F32, pcount=1)
        row_bf2 = ar.alloc([512], BF16, pcount=1)
        rb_bf2 = ar.alloc([512], BF16)
        wqa_sb = ar.alloc([2, 16, 128], BF16)

        wqa_r = wqa.rearrange("(dc p) m -> p dc m", p=128)

        sy.wait(tok_p1a_ve)
        sy.wait(tok_p1a_pe)
        sy.wait(tok_p1a_se)
        gp.wait(tok_p1a_ve)
        gp.wait(tok_p1a_pe)
        gp.wait(tok_p1a_se)

        tok_pe_read_fm2 = None
        tok_ln_prev2 = None
        buf_user = [None, None]
        bank_guard = {b_: (tok_p1a_ve, tok_p1a_se) for b_ in range(4)}
        tok_rb2_prev = tok_p1a_ve

        for g in p1b_range:
            gc = slice(512 * g, 512 * (g + 1))
            xg = xq[512 * g:512 * (g + 1), :].rearrange("(t p) d -> p t d", p=128)
            sy.wait(tok_ln_prev2)
            tok_x = None
            for t in range(4):
                tok_x = sy.dma("xq_tm", x_tm2[:, t, :], xg[:, t, :])
            ve.wait(tok_x)
            tok_ln = emit_ln_tm(fw, x_tm2, ln_bf2, st6b, mvb, vtmpb, rstdb,
                                4, H, EPS_LN)
            tok_ln_prev2 = tok_ln
            if P1B_CUT < 1:
                continue
            tok_fm2, _ = emit_transpose_tm2fm(
                fw, ln_bf2, ln_fm2, 4, H, ps_trs, id_bf, tok_ln,
                tok_dst_free=tok_pe_read_fm2)
            if P1B_CUT < 2:
                tok_pe_read_fm2 = pe.tok()
                continue

            pe.wait(tok_fm2)
            tok_sq_m = {}
            tok_stage_m = {}
            for m in range(12):
                bank = ps[m % 4]
                buf = m % 2
                if P1B_SUB >= 1:
                    gp.wait(buf_user[buf])
                    for i in range(4):
                        tok_w = gp.dma(f"wqa{buf}",
                                       wqa_sb[:, buf, 4 * i:4 * i + 4, :],
                                       wqa_r[:, 4 * i:4 * i + 4, 128 * m:128 * (m + 1)])
                    pe.wait(tok_w)
                if m % 4 in bank_guard:
                    t1, t2 = bank_guard[m % 4]
                    pe.wait(t1)
                    pe.wait(t2)
                for dc in range(16):
                    pe.op(lambda tt, dc=dc, m=m, bank=bank, buf=buf: tt.matmul(
                        bank[:, :], wqa_sb[:, buf, dc, :], ln_fm2[:, dc, :],
                        start=(dc == 0), stop=(dc == 15)), inc=(dc == 15))
                tok_mm = pe.tok()
                buf_user[buf] = tok_mm
                if P1B_SUB >= 2:
                    se.wait(tok_mm)
                    tok_sq_m[m] = se.op(lambda s, m=m, bank=bank: s.square(
                        sq12[:, m, :], bank[:, :]))
                else:
                    tok_sq_m[m] = tok_mm
                if P1B_SUB >= 3:
                    ve.wait(tok_sq_m[m])
                    tok_stage_m[m] = ve.op(lambda v, m=m, bank=bank: v.tensor_copy(
                        qa_bf[:, m, :], bank[:, :]))
                else:
                    tok_stage_m[m] = tok_mm
                bank_guard[m % 4] = (tok_sq_m[m], tok_stage_m[m])
            tok_pe_read_fm2 = pe.tok()

            if P1B_CUT < 3:
                continue
            for m in range(12):
                pe.wait(tok_sq_m[m])
            pe.wait(tok_rb2_prev)
            for m in range(12):
                pe.op(lambda tt, m=m: tt.matmul(
                    ps_row[0:1, 0:512], ones_bf[:, 0:1], sq12[:, m, :],
                    start=(m == 0), stop=(m == 11)), inc=(m == 11))
            tok_sum = pe.tok()
            ve.wait(tok_sum)
            ve.op(lambda v: v.tensor_scalar(
                row_f2[:, :], ps_row[0:1, 0:512], 1.0 / QL, EPS_RMS,
                ALU.mult, ALU.add))
            se.wait(ve.tok())
            se.op(lambda s: s.sqrt(row_f2[:, :], row_f2[:, :]))
            ve.wait(se.tok())
            ve.op(lambda v: v.reciprocal(row_f2[:, :], row_f2[:, :]))
            ve.wait(ve.tok())
            ve.op(lambda v: v.tensor_copy(row_bf2[:, :], row_f2[:, :]))
            pe.wait(ve.tok())
            pe.op(lambda tt: tt.matmul(ps_row[:, 0:512], ones_bf[0:1, :],
                                       row_bf2[:, :], start=True, stop=True))
            ve.wait(pe.tok())
            ve.op(lambda v: v.tensor_copy(rb_bf2[:, :], ps_row[:, 0:512]))
            tok_rb2_prev = ve.tok()
            ve.wait(tok_rb2_prev)
            if P1B_CUT < 4:
                continue
            for m in range(12):
                ve.op(lambda v, m=m, gc=gc: v.scalar_tensor_tensor(
                    qn[:, m, gc], qa_bf[:, m, :], gamq[:, m:m + 1], rb_bf2[:, :],
                    ALU.mult, ALU.mult), inc=(m == 11))
            tok_qa_read = ve.tok()
            for b_ in range(4):
                bank_guard[b_] = (bank_guard[b_][0], tok_qa_read)

        if not SKIP_P1B:
            tok_p1b_ve = ve.tok()
            tok_p1b_pe = pe.tok()
            tok_p1b_se = se.tok()
        ar.reset()

        if "qn" in taps:
            sy.wait(tok_p1b_ve)
            for m in range(12):
                sy.dma("taps", taps["qn"].rearrange("m p s -> p m s")[:, m, :],
                       qn[:, m, :])

        if PHASES >= 2:
            p2 = build_attention(nc, fw, ar, locals())
        ar.reset()

        if PHASES >= 3:
            p3 = build_p3(nc, fw, ar, locals())
        if PHASES >= 4:
            build_p4(nc, fw, ar, locals())

        if PHASES < 3:
            ar.mark()
            zz = ar.alloc([4, H], F32)
            sy.wait(tok_p1b_ve)
            sy.wait(tok_p1b_pe)
            sy.wait(tok_p1b_se)
            tok_st = None
            for g in range(NG_Q):
                xg = xq[512 * g:512 * (g + 1), :].rearrange("(t p) d -> p t d", p=128)
                og = out[512 * g:512 * (g + 1), :].rearrange("(t p) d -> p t d", p=128)
                sy.wait(tok_st)
                tok_x = None
                for t in range(4):
                    tok_x = sy.dma("stub_x", zz[:, t, :], xg[:, t, :])
                sy.wait(tok_x)
                for t in range(4):
                    tok_st = sy.dma("out", og[:, t, :], zz[:, t, :])
            ar.reset()

        for qq in ("pe", "ve", "se", "gp"):
            sy.wait(fw.q[qq].tok())
        for stn, st_ in fw.streams.items():
            sy.wait(("s", stn, 16 * st_.n))

        with nc.Block() as blk:
            fw.finalize(blk)
    return nc


def build_p4(nc, fw, ar, EV):
    pe, ve, se, gp, sy = fw.pe, fw.ve, fw.se, fw.gp, fw.sy
    ps = fw.ps
    wrt = EV["wrt"]; wsg = EV["wsg"]; wsu = EV["wsu"]; wsd = EV["wsd"]
    weg = EV["weg"]; weu = EV["weu"]; wed = EV["wed"]
    ln2_scr = EV["ln2_scr"]; x2_scr = EV["x2_scr"]; out = EV["out"]
    taps = EV["taps"]; id_bf = EV["id_bf"]; id_f32 = EV["id_f32"]
    sel_bf = EV["sel_bf"]

    ar.mark()
    ln2_sb = ar.alloc([16, 1024], BF16)
    h_e = ar.alloc([22, 1024], BF16)
    ffn_acc = ar.alloc([16, 1024], F32)
    wg_sb = ar.alloc([2, 16, 128], BF16)
    wu_sb = ar.alloc([2, 16, 128], BF16)
    wd_sb = ar.alloc([2, 22, 128], BF16)
    sg_bf = ar.alloc([2, 512], BF16)
    wrt_sb = ar.alloc([16, 8], BF16)
    lg_fm = ar.alloc([1024], F32, pcount=8)
    lg_tm = ar.alloc([8, 8], F32)
    pexp = ar.alloc([8, 8], F32)
    prow = ar.alloc([8, 8], F32)
    rmax8 = ar.alloc([8], F32)
    rsum8 = ar.alloc([8], F32)
    max1 = ar.alloc([8], F32)
    max2 = ar.alloc([8], F32)
    gates_tm = ar.alloc([8, 8], BF16)
    gates_fm = ar.alloc([1024], BF16, pcount=8)
    gb_bf = ar.alloc([1024], BF16)
    ffn_tm = ar.alloc([2, H], F32)
    x2_ld = ar.alloc([H], F32)
    st6 = ar.alloc([1, 4, 6], F32)
    mv = ar.alloc([1, 2], F32)
    vtmp = ar.alloc([1], F32)
    rstd = ar.alloc([1], F32)

    ps_bf = [ps[i].ap().bitcast(BF16) for i in range(8)]

    tok_p3 = [fw.q[n].tok() for n in ("pe", "ve", "se", "gp")]
    str_p3 = [("s", n, 16 * fw.streams[n].n)
              for n in ("ln2_scr", "x2_scr", "o_scr", "taps")
              if n in fw.streams]
    for q_ in (sy, gp, pe, ve, se):
        for t_ in tok_p3 + str_p3:
            q_.wait(t_)

    gp.dma("wrt", wrt_sb, wrt.rearrange("(dc p) e -> p dc e", p=128))

    wsg_r = wsg.rearrange("(dc p) m -> p dc m", p=128)
    wsu_r = wsu.rearrange("(dc p) m -> p dc m", p=128)
    wsd_r = wsd.rearrange("(fc p) m -> p fc m", p=128)
    weg_r = weg.rearrange("e (dc p) m -> e p dc m", p=128)
    weu_r = weu.rearrange("e (dc p) m -> e p dc m", p=128)
    wed_r = wed.rearrange("e (fc p) m -> e p fc m", p=128)

    br = {i: None for i in range(8)}
    bw = {i: None for i in range(8)}
    tok_guw_used = [None, None]
    tok_wd_used = [None, None]
    tok_ln2_rd = None
    tok_acc_rd = None
    tok_out_store = None
    tok_x2_rd = None
    tok_h_rd = None
    tok_sgrd = [None, None]

    ln2_r = ln2_scr.rearrange("c p t -> p c t")

    for qg2 in range(2):
        qc = slice(1024 * qg2, 1024 * (qg2 + 1))
        sy.wait(tok_ln2_rd)
        tok_ln2 = None
        for i in range(4):
            tok_ln2 = sy.dma("ln2_sb", ln2_sb[:, 4 * i:4 * i + 4, :],
                             ln2_r[:, 4 * i:4 * i + 4, qc])

        pe.wait(tok_ln2)
        pe.wait(("s", "wrt", 16 * fw.streams["wrt"].n))
        for half in range(2):
            bank = half
            pe.wait(br[bank])
            for dc in range(16):
                pe.op(lambda tt, half=half, dc=dc, bank=bank: tt.matmul(
                    ps[bank][0:8, :], wrt_sb[:, dc, :],
                    ln2_sb[:, dc, 512 * half:512 * (half + 1)],
                    start=(dc == 0), stop=(dc == 15)), inc=(dc == 15))
            bw[bank] = pe.tok()
            ve.wait(bw[bank])
            ve.op(lambda v, half=half, bank=bank: v.tensor_copy(
                lg_fm[:, 512 * half:512 * (half + 1)], ps[bank][0:8, :]))
            br[bank] = ve.tok()
        tok_lgfm = ve.tok()
        pe.wait(tok_lgfm)
        pe.wait(br[2])
        for t8 in range(8):
            pe.op(lambda tt, t8=t8: tt.transpose(
                ps[2][:, 8 * t8:8 * (t8 + 1)],
                lg_fm[:, 128 * t8:128 * (t8 + 1)], id_f32[0:8, 0:8]),
                inc=(t8 == 7))
        bw[2] = pe.tok()
        ve.wait(bw[2])
        ve.op(lambda v: v.tensor_copy(
            lg_tm[:, :, :], ps[2][:, 0:64].rearrange("p (a b) -> p a b", b=8)))
        br[2] = ve.tok()
        ve.wait(ve.tok())
        ve.op(lambda v: v.reduce_max(rmax8[:, :], lg_tm[:, :, :], axis=AX.X))
        ve.wait(ve.tok())
        ve.op(lambda v: v.tensor_tensor(
            pexp[:, :, :], lg_tm[:, :, :],
            rmax8[:, :].unsqueeze(2).broadcast_to([128, 8, 8]), ALU.subtract))
        tok_sub = ve.tok()
        se.wait(tok_sub)
        se.op(lambda s: s.activation(pexp[:, :, :], pexp[:, :, :], AF.Exp))
        tok_pexp = se.tok()
        ve.wait(tok_pexp)
        ve.op(lambda v: v.reduce_sum(rsum8[:, :], pexp[:, :, :], axis=AX.X),
              inc=False)
        ve.op(lambda v: v.reduce_max(max1[:, :], pexp[:, :, :], axis=AX.X))
        ve.wait(ve.tok())
        ve.op(lambda v: v.tensor_tensor(
            prow[:, :, :], pexp[:, :, :],
            max1[:, :].unsqueeze(2).broadcast_to([128, 8, 8]), ALU.is_ge))
        ve.wait(ve.tok())
        ve.op(lambda v: v.scalar_tensor_tensor(
            prow[:, :, :], prow[:, :, :], -1e30, pexp[:, :, :],
            ALU.mult, ALU.add))
        ve.wait(ve.tok())
        ve.op(lambda v: v.reduce_max(max2[:, :], prow[:, :, :], axis=AX.X),
              inc=False)
        ve.op(lambda v: v.reciprocal(rsum8[:, :], rsum8[:, :]))
        ve.wait(ve.tok())
        ve.op(lambda v: v.tensor_tensor(
            prow[:, :, :], pexp[:, :, :],
            max2[:, :].unsqueeze(2).broadcast_to([128, 8, 8]), ALU.is_ge))
        ve.wait(ve.tok())
        ve.op(lambda v: v.tensor_mul(prow[:, :, :], prow[:, :, :],
                                     pexp[:, :, :]))
        ve.wait(ve.tok())
        ve.op(lambda v: v.tensor_tensor(
            gates_tm[:, :, :], prow[:, :, :],
            rsum8[:, :].unsqueeze(2).broadcast_to([128, 8, 8]), ALU.mult))
        tok_gates_tm = ve.tok()
        pe.wait(tok_gates_tm)
        pe.wait(br[3])
        for t8 in range(8):
            pe.op(lambda tt, t8=t8: tt.transpose(
                ps_bf[3][0:8, 128 * t8:128 * (t8 + 1)],
                gates_tm[:, t8, :], id_bf), inc=(t8 == 7))
        bw[3] = pe.tok()
        ve.wait(bw[3])
        ve.op(lambda v: v.tensor_copy(gates_fm[:, :], ps_bf[3][0:8, 0:1024]))
        br[3] = ve.tok()
        tok_gates_fm = ve.tok()
        if "gates" in taps:
            sy.wait(tok_gates_fm)
            sy.dma("taps", taps["gates"][:, qc], gates_fm[:, :])

        for pi in range(9):
            nF = 22 if pi == 0 else 11
            if pi == 0:
                g_r, u_r, d_r = wsg_r, wsu_r, wsd_r
            else:
                g_r = weg_r[pi - 1]
                u_r = weu_r[pi - 1]
                d_r = wed_r[pi - 1]
            if pi >= 1:
                e = pi - 1
                pe.wait(tok_gates_fm)
                for half in range(2):
                    pe.wait(br[4])
                    pe.op(lambda tt, e=e, half=half: tt.matmul(
                        ps[4][:, :], sel_bf[0:8, 128 * e:128 * (e + 1)],
                        gates_fm[0:8, 512 * half:512 * (half + 1)],
                        start=True, stop=True))
                    bw[4] = pe.tok()
                    ve.wait(bw[4])
                    ve.op(lambda v, half=half: v.tensor_copy(
                        gb_bf[:, 512 * half:512 * (half + 1)], ps[4][:, :]))
                    br[4] = ve.tok()
            for f in range(nF):
                wb = f % 2
                gp.wait(tok_guw_used[wb])
                tok_wg = None
                for i in range(4):
                    tok_wg = gp.dma(f"wg{wb}", wg_sb[:, wb, 4 * i:4 * i + 4, :],
                                    g_r[:, 4 * i:4 * i + 4, 128 * f:128 * (f + 1)])
                    tok_wg = gp.dma(f"wg{wb}", wu_sb[:, wb, 4 * i:4 * i + 4, :],
                                    u_r[:, 4 * i:4 * i + 4, 128 * f:128 * (f + 1)])
                pe.wait(tok_wg)
                for half in range(2):
                    bg = (f % 2) * 4 + half * 2
                    bu = bg + 1
                    pe.wait(br[bg])
                    pe.wait(br[bu])
                    for dc in range(16):
                        pe.op(lambda tt, dc=dc, f=f, half=half, bg=bg, wb=wb:
                              tt.matmul(
                                  ps[bg][:, :], wg_sb[:, wb, dc, :],
                                  ln2_sb[:, dc, 512 * half:512 * (half + 1)],
                                  start=(dc == 0), stop=(dc == 15)),
                              inc=(dc == 15))
                        pe.op(lambda tt, dc=dc, f=f, half=half, bu=bu, wb=wb:
                              tt.matmul(
                                  ps[bu][:, :], wu_sb[:, wb, dc, :],
                                  ln2_sb[:, dc, 512 * half:512 * (half + 1)],
                                  start=(dc == 0), stop=(dc == 15)),
                              inc=(dc == 15))
                    bw[bg] = bw[bu] = pe.tok()
                    se.wait(bw[bg])
                    se.wait(tok_sgrd[half])
                    se.op(lambda s, f=f, half=half, bg=bg: s.activation(
                        sg_bf[:, half, :], ps[bg][:, :], AF.Silu))
                    tok_sg = se.tok()
                    br[bg] = tok_sg
                    ve.wait(tok_sg)
                    ve.wait(tok_h_rd)
                    ve.op(lambda v, f=f, half=half, bu=bu: v.tensor_tensor(
                        h_e[:, f, 512 * half:512 * (half + 1)], ps[bu][:, :],
                        sg_bf[:, half, :], ALU.mult))
                    br[bu] = ve.tok()
                    tok_sgrd[half] = br[bu]
                    if pi >= 1:
                        ve.wait(ve.tok())
                        ve.op(lambda v, f=f, half=half: v.tensor_mul(
                            h_e[:, f, 512 * half:512 * (half + 1)],
                            h_e[:, f, 512 * half:512 * (half + 1)],
                            gb_bf[:, 512 * half:512 * (half + 1)]))
                tok_guw_used[wb] = pe.tok()
            tok_h_done = ve.tok()
            pe.wait(tok_h_done)
            for mc in range(16):
                wb = mc % 2
                gp.wait(tok_wd_used[wb])
                tok_wd = None
                f0 = 0
                while f0 < nF:
                    f1 = min(f0 + 4, nF)
                    tok_wd = gp.dma(f"wd{wb}", wd_sb[:, wb, f0:f1, :],
                                    d_r[:, f0:f1, 128 * mc:128 * (mc + 1)])
                    f0 = f1
                pe.wait(tok_wd)
                for half in range(2):
                    bank = (mc % 2) * 2 + half
                    pe.wait(br[bank])
                    for f in range(nF):
                        pe.op(lambda tt, f=f, mc=mc, half=half, bank=bank,
                              wb=wb, nF=nF: tt.matmul(
                                  ps[bank][:, :], wd_sb[:, wb, f, :],
                                  h_e[:, f, 512 * half:512 * (half + 1)],
                                  start=(f == 0), stop=(f == nF - 1)),
                              inc=(f == nF - 1))
                    bw[bank] = pe.tok()
                    ve.wait(bw[bank])
                    if pi == 0:
                        ve.wait(tok_acc_rd)
                        ve.op(lambda v, mc=mc, half=half, bank=bank:
                              v.tensor_copy(
                                  ffn_acc[:, mc, 512 * half:512 * (half + 1)],
                                  ps[bank][:, :]))
                    else:
                        ve.op(lambda v, mc=mc, half=half, bank=bank:
                              v.tensor_add(
                                  ffn_acc[:, mc, 512 * half:512 * (half + 1)],
                                  ps[bank][:, :],
                                  ffn_acc[:, mc, 512 * half:512 * (half + 1)]))
                    br[bank] = ve.tok()
                tok_wd_used[wb] = pe.tok()
            tok_h_rd = pe.tok()
        tok_ln2_rd = pe.tok()
        tok_acc_done = ve.tok()

        for t8 in range(8):
            tb = t8 % 2
            gtile = 8 * qg2 + t8
            for mg in range(4):
                bank = 6 + mg % 2
                pe.wait(tok_acc_done)
                pe.wait(br[bank])
                for mi in range(4):
                    m = 4 * mg + mi
                    pe.op(lambda tt, m=m, mi=mi, t8=t8, bank=bank: tt.transpose(
                        ps[bank][:, 128 * mi:128 * (mi + 1)],
                        ffn_acc[:, m, 128 * t8:128 * (t8 + 1)], id_f32),
                        inc=(mi == 3))
                bw[bank] = pe.tok()
                ve.wait(bw[bank])
                ve.op(lambda v, t8=t8, mg=mg, tb=tb, bank=bank: v.tensor_copy(
                    ffn_tm[:, tb, 512 * mg:512 * (mg + 1)], ps[bank][:, :]))
                br[bank] = ve.tok()
            tok_strip = ve.tok()
            ve.wait(tok_strip)
            for k in range(4):
                ve.op(lambda v, k=k, tb=tb: v.bn_stats(
                    st6[:, 0, k, :], ffn_tm[:, tb, 512 * k:512 * (k + 1)]),
                    inc=(k == 3))
            ve.wait(ve.tok())
            ve.op(lambda v: v.bn_aggr(mv[:, 0, :], st6[:, 0, :, :].rearrange(
                "p a b -> p (a b)")))
            ve.wait(ve.tok())
            ve.op(lambda v: v.tensor_scalar_add(vtmp[:, :], mv[:, :, 1],
                                                EPS_LN))
            se.wait(ve.tok())
            se.op(lambda s: s.sqrt(vtmp[:, :], vtmp[:, :]))
            ve.wait(se.tok())
            ve.op(lambda v: v.reciprocal(rstd[:, :], vtmp[:, :]))
            ve.wait(ve.tok())
            sy.wait(tok_x2_rd)
            sy.wait(tok_out_store)
            tok_x2l = sy.dma("x2_ld", x2_ld[:, :], x2_scr[gtile])
            ve.wait(tok_x2l)
            ve.op(lambda v, tb=tb: v.tensor_scalar(
                ffn_tm[:, tb, :], ffn_tm[:, tb, :], mv[:, 0, 0:1],
                rstd[:, 0:1], ALU.subtract, ALU.mult))
            ve.wait(ve.tok())
            ve.op(lambda v, tb=tb: v.tensor_add(x2_ld[:, :], x2_ld[:, :],
                                                ffn_tm[:, tb, :]))
            tok_out = ve.tok()
            tok_x2_rd = tok_out
            sy.wait(tok_out)
            tok_out_store = sy.dma(
                "out", out[128 * gtile:128 * (gtile + 1), :], x2_ld[:, :])
        tok_acc_rd = pe.tok()

    ar.reset()
    return dict(tok_out_store=tok_out_store)


def build_p3(nc, fw, ar, EV):
    pe, ve, se, gp, sy = fw.pe, fw.ve, fw.se, fw.gp, fw.sy
    ps = fw.ps
    wo = EV["wo"]; xq = EV["xq"]; o_scr = EV["o_scr"]
    x2_scr = EV["x2_scr"]; ln2_scr = EV["ln2_scr"]
    taps = EV["taps"]; id_bf = EV["id_bf"]; id_f32 = EV["id_f32"]
    p2 = EV["p2"]

    ar.mark()
    o_sb2 = ar.alloc([2, 16, 512], BF16)
    wo_sb = ar.alloc([2, 16, 128], BF16)
    attn_fm = ar.alloc([16, 512], F32)
    attn_tm = ar.alloc([4, H], F32)
    x_tm = ar.alloc([4, H], F32)
    tmp = ar.alloc([H], F32)
    ln2_bf = ar.alloc([4, H], BF16)
    ln2_fm = ar.alloc([16, 512], BF16)
    st6 = ar.alloc([4, 4, 6], F32)
    mv = ar.alloc([4, 2], F32)
    vtmp = ar.alloc([4], F32)
    rstd = ar.alloc([4], F32)

    ps_trs_bf = [ps[2].ap().bitcast(BF16), ps[3].ap().bitcast(BF16)]

    tok_p2 = [fw.q[n].tok() for n in ("pe", "ve", "se", "gp")]
    for q_ in (sy, gp, pe, ve, se):
        for t_ in tok_p2:
            q_.wait(t_)
    tok_o_final = ("s", "o_scr", 16 * fw.streams["o_scr"].n)
    sy.wait(tok_o_final)

    wo_r = wo.rearrange("(hc p) m -> p hc m", p=128)
    o_r = o_scr.rearrange("h p q -> p h q")

    tok_ln2_store = None
    tok_x2_store = None
    tok_pe_attn = {}
    tok_wo_used = [None, None]
    br = {i: None for i in range(8)}
    bw = {i: None for i in range(8)}

    for qg in range(NG_Q):
        ob_ = qg % 2
        o_sb = o_sb2[:, ob_]
        qc = slice(512 * qg, 512 * (qg + 1))
        sy.wait(tok_pe_attn.get(qg - 2))
        tok_o = None
        for i in range(4):
            tok_o = sy.dma(f"o_sb{ob_}", o_sb[:, 4 * i:4 * i + 4, :],
                           o_r[:, 4 * i:4 * i + 4, qc])
        for mc in range(16):
            wb = mc % 2
            gp.wait(tok_wo_used[wb])
            tok_w = None
            for i in range(4):
                tok_w = gp.dma(f"wo{wb}", wo_sb[:, wb, 4 * i:4 * i + 4, :],
                               wo_r[:, 4 * i:4 * i + 4, 128 * mc:128 * (mc + 1)])
            bank = mc % 2
            pe.wait(tok_w)
            pe.wait(tok_o)
            pe.wait(br[bank])
            for hc in range(16):
                pe.op(lambda tt, hc=hc, mc=mc, bank=bank, wb=wb, o_sb=o_sb:
                      tt.matmul(
                          ps[bank][:, :], wo_sb[:, wb, hc, :], o_sb[:, hc, :],
                          start=(hc == 0), stop=(hc == 15)), inc=(hc == 15))
            bw[bank] = pe.tok()
            tok_wo_used[wb] = bw[bank]
            ve.wait(bw[bank])
            ve.op(lambda v, mc=mc, bank=bank: v.tensor_copy(
                attn_fm[:, mc, :], ps[bank][:, :]))
            br[bank] = ve.tok()
        tok_attn_fm = ve.tok()
        tok_pe_attn[qg] = pe.tok()

        pe.wait(tok_attn_fm)
        for mc in range(16):
            bank = 2 + mc % 2
            pe.wait(br[bank])
            for t4 in range(4):
                pe.op(lambda tt, mc=mc, t4=t4, bank=bank: tt.transpose(
                    ps[bank][:, 128 * t4:128 * (t4 + 1)],
                    attn_fm[:, mc, 128 * t4:128 * (t4 + 1)], id_f32),
                    inc=(t4 == 3))
            bw[bank] = pe.tok()
            ve.wait(bw[bank])
            ve.op(lambda v, mc=mc, bank=bank: v.tensor_copy(
                attn_tm[:, :, 128 * mc:128 * (mc + 1)],
                ps[bank][:, :].rearrange("p (a b) -> p a b", b=128)))
            br[bank] = ve.tok()
        tok_attn_tm = ve.tok()

        sy.wait(tok_x2_store)
        tok_x = None
        for t in range(4):
            tok_x = sy.dma("x3_tm", x_tm[:, t, :],
                           xq[512 * qg + 128 * t:512 * qg + 128 * (t + 1), :])
        ve.wait(tok_attn_tm)
        tok_stats = emit_ln_tm(fw, attn_tm, None, st6, mv, vtmp, rstd, 4, H,
                               EPS_LN)
        ve.wait(tok_stats)
        ve.wait(tok_x)
        for t in range(4):
            ve.op(lambda v, t=t: v.tensor_scalar(
                tmp[:, :], attn_tm[:, t, :], mv[:, t, 0:1], rstd[:, t:t + 1],
                ALU.subtract, ALU.mult))
            ve.wait(ve.tok())
            ve.op(lambda v, t=t: v.tensor_add(x_tm[:, t, :], x_tm[:, t, :],
                                              tmp[:, :]))
            ve.wait(ve.tok())
        tok_x2 = ve.tok()
        sy.wait(tok_x2)
        for t in range(4):
            tok_x2_store = sy.dma("x2_scr", x2_scr[4 * qg + t], x_tm[:, t, :])
        if "x2" in taps:
            for t in range(4):
                tok_x2_store = sy.dma("x2_scr", taps["x2"][4 * qg + t],
                                      x_tm[:, t, :])
        tok_ln2 = emit_ln_tm(fw, x_tm, ln2_bf, st6, mv, vtmp, rstd, 4, H,
                             EPS_LN)
        tok_fm, tok_tr_pe = emit_transpose_tm2fm(
            fw, ln2_bf, ln2_fm, 4, H, ps_trs_bf, id_bf, tok_ln2,
            tok_dst_free=tok_ln2_store)
        sy.wait(tok_fm)
        tok_st = None
        for i in range(4):
            tok_st = sy.dma("ln2_scr", ln2_scr[:, :, qc].rearrange(
                "c p t -> p c t")[:, 4 * i:4 * i + 4, :],
                ln2_fm[:, 4 * i:4 * i + 4, :])
            if "ln2" in taps:
                tok_st = sy.dma("ln2_scr", taps["ln2"][:, :, qc].rearrange(
                    "c p t -> p c t")[:, 4 * i:4 * i + 4, :],
                    ln2_fm[:, 4 * i:4 * i + 4, :])
        tok_ln2_store = ("c", "ve", 0)
        tok_ln2_store = tok_st

    ar.reset()
    return dict(tok_x2_store=tok_x2_store, tok_ln2_store=tok_ln2_store)


def build_attention(nc, fw, ar, EV):
    pe, ve, se, gp, sy = fw.pe, fw.ve, fw.se, fw.gp, fw.sy
    ps = fw.ps
    qn = EV["qn"]; c_norm = EV["c_norm"]; k_pe = EV["k_pe"]
    wqb = EV["wqb"]; wkvb = EV["wkvb"]
    cosq_in = EV["cosq_in"]; sinqm_in = EV["sinqm_in"]; dmask_in = EV["dmask_in"]
    o_scr = EV["o_scr"]; taps = EV["taps"]; id_bf = EV["id_bf"]
    ones_bf = EV["ones_bf"]; id_f32 = EV["id_f32"]
    tok_p1b_ve = EV["tok_p1b_ve"]; tok_p1b_pe = EV["tok_p1b_pe"]
    tok_p1b_se = EV["tok_p1b_se"]

    ar.mark()
    cosq = ar.alloc([TQ], BF16, pcount=DR)
    sinqm = ar.alloc([TQ], BF16, pcount=DR)
    masks = ar.alloc([2, 512], F32)
    knope = ar.alloc([S], BF16)
    v_tm = ar.alloc([32, 128], BF16)
    qh_n = ar.alloc([TQ], BF16)
    qpe_f = ar.alloc([TQ], F32, pcount=DR)
    qpe_p = ar.alloc([TQ], F32, pcount=DR)
    qpe_bf = ar.alloc([TQ], BF16, pcount=DR)
    p_bf2 = ar.alloc([2, S], BF16)
    pT = ar.alloc([32, 128], BF16)
    o_buf = ar.alloc([1, TQ], BF16)
    wqb_sb = ar.alloc([2, 12, 256], BF16)
    wkvb_sb = ar.alloc([2, 4, 256], BF16)
    sq_kpe = ar.alloc([S], BF16, pcount=DR)
    sq_qpe = ar.alloc([TQ], BF16, pcount=DR)
    gblk = ar.alloc([8], F32, pcount=1)
    gmax = ar.alloc([1], F32, pcount=1)
    nrow = qpe_p[0:1, :]
    bias_t = ar.alloc([16], F32)
    rsb = ar.alloc([2, 8], F32)
    rsum = ar.alloc([2, 1], F32)
    rs = ar.alloc([2, 1], F32)
    diag_rs = ar.alloc([2, 128], BF16)

    ps_trs = [ps[6].ap(), ps[7].ap(), ps[3].ap()]

    for q_ in (sy, gp):
        q_.wait(tok_p1b_ve)
        q_.wait(tok_p1b_pe)
        q_.wait(tok_p1b_se)

    gp.dma("qrope", cosq, cosq_in)
    tok_qt = gp.dma("qrope", sinqm, sinqm_in)
    tok_mask = sy.dma("dmask2", masks, dmask_in.rearrange("j p n -> p j n"))

    wqb_r = wqb.rearrange("(qc p) m -> p qc m", p=128)
    wkvb_r = wkvb.rearrange("(cc p) m -> p cc m", p=128)

    bw = {i: None for i in range(8)}
    br = {i: tok_p1b_ve for i in range(8)}
    br_se = {i: tok_p1b_se for i in range(8)}

    def wstream(h):
        b = h % 2
        t = None
        for i in range(3):
            t = gp.dma(f"wqb{b}", wqb_sb[:, b, 4 * i:4 * i + 4, :],
                       wqb_r[:, 4 * i:4 * i + 4, 256 * h:256 * (h + 1)])
        t2 = gp.dma(f"wkvb{b}", wkvb_sb[:, b, :, :],
                    wkvb_r[:, :, 256 * h:256 * (h + 1)])
        return t, t2

    se.wait(tok_p1b_se)
    se.wait(("c", "ve", EV["tok_p1b_ve"][2]) if False else tok_p1b_ve)
    se.op(lambda s: s.square(sq_kpe[:, :], k_pe[:, :]))
    tok_sq_kpe = se.tok()

    tok_w = {0: wstream(0)}
    tok_o_store = {}
    tok_kv_users = tok_p1b_pe
    tok_pT_pe = None
    tok_av_prev = None
    tok_sq_free = None
    tok_bias_guard = None
    tok_pT_buf = [None, None]
    tok_rs_buf = [None, None]
    tok_diag_rd = [None, None]

    for h in range(NH):
        b = h % 2
        if h + 1 < NH:
            if h >= 1:
                gp.wait(tok_w_used[(h + 1) % 2])
            tok_w[h + 1] = wstream(h + 1)
        if h == 0:
            tok_w_used = {}

        pe.wait(tok_w[h][0])
        pe.wait(tok_w[h][1])
        tok_ev_kn = {}
        for kb in range(8):
            bank = 6 + kb % 2
            pe.wait(br[bank])
            for cc in range(4):
                pe.op(lambda tt, kb=kb, cc=cc, bank=bank, b=b: tt.matmul(
                    ps[bank][:, :], wkvb_sb[:, b, cc, 0:128],
                    c_norm[:, cc, 512 * kb:512 * (kb + 1)],
                    start=(cc == 0), stop=(cc == 3)), inc=(cc == 3))
            bw[bank] = pe.tok()
            ve.wait(bw[bank])
            ve.wait(tok_kv_users)
            ve.op(lambda v, kb=kb, bank=bank: v.tensor_copy(
                knope[:, 512 * kb:512 * (kb + 1)], ps[bank][:, :]))
            br[bank] = ve.tok()
        tok_knope_ev = ve.tok()

        sq_scr = p_bf2[:, 0, :]
        se.wait(tok_knope_ev)
        se.wait(tok_sq_free)
        se.wait(tok_pT_buf[0])
        se.wait(tok_pT_buf[1])
        se.wait(tok_sq_kpe)
        se.op(lambda s: s.square(sq_scr[:, :], knope[:, :]))
        tok_sqk = se.tok()

        for grp in range(8):
            bank = 4 + grp % 2
            pe.wait(br[bank])
            for kt in range(4):
                ki = 4 * grp + kt
                for cc in range(4):
                    pe.op(lambda tt, ki=ki, kt=kt, cc=cc, bank=bank, b=b: tt.matmul(
                        ps[bank][:, 128 * kt:128 * (kt + 1)],
                        c_norm[:, cc, 128 * ki:128 * (ki + 1)],
                        wkvb_sb[:, b, cc, 128:256],
                        start=(cc == 0), stop=(cc == 3)),
                        inc=(kt == 3 and cc == 3))
            bw[bank] = pe.tok()
            ve.wait(bw[bank])
            ve.op(lambda v, grp=grp, bank=bank: v.tensor_copy(
                v_tm[:, 4 * grp:4 * (grp + 1), :].rearrange("p a b -> p (a b)"),
                ps[bank][:, :]))
            br[bank] = ve.tok()
        tok_v_ev = ve.tok()

        pe.wait(tok_sqk)
        for blk in range(8):
            bank = 6 + blk % 2
            pe.wait(br[bank])
            pe.wait(br_se[bank])
            pe.op(lambda tt, blk=blk, bank=bank: tt.matmul(
                ps[bank][0:1, 0:512], ones_bf[:, 0:1],
                sq_scr[:, 512 * blk:512 * (blk + 1)],
                start=True, stop=False), inc=False)
            pe.op(lambda tt, blk=blk, bank=bank: tt.matmul(
                ps[bank][0:1, 0:512], ones_bf[0:DR, 0:1],
                sq_kpe[:, 512 * blk:512 * (blk + 1)],
                start=False, stop=True))
            bw[bank] = pe.tok()
            ve.wait(bw[bank])
            ve.op(lambda v, blk=blk, bank=bank: v.reduce_max(
                gblk[:, blk:blk + 1], ps[bank][0:1, 0:512], axis=AX.X))
            br[bank] = ve.tok()
        ve.wait(ve.tok())
        ve.op(lambda v: v.reduce_max(gmax[:, :], gblk[:, :], axis=AX.X))
        se.wait(ve.tok())
        se.wait(se.tok())
        se.op(lambda s: s.sqrt(gmax[:, :], gmax[:, :]))
        tok_G = se.tok()
        tok_sq_k_done = pe.tok()

        tok_qn_ev = {}
        for qg in range(4):
            for part in range(2):
                bank = 4 + (qg * 2 + part) % 2
                pe.wait(br[bank])
                for qc in range(12):
                    pe.op(lambda tt, qg=qg, part=part, qc=qc, bank=bank, b=b: tt.matmul(
                        ps[bank][:, :], wqb_sb[:, b, qc, 128 * part:128 * (part + 1)],
                        qn[:, qc, 512 * qg:512 * (qg + 1)],
                        start=(qc == 0), stop=(qc == 11)), inc=(qc == 11))
                bw[bank] = pe.tok()
                ve.wait(bw[bank])
                if part == 0:
                    ve.op(lambda v, qg=qg, bank=bank: v.tensor_copy(
                        qh_n[:, 512 * qg:512 * (qg + 1)], ps[bank][:, :]))
                else:
                    ve.op(lambda v, qg=qg, bank=bank: v.tensor_copy(
                        qpe_f[:, 512 * qg:512 * (qg + 1)], ps[bank][0:DR, :]),
                        inc=False)
                    ve.op(lambda v, qg=qg, bank=bank: v.tensor_copy(
                        qpe_p[:, 512 * qg:512 * (qg + 1)], ps[bank][DR:2 * DR, :]))
                br[bank] = ve.tok()
        ve.wait(tok_qt)
        ve.wait(ve.tok())
        ve.op(lambda v: v.tensor_mul(qpe_p[:, :], qpe_p[:, :], sinqm[:, :]),
              inc=False)
        ve.op(lambda v: v.tensor_mul(qpe_f[:, :], qpe_f[:, :], cosq[:, :]))
        ve.wait(ve.tok())
        ve.op(lambda v: v.tensor_add(qpe_bf[:, :], qpe_p[:, :], qpe_f[:, :]))
        tok_q_ready = ve.tok()

        se.wait(tok_sq_k_done)
        se.wait(tok_q_ready)
        se.op(lambda s: s.square(sq_scr[:, 0:TQ], qh_n[:, :]), inc=False)
        se.op(lambda s: s.square(sq_qpe[:, :], qpe_bf[:, :]))
        tok_sqq = se.tok()
        pe.wait(tok_sqq)
        for qg in range(4):
            bank = 6 + qg % 2
            pe.wait(br[bank])
            pe.wait(br_se[bank])
            pe.op(lambda tt, qg=qg, bank=bank: tt.matmul(
                ps[bank][0:1, 0:512], ones_bf[:, 0:1],
                sq_scr[:, 512 * qg:512 * (qg + 1)],
                start=True, stop=False), inc=False)
            pe.op(lambda tt, qg=qg, bank=bank: tt.matmul(
                ps[bank][0:1, 0:512], ones_bf[0:DR, 0:1],
                sq_qpe[:, 512 * qg:512 * (qg + 1)],
                start=False, stop=True))
            bw[bank] = pe.tok()
            se.wait(bw[bank])
            se.wait(tok_q_ready)
            se.op(lambda s, qg=qg, bank=bank: s.sqrt(
                nrow[:, 512 * qg:512 * (qg + 1)], ps[bank][0:1, 0:512]))
            br_se[bank] = se.tok()
        tok_sq_free = pe.tok()
        tok_nrow = se.tok()
        ve.wait(tok_nrow)
        ve.wait(tok_G)
        ve.op(lambda v: v.tensor_scalar(
            nrow[:, :], nrow[:, :], gmax[:, :], None, ALU.mult))
        ve.wait(ve.tok())
        ve.op(lambda v: v.tensor_scalar(
            nrow[:, :], nrow[:, :], -SOFTMAX_SCALE, EXP_SHIFT,
            ALU.mult, ALU.add))
        tok_brow = ve.tok()
        pe.wait(tok_brow)
        pe.wait(br[6])
        pe.wait(br_se[6])
        for t16 in range(16):
            pe.op(lambda tt, t16=t16: tt.transpose(
                ps[6][:, t16:t16 + 1], nrow[0:1, 128 * t16:128 * (t16 + 1)],
                id_f32[0:1, 0:1]), inc=(t16 == 15))
        bw[6] = pe.tok()
        ve.wait(bw[6])
        ve.wait(tok_bias_guard)
        ve.op(lambda v: v.tensor_copy(bias_t[:, :], ps[6][:, 0:16]))
        br[6] = ve.tok()
        tok_bias = ve.tok()

        def emit_blocks(j):
            nb = j // 2 + 1
            tb = j % 2
            pe.wait(tok_q_ready)
            pe.wait(tok_knope_ev)
            ve.wait(tok_mask)
            se.wait(tok_bias)
            se.wait(tok_pT_buf[tb])
            se.wait(tok_rs_buf[tb])
            for kb in range(nb):
                bank = scb[j][kb]
                pe.wait(br[bank])
                pe.wait(br_se[bank])
                pe.op(lambda tt, j=j, kb=kb, bank=bank: tt.matmul(
                    ps[bank][:, :], qh_n[:, 128 * j:128 * (j + 1)],
                    knope[:, 512 * kb:512 * (kb + 1)],
                    start=True, stop=False), inc=False)
                pe.op(lambda tt, j=j, kb=kb, bank=bank: tt.matmul(
                    ps[bank][:, :], qpe_bf[:, 128 * j:128 * (j + 1)],
                    k_pe[:, 512 * kb:512 * (kb + 1)],
                    start=False, stop=True))
                bw[bank] = pe.tok()
                if kb == nb - 1:
                    ve.wait(bw[bank])
                    ve.op(lambda v, j=j, bank=bank: v.tensor_tensor(
                        ps[bank][:, :], ps[bank][:, :], masks[:, j % 2, :],
                        ALU.add))
                    br[bank] = ve.tok()
                    se.wait(br[bank])
                else:
                    se.wait(bw[bank])
                se.op(lambda s, j=j, kb=kb, bank=bank, tb=tb: s.activation(
                    p_bf2[:, tb, 512 * kb:512 * (kb + 1)], ps[bank][:, :],
                    AF.Exp, bias=bias_t[:, j:j + 1], scale=SOFTMAX_SCALE,
                    accum_out=rsb[:, tb, kb:kb + 1]))
                br_se[bank] = se.tok()

        def emit_pTAV(j):
            nonlocal tok_pT_pe, tok_av_prev
            nb = j // 2 + 1
            tb_ = j % 2
            pe.wait(tok_norm_t[j])
            grp_ev = {}
            for grp in range(nb):
                bank = (6, 7, 3)[grp % 3]
                tbank = ps_trs[grp % 3]
                pe.wait(br[bank])
                pe.wait(br_se[bank])
                for t4 in range(4):
                    pe.op(lambda tt, grp=grp, t4=t4, tbank=tbank, tb_=tb_:
                          tt.matmul(
                              tbank[:, 128 * t4:128 * (t4 + 1)],
                              p_bf2[:, tb_,
                                    512 * grp + 128 * t4:512 * grp + 128 * (t4 + 1)],
                              diag_rs[:, tb_, :],
                              start=True, stop=True), inc=(t4 == 3))
                bw[bank] = pe.tok()
                eng = ve if grp % 2 == 0 else se
                eng.wait(bw[bank])
                eng.op(lambda v, grp=grp, tbank=tbank: v.tensor_copy(
                    pT[:, 4 * grp:4 * (grp + 1), :].rearrange("p a b -> p (a b)"),
                    tbank[:, 0:512])
                       if grp % 2 == 0 else
                       v.copy(pT[:, 4 * grp:4 * (grp + 1), :].rearrange(
                           "p a b -> p (a b)"), tbank[:, 0:512]))
                if grp % 2 == 0:
                    br[bank] = eng.tok()
                else:
                    br_se[bank] = eng.tok()
                grp_ev[grp] = (eng.tok())
            tok_pT_buf[tb_] = pe.tok()
            tok_diag_rd[tb_] = tok_pT_buf[tb_]
            tok_pT_pe = tok_pT_buf[tb_]
            av_bank = 4 + j % 2
            pe.wait(tok_v_ev)
            pe.wait(br[av_bank])
            for kc in range(4 * nb):
                pe.wait(grp_ev[kc // 4])
                pe.op(lambda tt, kc=kc, av_bank=av_bank, nb=nb: tt.matmul(
                    ps[av_bank][:, 0:128], v_tm[:, kc, :], pT[:, kc, :],
                    start=(kc == 0), stop=(kc == 4 * nb - 1)),
                    inc=(kc == 4 * nb - 1))
            bw[av_bank] = pe.tok()
            tok_av_prev = bw[av_bank]
            ve.wait(bw[av_bank])
            if h >= 1:
                ve.wait(tok_o_store.get(h - 1))
            ve.op(lambda v, j=j, av_bank=av_bank: v.tensor_copy(
                o_buf[:, 0, 128 * j:128 * (j + 1)], ps[av_bank][:, 0:128]))
            br[av_bank] = ve.tok()

        scb = {}
        rb = 0
        for j in range(16):
            nb = j // 2 + 1
            scb[j] = [(rb + i) % 3 for i in range(nb)]
            rb = (rb + nb) % 3

        tok_norm_t = {}
        for j in range(16):
            nb = j // 2 + 1
            tb = j % 2
            emit_blocks(j)
            if j >= 1:
                emit_pTAV(j - 1)
            tok_exp = se.tok()
            ve.wait(tok_exp)
            ve.wait(tok_diag_rd[tb])
            ve.op(lambda v, nb=nb, tb=tb: v.reduce_sum(
                rsum[:, tb, :], rsb[:, tb, 0:nb], axis=AX.X))
            ve.wait(ve.tok())
            ve.op(lambda v, tb=tb: v.reciprocal(rs[:, tb, :], rsum[:, tb, :]))
            tok_rs_buf[tb] = ve.tok()
            ve.wait(tok_rs_buf[tb])
            ve.op(lambda v, tb=tb: v.tensor_mul(
                diag_rs[:, tb, :], id_bf[:, :],
                rs[:, tb, :].broadcast_to([128, 128])))
            tok_norm_t[j] = ve.tok()
        emit_pTAV(15)
        tok_bias_guard = se.tok()

        tok_kv_users = pe.tok()
        tok_w_used[b] = pe.tok()
        tok_o_ev = ve.tok()
        sy.wait(tok_o_ev)
        tok_o_store[h] = sy.dma("o_scr", o_scr[h], o_buf[:, 0, :])
        if "o" in taps:
            tok_o_store[h] = sy.dma("o_scr", taps["o"][h], o_buf[:, 0, :])

    ar.reset()
    return dict(tok_o_store=tok_o_store)


DEBUG_SHAPES = {
    "c_norm": ([KVL // 128, 128, S], BF16),
    "k_pe": ([DR, S], BF16),
    "qn": ([QL // 128, 128, TQ], BF16),
    "o": ([NH, DV, TQ], BF16),
    "x2": ([TQ // 128, 128, H], F32),
    "ln2": ([H // 128, 128, TQ], BF16),
    "gates": ([8, TQ], BF16),
}



def _rope_tables():
    inv = 1.0 / ROPE_THETA ** (np.arange(0, DR, 2, dtype=np.float64) / DR)
    t = np.arange(S, dtype=np.float64)
    fr = np.outer(t, inv)
    emb = np.concatenate([fr, fr], axis=1)
    cos = np.cos(emb).astype(np.float32)
    sin = np.sin(emb).astype(np.float32)
    sinm = sin.copy()
    sinm[:, :DR // 2] = -sinm[:, :DR // 2]
    return np.ascontiguousarray(cos.T), np.ascontiguousarray(sinm.T)


def _own_rows(half):
    rows = []
    for j in range(16):
        g = 2 * j + half
        rows.extend(range(128 * g, 128 * (g + 1)))
    return np.array(rows)


def make_in_maps(inputs):
    cos_t, sinm_t = _rope_tables()
    gamq = np.ascontiguousarray(
        np.asarray(inputs["L0_attn_q_a_ln_w"], np.float32).reshape(QL // 128, 128).T)
    gamkv = np.ascontiguousarray(
        np.asarray(inputs["L0_attn_kv_a_ln_w"], np.float32).reshape(KVL // 128, 128).T)
    ident = np.eye(128, dtype=np.float32)
    ones = np.ones((128, 128), dtype=np.float32)
    sel = np.zeros((8, E * 128), dtype=np.float32)
    for e in range(E):
        sel[e, 128 * e:128 * (e + 1)] = 1.0
    x = np.asarray(inputs["L0_attn_x"], dtype=np.float32)
    wkva0 = np.asarray(inputs["L0_attn_wkv_a"], np.float32)
    wkva_ext = np.concatenate(
        [wkva0, wkva0[:, KVL + 32:KVL + 64], wkva0[:, KVL:KVL + 32]], axis=1)
    wqb0 = np.asarray(inputs["L0_attn_wq_b"], np.float32)
    wqb_h = wqb0.reshape(QL, NH, QH)
    wqb_ext = np.concatenate(
        [wqb_h, wqb_h[:, :, DN + 32:DN + 64], wqb_h[:, :, DN:DN + 32]],
        axis=2).reshape(QL, NH * 256)
    shared = dict(
        wqa=np.asarray(inputs["L0_attn_wq_a"], np.float32),
        wqb=np.ascontiguousarray(wqb_ext),
        wkva=np.ascontiguousarray(wkva_ext),
        wkvb=np.asarray(inputs["L0_attn_wkv_b"], np.float32),
        wo=np.asarray(inputs["L0_attn_wo"], np.float32),
        wrt=np.asarray(inputs["L0_ffn_W_router"], np.float32),
        weg=np.asarray(inputs["L0_ffn_We_gate"], np.float32),
        weu=np.asarray(inputs["L0_ffn_We_up"], np.float32),
        wed=np.asarray(inputs["L0_ffn_We_down"], np.float32),
        wsg=np.asarray(inputs["L0_ffn_Ws_gate"], np.float32),
        wsu=np.asarray(inputs["L0_ffn_Ws_up"], np.float32),
        wsd=np.asarray(inputs["L0_ffn_Ws_down"], np.float32),
        gamq=gamq, gamkv=gamkv, id_in=ident, ones_in=ones, sel_in=sel,
        cosk=cos_t, sinkm=sinm_t,
    )
    in_maps = []
    for c in range(8):
        b, half = c // 2, c % 2
        rows = _own_rows(half)
        dmask = np.zeros((2, 128, 512), dtype=np.float32)
        for jp in range(2):
            off = ((2 * jp + half) % 4) * 128
            r = np.arange(128)[:, None]
            k = np.arange(512)[None, :]
            dmask[jp] = np.where(k <= off + r, 0.0, NEG)
        m = dict(shared)
        m["xb"] = np.ascontiguousarray(x[b])
        m["xq"] = np.ascontiguousarray(x[b][rows])
        m["cosq"] = np.ascontiguousarray(cos_t[:, rows])
        m["sinqm"] = np.ascontiguousarray(sinm_t[:, rows])
        m["dmask"] = dmask
        in_maps.append(m)
    return in_maps


_NC_CACHE = {}


def _get_nc():
    key = (tuple(DEBUG_TAPS), PHASES, SKIP_P1B, P1A_CUT, P1B_CUT, P1B_SUB)
    if key not in _NC_CACHE:
        _NC_CACHE[key] = build_program()
    return _NC_CACHE[key]


def kernel(**inputs):
    nc = _get_nc()
    in_maps = make_in_maps(inputs)
    res = run_bass_kernel_spmd(nc, in_maps, core_ids=list(range(8)))
    out = np.empty((B, S, H), dtype=np.float32)
    for c in range(8):
        b, half = c // 2, c % 2
        rows = _own_rows(half)
        out[b][rows] = np.asarray(res.results[c]["out"])
    kernel.last_results = res
    return out
